# version 08
# baseline (speedup 1.0000x reference)
import numpy as np
import ml_dtypes
from contextlib import ExitStack

import concourse.bass as bass
import concourse.mybir as mybir
from concourse.bass_utils import run_bass_kernel_spmd
from concourse.bass_core_rust import IndirectOffsetOnAxis

F32 = mybir.dt.float32
BF16 = mybir.dt.bfloat16
I32 = mybir.dt.int32
AF = mybir.ActivationFunctionType
ALU = mybir.AluOpType
AX = mybir.AxisListType
BF = ml_dtypes.bfloat16

B = 2
S = 4096
D = 2048
FF = 8192
RPC = 1024
NT = 8
TPC = RPC * 128
EPS = 1e-5
G4 = [[0, 1, 2, 3], [4, 5, 6, 7]]
G8 = [[0, 1, 2, 3, 4, 5, 6, 7]]

import os
DBG_STAGE = int(os.environ.get("KBD_STAGE", "99"))

ENGS = ("sync", "gpsimd", "vector", "scalar", "tensor")


def core_tiles(p):
    return [4 * j + (p % 4) for j in range(NT)]


def core_rows(p):
    b = p // 4
    out = []
    for g in core_tiles(p):
        st = b * S + g * 128
        out.extend(range(st, st + 128))
    return np.array(out)


class Sched:
    def __init__(self):
        self.ops = {e: [] for e in ENGS}
        self.semnames = []
        self.counts = {}

    def sem(self, name):
        if name not in self.counts:
            self.semnames.append(name)
            self.counts[name] = 0
        return name

    def emit(self, eng, fn, waits=(), inc=None, inc_amt=1):
        tok = None
        if inc is not None:
            self.sem(inc)
            self.counts[inc] += inc_amt
            tok = (inc, self.counts[inc])
        self.ops[eng].append((fn, [tuple(w) for w in waits], inc, inc_amt))
        return tok

    def dma(self, queue, out, in_, waits=(), sem=None, **kw):
        if sem is None:
            sem = "d_" + queue
        self.sem(sem)
        gate = [(sem, self.counts[sem])] if self.counts[sem] else []
        return self.emit(queue, lambda e: e.dma_start(out=out, in_=in_, **kw),
                         waits=list(waits) + gate, inc=sem, inc_amt=16)

    def barrier_token(self, sem):
        return (sem, self.counts.get(sem, 0))


def replay(block, sch, sems):
    def mk(eng_name):
        def body(eng):
            for fn, waits, inc, amt in sch.ops[eng_name]:
                for s, v in waits:
                    if v > 0:
                        eng.wait_ge(sems[s], v)
                ins = fn(eng)
                if inc is not None:
                    if ins is None:
                        eng.nop().then_inc(sems[inc], amt)
                    else:
                        ins.then_inc(sems[inc], amt)
        return body
    block.sync(mk("sync"))
    block.gpsimd(mk("gpsimd"))
    block.vector(mk("vector"))
    block.scalar(mk("scalar"))
    block.tensor(mk("tensor"))



def build_program():
    nc = bass.Bass("TRN2", target_bir_lowering=False, debug=False)
    S_ = Sched()
    P = {}

    def din(name, shape, dt=F32):
        return nc.dram_tensor(name, shape, dt, kind="ExternalInput").ap()

    def dint(name, shape, dt=F32, **kw):
        return nc.dram_tensor(name, shape, dt, kind="Internal", **kw).ap()

    x_in = din("x_in", [RPC, D])
    out_t = nc.dram_tensor("out", [RPC, D], F32, kind="ExternalOutput").ap()
    w1s = din("w1s", [256, FF])
    w2s = din("w2s", [1024, D])
    W0s = din("W0s", [256, FF])
    W1s = din("W1s", [1024, D])
    onesrow = din("onesrow", [1, 128])
    wqs = din("wqs", [256, D])
    wkvs = din("wkvs", [256, 512])
    wos = din("wos", [256, D])
    cstab_in = din("cstab_in", [128, NT * 128], BF16)
    dmask_in = din("dmask_in", [128, 512], BF16)
    b0cols = din("b0cols", [128, 64])
    qkvbd = din("qkvbd", [128, 3 * 128])
    projbd = din("projbd", [128, 4 * 128])
    ident_in = din("ident_in", [128, 128])
    pconst = din("pconst", [128, 32])
    vrows = din("vrows", [10, FF])
    cmask = din("cmask", [128, 128])
    postab = din("postab", [128, NT])
    gsel_in = din("gsel_in", [128, 16])
    ebc_in = din("ebc_in", [16, 128])
    i16_in = din("i16_in", [16, 16])
    glnsel_in = din("glnsel_in", [128, 8])
    ebc8_in = din("ebc8_in", [8, 128])
    gidx = din("gidx", [128, NT], I32)

    w1slab = dint("w1slab", [256, FF])
    h_w1q = dint("h_w1q", [256, FF], BF16)
    h_w1g = dint("h_w1g", [D, FF], BF16)
    h_w2q = dint("h_w2q", [1024, D], BF16)
    h_w2g = dint("h_w2g", [FF, D], BF16)
    h_W1c = dint("h_W1c", [1024, D], BF16)
    h_W1g = dint("h_W1g", [FF, D], BF16)
    h_stat_i = dint("h_stat_i", [1, 2])
    h_stat_o = dint("h_stat_o", [1, 2])
    h_qkvt = dint("h_qkvt", [NT, 16, 128, 384], BF16)
    h_Tt = dint("h_Tt", [NT, 16, 128, 128], BF16)
    h_tok2 = dint("h_tok2", [16, TPC], BF16)
    h_tok2g = dint("h_tok2g", [4 * 16, TPC], BF16)
    h_kv = dint("h_kv", [RPC, 512], BF16)
    h_kvg = dint("h_kvg", [4 * RPC, 512], BF16)
    h_h = dint("h_h", [RPC, FF], BF16)
    h_hqT = dint("h_hqT", [FF, RPC], BF16)
    h_x1 = dint("h_x1", [RPC, D])
    h_qf = dint("h_qf", [16 * 128, RPC], BF16)
    h_DqT = dint("h_DqT", [FF, RPC], BF16)
    h_dbg = dint("h_dbg", [RPC, D])
    h_wqcs = dint("h_wqcs", [256, D], BF16)
    h_wqc = dint("h_wqc", [D, D], BF16)
    h_wkvcs = dint("h_wkvcs", [256, 512], BF16)
    h_wkvc = dint("h_wkvc", [D, 512], BF16)
    h_wocs = dint("h_wocs", [256, D], BF16)
    h_woc = dint("h_woc", [D, D], BF16)
    h_W0cs = dint("h_W0cs", [256, FF], BF16)
    h_W0g = dint("h_W0g", [D, FF], BF16)

    ctx = ExitStack()
    with ctx:
        def sb(name, shape, dt=F32):
            return ctx.enter_context(nc.sbuf_tensor(name, shape, dt)).ap()

        def psum(name, shape, dt=F32):
            return ctx.enter_context(nc.psum_tensor(name, shape, dt)).ap()

        x_sb = sb("x_sb", [128, NT * D])
        ar1 = sb("ar1", [128, FF])
        ar2 = sb("ar2", [128, FF])
        ar3 = sb("ar3", [128, FF])
        ar4 = sb("ar4", [128, 4096])
        ar56 = sb("ar56", [128, 4096])
        smalls = sb("smalls", [128, 1840])
        prod_f = sb("prod_f", [128, 256])
        pv_f = sb("pv_f", [128, 256])
        pconst_sb = sb("pconst_sb", [128, 32])
        identb = sb("identb", [128, 128], BF16)
        qkvbd_sb = sb("qkvbd_sb", [128, 3 * 128], BF16)
        projbd_sb = sb("projbd_sb", [128, 4 * 128], BF16)
        cmask_sb = sb("cmask_sb", [128, 128], BF16)
        gsel_sb = sb("gsel_sb", [128, 16], BF16)
        ebc_sb = sb("ebc_sb", [16, 128], BF16)
        i16_sb = sb("i16_sb", [16, 16], BF16)
        glnsel_sb = sb("glnsel_sb", [128, 8], BF16)
        ebc8_sb = sb("ebc8_sb", [8, 128], BF16)
        gidx_sb = sb("gidx_sb", [128, NT], I32)
        onesrow_sb = sb("onesrow_sb", [1, 128])
        onesrow_b = sb("onesrow_b", [1, 128], BF16)
        onescol_b = sb("onescol_b", [128, 1], BF16)
        b0c_sb = sb("b0c_sb", [128, 64])

        pb = [psum(f"pb{i}", [128, 512], F32) for i in range(6)]
        pbt = psum("pbt", [128, 1024], BF16)
        pbt2 = psum("pbt2", [128, 1024], BF16)

        xn_sb = ar1.bitcast(BF16)
        pk = ar2.bitcast(BF16)
        pk2 = ar3.bitcast(BF16)
        ar4b = ar4.bitcast(BF16)
        ar56b = ar56.bitcast(BF16)
        ar56i = ar56.bitcast(I32)
        prod_sb = prod_f.bitcast(BF16)
        pv_sb = pv_f.bitcast(BF16)
        smb = smalls.bitcast(BF16)
        smb_base = 2144

        P.update(locals())

        emit_phase0(S_, P)
        if DBG_STAGE >= 2:
            emit_wprep_stats(S_, P)
            P["attn_midhook"] = lambda: emit_wprep_quant(S_, P)
        if DBG_STAGE >= 1:
            emit_l0attn(S_, P)
        if DBG_STAGE >= 2:
            emit_wprep_ags(S_, P)
        if DBG_STAGE >= 2:
            emit_l0ffn(S_, P)
        if DBG_STAGE >= 3:
            emit_x1(S_, P)
        if DBG_STAGE >= 4:
            emit_l1(S_, P)
        emit_debug_out(S_, P)

        sems = {}
        for name in S_.semnames:
            sems[name] = ctx.enter_context(nc.semaphore(name))
        block = ctx.enter_context(nc.Block())
        with nc.allow_low_precision(reason="bf16 pipeline; tolerance spec "
                                    "permits bf16 matmuls + bf16 stores"):
            replay(block, S_, sems)
    return nc



def emit_phase0(S, P):
    x_in, x_sb = P["x_in"], P["x_sb"]
    ar1, smalls = P["ar1"], P["smalls"]
    xn_sb = P["xn_sb"]

    S.dma("sync", P["pconst_sb"], P["pconst"], sem="d_cn")
    S.dma("gpsimd", P["identb"], P["ident_in"])
    S.dma("gpsimd", P["qkvbd_sb"], P["qkvbd"])
    S.dma("gpsimd", P["projbd_sb"], P["projbd"])
    S.dma("gpsimd", P["cmask_sb"], P["cmask"])
    S.dma("gpsimd", P["gsel_sb"], P["gsel_in"])
    S.dma("gpsimd", P["ebc_sb"], P["ebc_in"])
    S.dma("gpsimd", P["i16_sb"], P["i16_in"])
    S.dma("gpsimd", P["glnsel_sb"], P["glnsel_in"])
    S.dma("gpsimd", P["ebc8_sb"], P["ebc8_in"])
    P["tok_gidx"] = S.dma("gpsimd", P["gidx_sb"], P["gidx"])
    S.dma("gpsimd", P["onesrow_b"], P["onesrow"])
    S.dma("gpsimd", P["onescol_b"], P["onesrow"].transpose([1, 0]))
    S.dma("sync", P["onesrow_sb"], P["onesrow"], sem="d_cn")
    S.dma("sync", P["b0c_sb"], P["b0cols"], sem="d_cn")
    P["tok_consts"] = S.barrier_token("d_cn")
    P["tok_constsg"] = S.barrier_token("d_gpsimd")

    for t in range(NT):
        S.dma("sync", x_sb[:, t * D:(t + 1) * D], x_in[t * 128:(t + 1) * 128, :])
    tok_x = S.barrier_token("d_sync")
    P["tok_x"] = tok_x

    mu = P["smalls"][:, 0:1]
    esq = P["smalls"][:, 1:2]
    var = P["smalls"][:, 2:3]
    std = P["smalls"][:, 3:4]
    rstd = P["smalls"][:, 4:5]
    epsc = P["pconst_sb"][:, 16:17]
    ftmp = P["ar3"]
    sc5 = P["smalls"][:, 5:6]
    t8 = None
    t7 = None
    for t in range(NT):
        xt = x_sb[:, t * D:(t + 1) * D]
        head = [tok_x] + ([t8] if t8 else [])
        t1 = S.emit("vector", lambda e, xt=xt: e.tensor_reduce(
            out=mu, in_=xt, axis=AX.X, op=ALU.add), waits=head, inc="tv")
        t2 = S.emit("scalar", lambda e, xt=xt: e.activation(
            out=ftmp[:, 0:D], in_=xt, func=AF.Square, scale=float(D) ** -0.5,
            accum_out=esq), waits=head, inc="tsc")
        t3 = S.emit("vector", lambda e: e.tensor_scalar(
            out=mu, in0=mu, scalar1=1.0 / D, scalar2=None, op0=ALU.mult),
            waits=[t1, t2], inc="tv")
        t5 = S.emit("vector", lambda e: e.tensor_mul(
            out=sc5, in0=mu, in1=mu), waits=[t3], inc="tv")
        t5b = S.emit("vector", lambda e: e.tensor_sub(out=var, in0=esq, in1=sc5),
                     waits=[t5], inc="tv")
        t6 = S.emit("scalar", lambda e: e.activation(
            out=std, in_=var, func=AF.Sqrt, bias=epsc, scale=1.0),
            waits=[t5b, P["tok_consts"]] + ([t7] if t7 else []), inc="tsc")
        t7 = S.emit("vector", lambda e: e.reciprocal(out=rstd, in_=std),
                    waits=[t6], inc="tv")
        t7b = S.emit("vector", lambda e, t=t: e.tensor_copy(
            out=P["smalls"][:, 24 + t:25 + t], in_=mu), waits=[t7], inc="tv")
        t7c = S.emit("vector", lambda e, t=t: e.tensor_copy(
            out=P["smalls"][:, 32 + t:33 + t], in_=rstd), waits=[t7],
            inc="tv")
        t8 = S.emit("vector", lambda e, t=t, xt=xt: e.tensor_scalar(
            out=xn_sb[:, t * D:(t + 1) * D], in0=xt, scalar1=mu, scalar2=rstd,
            op0=ALU.subtract, op1=ALU.mult), waits=[t7c], inc="tv")
    P["tok_xn"] = S.barrier_token("tv")
    P["mucols"] = P["smalls"][:, 24:32]
    P["rstdcols"] = P["smalls"][:, 32:40]


def emit_l0attn(S, P):
    xn_sb = P["xn_sb"]
    identb, qkvbd_sb, projbd_sb = P["identb"], P["qkvbd_sb"], P["projbd_sb"]
    pbt, pbt2, pb = P["pbt"], P["pbt2"], P["pb"]
    ar4b = P["ar4b"]
    ar56b = P["ar56b"]
    pk, pk2 = P["pk"], P["pk2"]
    smalls = P["smalls"]
    h_tok2 = P["h_tok2"]
    pc = P["pconst_sb"]
    prod, pv = P["prod_sb"], P["pv_sb"]
    gsel, i16b, ebc = P["gsel_sb"], P["i16_sb"], P["ebc_sb"]
    glnsel, ebc8 = P["glnsel_sb"], P["ebc8_sb"]

    Q1 = pk[:, 0:4096]; K1 = pk[:, 4096:8192]
    V1 = pk[:, 8192:12288]; Q2 = pk[:, 12288:16384]
    K2 = pk2[:, 0:4096]; V2 = pk2[:, 4096:8192]
    tokb = [pk2[:, 8192:12288], pk2[:, 12288:16384]]
    QKV = [(Q1, K1, V1), (Q2, K2, V2)]
    On = [ar4b[:, 0:4096], ar56b[:, 0:4096]]
    lep = [ar4b[:, 4096:8192], ar56b[:, 4096:8192]]
    tok2b = [ar4b[:, 0:4096], ar4b[:, 4096:8192]]
    qt = ar56b

    smb = P["smb"]
    pbase = P["smb_base"]
    Pslot = [smb[0:16, pbase:pbase + 512], smb[0:16, pbase + 512:pbase + 1024]]
    rden = smb[0:16, pbase + 1024:pbase + 1536]
    attsb = smb[:, pbase:pbase + 512]
    sqsb = smb[:, pbase + 512:pbase + 1024]
    musb = smb[0:8, 96:608]
    varsb = smb[0:8, 608:1120]
    stdsb = smb[0:8, 1120:1632]
    rstdsb = smb[0:8, 1632:2144]

    def vsl(buf, brr):
        if brr == 0:
            v = buf.rearrange("p (kk d r) -> p kk d r", kk=4, d=8, r=128)
            return lambda t: v[:, :, t, :]
        v = buf.rearrange("p (kk d i j rr) -> p kk d i j rr",
                          kk=4, d=8, i=4, j=8, rr=4)
        return lambda t: v[:, :, :, :, t, :]

    def flatlike(flat, brr):
        if brr == 0:
            return flat.rearrange("p (a b) -> p a b", a=4, b=128)
        return flat.rearrange("p (a b c e) -> p a b c e", a=4, b=8, c=4, e=4)

    tok_need = [P["tok_xn"], P["tok_constsg"], P["tok_consts"]]

    for chunk in range(2):
        chb = [S.barrier_token(x) for x in ("tv", "tt", "tsc", "d_t2")]
        last_rp = None
        for rtl in range(4):
            rt = chunk * 4 + rtl
            prev_evac = P.get("_t_evac_prev")
            ttoks = []
            for k in range(16):
                bank = pbt if k < 8 else pbt2
                kk8 = k % 8
                w = []
                if k == 0:
                    w = list(tok_need) + chb + \
                        ([prev_evac] if prev_evac else []) + \
                        ([last_rp] if last_rp else [])
                ttoks.append(S.emit("tensor", lambda e, rt=rt, k=k, bank=bank,
                                    kk8=kk8: e.transpose(
                    bank[:, kk8 * 128:(kk8 + 1) * 128],
                    xn_sb[:, rt * D + k * 128: rt * D + (k + 1) * 128],
                    identb), waits=w, inc="tt"))
            ev1 = S.emit("vector", lambda e: e.tensor_copy(
                out=qt[:, 6144:7168], in_=pbt[:, :]),
                waits=[ttoks[7]] + chb + ([last_rp] if last_rp else []),
                inc="tv")
            ev2 = S.emit("vector", lambda e: e.tensor_copy(
                out=qt[:, 7168:8192], in_=pbt2[:, :]), waits=[ttoks[15]],
                inc="tv")
            P["_t_evac_prev"] = ev2
            qk_evacs = []
            for k in range(16):
                bank = pb[k % 2]
                mm = None
                for g in range(3):
                    w = []
                    if g == 0:
                        w = [ev1 if k < 8 else ev2]
                        if len(qk_evacs) >= 2:
                            w.append(qk_evacs[-2])
                    mm = S.emit("tensor", lambda e, k=k, g=g, bank=bank:
                                e.matmul(
                        bank[:, g * 128:(g + 1) * 128],
                        qkvbd_sb[:, g * 128:(g + 1) * 128],
                        qt[:, 6144 + k * 128:6144 + (k + 1) * 128],
                        start=True, stop=True), waits=w, inc="tt")
                qk_evacs.append(S.emit("vector", lambda e, k=k, bank=bank:
                                       e.tensor_copy(
                    out=qt[:, k * 384:(k + 1) * 384], in_=bank[:, 0:384]),
                    waits=[mm], inc="tv"))
            wdma = [qk_evacs[-1]]
            last_rp = S.dma("sync", P["h_qkvt"][rt].transpose([1, 0, 2]),
                qt[:, 0:6144].rearrange("p (k gr) -> p k gr", k=16),
                waits=wdma, sem="d_rp")
            last_rp = S.dma("sync", P["h_Tt"][rt].transpose([1, 0, 2]),
                qt[:, 6144:8192].rearrange("p (k r) -> p k r", k=16),
                waits=wdma, sem="d_rp")
        spill_tok = S.barrier_token("d_rp")
        if chunk == 0:
            P["tok_attn_tt_c0"] = S.barrier_token("tt")
        if chunk == 1 and "attn_midhook" in P:
            P["attn_midhook"]()
            del P["attn_midhook"]

        qv = P["h_qkvt"].rearrange(
            "rt (kq kk) (d c) (g r) -> rt kq kk d c g r",
            kq=4, kk=4, d=8, c=16, g=3, r=128)
        tv = P["h_Tt"].rearrange(
            "rt (kq kk) (d c) r -> rt kq kk d c r", kq=4, kk=4, d=8, c=16)
        rts = slice(chunk * 4, chunk * 4 + 4)
        rp2 = []
        for br in range(2):
            Qb, Kb, Vb = QKV[br]
            cs = slice(br * 8, br * 8 + 8)
            for g, dstb in ((0, Qb), (1, Kb), (2, Vb)):
                src = qv[rts, :, :, :, cs, g, :].transpose([4, 0, 1, 2, 3, 5])
                rp2.append(S.dma("sync", dstb, src,
                                 waits=[spill_tok] +
                                 ([P["tok_wprep_v"], P["tok_wprep_bufs_free"]]
                                  if "tok_wprep_v" in P else []),
                                 sem="d_rp"))
        for oct_ in range(2):
            rto = slice(chunk * 4 + oct_ * 2, chunk * 4 + oct_ * 2 + 2)
            src = tv[rto].transpose([4, 0, 1, 2, 3, 5])
            rp2.append(S.dma("sync", tokb[oct_], src,
                             waits=[spill_tok] +
                             ([P["tok_wprep_v"], P["tok_wprep_bufs_free"]]
                              if "tok_wprep_v" in P else []), sem="d_rp"))
        tok_rp = S.barrier_token("d_rp")

        for br in range(2):
            Qb, Kb, Vb = QKV[br]
            sq_ = vsl(Qb, br); sk_ = vsl(Kb, br); sv_ = vsl(Vb, br)
            lepb, Onb = lep[br], On[br]
            so_ = vsl(Onb, br); sL_ = vsl(lepb, br)
            l1 = S.emit("vector", lambda e, Vb=Vb, lepb=lepb, br=br:
                        e.tensor_scalar(
                out=lepb, in0=Vb, scalar1=pc[:, br * 3 + 1:br * 3 + 2],
                scalar2=pc[:, 6 + br:7 + br], op0=ALU.mult, op1=ALU.add),
                waits=[tok_rp, S.barrier_token("tt")], inc="tv")
            lprev = l1
            for tap, dst_t, src_t in ((2, slice(0, 7), slice(1, 8)),
                                      (0, slice(1, 8), slice(0, 7))):
                if br == 0:
                    Vv = Vb.rearrange("p (kk d r) -> p kk d r", kk=4, d=8, r=128)
                    Lv = lepb.rearrange("p (kk d r) -> p kk d r",
                                        kk=4, d=8, r=128)
                    vsrc = Vv[:, :, src_t, :]
                    ldst = Lv[:, :, dst_t, :]
                else:
                    Vv = Vb.rearrange("p (kk d i j rr) -> p kk d i j rr",
                                      kk=4, d=8, i=4, j=8, rr=4)
                    Lv = lepb.rearrange("p (kk d i j rr) -> p kk d i j rr",
                                        kk=4, d=8, i=4, j=8, rr=4)
                    vsrc = Vv[:, :, :, :, src_t]
                    ldst = Lv[:, :, :, :, dst_t]
                lprev = S.emit("vector", lambda e, vsrc=vsrc, ldst=ldst,
                               tap=tap, br=br: e.scalar_tensor_tensor(
                    out=ldst, in0=vsrc, scalar=pc[:, br * 3 + tap:br * 3 + tap + 1],
                    in1=ldst, op0=ALU.mult, op1=ALU.add), waits=[lprev],
                    inc="tv")

            prev_on = None
            prev_rc = None
            for i in range(8):
                den_mm = None
                o_mm = None
                prev_gmm = None
                prev_den = {}
                prev_pb = {}
                for j in range(8):
                    slot = Pslot[j % 2]
                    pw = [tok_rp]
                    if prev_gmm:
                        pw.append(prev_gmm)
                    if prev_on:
                        pw.append(prev_on)
                    pr = S.emit("vector", lambda e, a=sq_(i), b=sk_(j),
                                o=flatlike(prod, br): e.tensor_mul(
                        out=o, in0=a, in1=b), waits=pw, inc="tv")
                    g_mm = S.emit("tensor", lambda e: e.matmul(
                        pb[2][0:16, :], gsel, prod, start=True, stop=True),
                        waits=[pr], inc="tt")
                    prev_gmm = g_mm
                    exw = [g_mm]
                    if j >= 2:
                        exw += [prev_den[j - 2], prev_pb[j - 2]]
                    ex = S.emit("scalar", lambda e, slot=slot: e.activation(
                        out=slot, in_=pb[2][0:16, :], func=AF.Exp,
                        scale=float(8) ** -0.5), waits=exw, inc="tsc")
                    dw = [ex]
                    if j == 0 and prev_rc:
                        dw.append(prev_rc)
                    den_mm = S.emit("tensor", lambda e, j=j, slot=slot:
                                    e.matmul(
                        pb[3][0:16, :], i16b, slot,
                        start=(j == 0), stop=(j == 7)), waits=dw, inc="tt")
                    prev_den[j] = den_mm
                    pb_mm = S.emit("tensor", lambda e, slot=slot: e.matmul(
                        pb[4][:, :], ebc, slot, start=True, stop=True),
                        waits=[ex], inc="tt")
                    prev_pb[j] = pb_mm
                    mu_ = S.emit("vector", lambda e, b=sv_(j),
                                 o=flatlike(pv, br),
                                 a=flatlike(pb[4][:, :], br): e.tensor_mul(
                        out=o, in0=a, in1=b),
                        waits=[pb_mm] + ([o_mm] if o_mm else []), inc="tv")
                    o_mm = S.emit("tensor", lambda e, j=j: e.matmul(
                        pb[5][:, :], identb, pv,
                        start=(j == 0), stop=(j == 7)), waits=[mu_], inc="tt")
                rc = S.emit("vector", lambda e: e.reciprocal(
                    out=rden, in_=pb[3][0:16, :]), waits=[den_mm], inc="tv")
                prev_rc = rc
                rb_mm = S.emit("tensor", lambda e: e.matmul(
                    pb[2][:, :], ebc, rden, start=True, stop=True),
                    waits=[rc], inc="tt")
                rb_ev = S.emit("vector", lambda e: e.tensor_copy(
                    out=pv, in_=pb[2][:, :]), waits=[rb_mm, o_mm], inc="tv")
                prev_on = S.emit("vector", lambda e, oi=so_(i),
                                 a=flatlike(pb[5][:, :], br),
                                 b=flatlike(pv, br): e.tensor_mul(
                    out=oi, in0=a, in1=b), waits=[rb_ev], inc="tv")

        brc = [S.barrier_token("tv"), S.barrier_token("tt"),
               S.barrier_token("tsc")]

        apply_done = []
        for oct_ in range(2):
            for nch in range(8):
                csl = slice(nch * 512, (nch + 1) * 512)
                bank = pb[4 + (nch % 2)]
                mm = None
                first = True
                for br in range(2):
                    lhs = projbd_sb[:, (br * 2 + oct_) * 128:
                                    (br * 2 + oct_ + 1) * 128]
                    for buf in (On[br], lep[br]):
                        w = []
                        if first:
                            w = brc + ([apply_done[-2]] if len(apply_done) >= 2
                                       else [])
                        mm = S.emit("tensor", lambda e, lhs=lhs, buf=buf,
                                    csl=csl, bank=bank, first=first,
                                    last=(br == 1 and buf is lep[1]): e.matmul(
                            bank[:, :], lhs, buf[:, csl], start=first,
                            stop=last), waits=w, inc="tt")
                        first = False
                aev = S.emit("vector", lambda e, bank=bank: e.tensor_scalar(
                    out=attsb, in0=bank[:, :], scalar1=pc[:, 10:11],
                    scalar2=None, op0=ALU.add),
                    waits=[mm] + ([apply_done[-1]] if apply_done else []),
                    inc="tv")
                sq = S.emit("scalar", lambda e: e.activation(
                    out=sqsb, in_=attsb, func=AF.Square), waits=[aev],
                    inc="tsc")
                mu_mm = S.emit("tensor", lambda e: e.matmul(
                    pb[2][0:8, :], glnsel, attsb, start=True, stop=True),
                    waits=[aev], inc="tt")
                sq_mm = S.emit("tensor", lambda e: e.matmul(
                    pb[3][0:8, :], glnsel, sqsb, start=True, stop=True),
                    waits=[sq], inc="tt")
                muev = S.emit("vector", lambda e: e.tensor_copy(
                    out=musb, in_=pb[2][0:8, :]), waits=[mu_mm], inc="tv")
                msq = S.emit("vector", lambda e: e.tensor_mul(
                    out=varsb, in0=musb, in1=musb), waits=[muev], inc="tv")
                vv = S.emit("vector", lambda e: e.tensor_sub(
                    out=varsb, in0=pb[3][0:8, :], in1=varsb),
                    waits=[msq, sq_mm], inc="tv")
                sd = S.emit("scalar", lambda e: e.activation(
                    out=stdsb, in_=varsb, func=AF.Sqrt,
                    bias=pc[0:8, 16:17], scale=1.0), waits=[vv], inc="tsc")
                rs = S.emit("vector", lambda e: e.reciprocal(
                    out=rstdsb, in_=stdsb), waits=[sd], inc="tv")
                mb_mm = S.emit("tensor", lambda e: e.matmul(
                    pb[2][:, :], ebc8, musb, start=True, stop=True),
                    waits=[rs], inc="tt")
                rb_mm2 = S.emit("tensor", lambda e: e.matmul(
                    pb[3][:, :], ebc8, rstdsb, start=True, stop=True),
                    waits=[rs], inc="tt")
                t1 = S.emit("vector", lambda e: e.tensor_sub(
                    out=sqsb, in0=attsb, in1=pb[2][:, :]),
                    waits=[mb_mm], inc="tv")
                t2 = S.emit("vector", lambda e: e.tensor_tensor(
                    out=attsb, in0=sqsb, in1=pb[3][:, :], op=ALU.mult),
                    waits=[t1, rb_mm2], inc="tv")
                t3 = S.emit("vector", lambda e: e.tensor_scalar(
                    out=sqsb, in0=attsb, scalar1=pc[:, 8:9],
                    scalar2=pc[:, 9:10], op0=ALU.mult, op1=ALU.add),
                    waits=[t2], inc="tv")
                apply_done.append(S.emit("vector",
                    lambda e, oct_=oct_, csl=csl: e.tensor_tensor(
                    out=tokb[oct_][:, csl], in0=sqsb,
                    in1=tokb[oct_][:, csl], op=ALU.add), waits=[t3], inc="tv"))
        apb = S.barrier_token("tv")

        for oct_ in range(2):
            ro_src = tokb[oct_].rearrange("p (kk d r) -> p kk d r",
                                          kk=4, d=8, r=128)
            ro_src = ro_src.transpose([0, 3, 1, 2])
            ro_dst = tok2b[oct_].rearrange("p (r kk d) -> p r kk d",
                                           r=128, kk=4, d=8)
            ro = S.emit("vector", lambda e, ro_dst=ro_dst, ro_src=ro_src:
                        e.tensor_copy(out=ro_dst, in_=ro_src),
                        waits=[apb], inc="tv")
            dstv = h_tok2.rearrange(
                "o (rt r kq m) -> o rt kq r m", rt=8, r=128, kq=4, m=32)
            for quad in range(2):
                rtl = oct_ * 2 + quad
                rt = chunk * 4 + rtl
                for o in range(16):
                    src = tok2b[oct_][o * 8 + quad * 4:o * 8 + quad * 4 + 4, :]
                    src2 = src.rearrange("so (r m) -> so r m", r=128, m=32)
                    S.dma("sync", dstv[o, rt], src2, waits=[ro], sem="d_t2")



def emit_wprep_stats(S, P):
    ar2, ar3 = P["ar2"], P["ar3"]
    smalls = P["smalls"]
    pc = P["pconst_sb"]
    pb = P["pb"]
    w0 = [P["tok_consts"], P["tok_constsg"], P["tok_xn"],
          S.barrier_token("tsc")]

    acc = smalls[:, 0:1]
    acc2 = smalls[:, 1:2]
    tmpc = smalls[:, 2:3]

    w2v = P["w2s"].rearrange("(a p) d -> a p d", p=128)
    srcs = [P["w1s"][0:128], P["w1s"][128:256],
            w2v[0:4].transpose([1, 0, 2]), w2v[4:8].transpose([1, 0, 2])]
    P["wp_srcs"] = srcs
    prev_red = None
    reds = []
    for i, src in enumerate(srcs):
        buf = ar2 if i % 2 == 0 else ar3
        bufv = buf if i < 2 else buf.rearrange("p (a d) -> p a d", a=4)
        tw = S.dma("scalar", bufv, src, waits=w0 + ([reds[i - 2]] if i >= 2
                                                    else []), sem="d_wps")
        pr = S.emit("vector", lambda e, buf=buf: e.tensor_reduce(
            out=tmpc, in_=buf, axis=AX.X, op=ALU.add,
            apply_absolute_value=True),
            waits=[tw] + ([prev_red] if prev_red else []), inc="tv")
        dst = acc if i < 2 else acc2
        prev_red = S.emit("vector", lambda e, dst=dst, first=(i % 2 == 0):
                          e.tensor_copy(out=dst, in_=tmpc) if first else
                          e.tensor_add(out=dst, in0=dst, in1=tmpc),
                          waits=[pr], inc="tv")
        reds.append(prev_red)
    pm1 = S.emit("tensor", lambda e: e.matmul(
        pb[0][0:1, 0:2], pc[:, 11:12], smalls[:, 0:2],
        start=True, stop=True), waits=[prev_red], inc="tt")
    ev = S.emit("vector", lambda e: e.tensor_copy(
        out=smalls[0:1, 4:6], in_=pb[0][0:1, 0:2]), waits=[pm1], inc="tv")
    st1 = S.dma("sync", P["h_stat_i"], smalls[0:1, 4:6], waits=[ev],
                sem="d_wp")
    wc = S.dma("gpsimd", P["h_W1c"], P["W1s"], waits=[], sem="d_wc")
    wc_wq = S.dma("gpsimd", P["h_wqcs"], P["wqs"], waits=[], sem="d_wc")
    wc_wkv = S.dma("gpsimd", P["h_wkvcs"], P["wkvs"], waits=[], sem="d_wc")
    wc_wo = S.dma("gpsimd", P["h_wocs"], P["wos"], waits=[], sem="d_wc")
    wc_w0 = S.dma("gpsimd", P["h_W0cs"], P["W0s"], waits=[], sem="d_wc")
    P["tok_wc_W1"] = wc
    P["tok_wc_W0"] = wc_w0
    P["tok_wc_wq"], P["tok_wc_wkv"], P["tok_wc_wo"] = wc_wq, wc_wkv, wc_wo
    ar_tok = S.emit("gpsimd", lambda e: e.collective_compute(
        "AllReduce", ALU.add, replica_groups=G8,
        ins=[P["h_stat_i"]], outs=[P["h_stat_o"]]), waits=[st1], inc="tpo")
    P["tok_stat_ar"] = ar_tok
    P["tok_stat_ld"] = S.dma("gpsimd", smalls[0:1, 5:7], P["h_stat_o"],
                             waits=[ar_tok], sem="d_wc")
    P["tok_wprep_bufs_free"] = S.barrier_token("d_wps")
    P["tok_wprep_v"] = S.barrier_token("tv")


def emit_wprep_quant(S, P):
    ar1 = P["ar1"]
    ar1b = P["xn_sb"]
    ar1i = ar1.bitcast(I32)
    smalls = P["smalls"]
    pb = P["pb"]
    srcs = P["wp_srcs"]
    head = [P["tok_stat_ld"], S.barrier_token("tt"), P["tok_wprep_v"]]

    c1_ = S.emit("vector", lambda e: e.tensor_scalar(
        out=smalls[0:1, 5:7], in0=smalls[0:1, 5:7],
        scalar1=1.0 / (D * FF), scalar2=EPS, op0=ALU.mult, op1=ALU.max),
        waits=head, inc="tv")
    c1b_ = S.emit("vector", lambda e: e.tensor_copy(
        out=smalls[0:1, 3:5], in_=smalls[0:1, 5:7]), waits=[c1_], inc="tv")
    c2_ = S.emit("vector", lambda e: e.reciprocal(
        out=smalls[0:1, 5:7], in_=smalls[0:1, 5:7]), waits=[c1b_], inc="tv")
    bc = S.emit("tensor", lambda e: e.matmul(
        pb[0][:, 0:4], P["onesrow_sb"], smalls[0:1, 3:7],
        start=True, stop=True), waits=[c2_, S.barrier_token("tt")],
        inc="tt")
    sc = S.emit("vector", lambda e: e.tensor_copy(
        out=smalls[:, 4:8], in_=pb[0][:, 0:4]), waits=[bc], inc="tv")
    s1col = smalls[:, 6:7]
    s2col = smalls[:, 7:8]
    P["s1col"], P["s2col"] = s1col, s2col
    P["m1col"] = smalls[:, 4:5]
    P["m2col"] = smalls[:, 5:6]

    wslot = [ar1[:, 0:2048], ar1[:, 2048:4096]]
    yb = ar1[:, 4096:5120]
    tib = ar1i[:, 5120:6144]
    c1b = ar1[:, 6144:7168]
    cmb = ar1[:, 7168:8192]
    w2qv = P["h_w2q"].rearrange("(a p) d -> a p d", p=128)
    dsts = [P["h_w1q"][0:128], P["h_w1q"][128:256],
            w2qv[0:4].transpose([1, 0, 2]), w2qv[4:8].transpose([1, 0, 2])]
    slot_store = [None, None]
    slot_lastread = [None, None]
    qprev = sc
    ag1 = None
    for i in range(4):
        scol = s1col if i < 2 else s2col
        for q in range(4):
            sl = (i * 4 + q) % 2
            lsrc = srcs[i][:, q * 2048:(q + 1) * 2048] if i < 2 else \
                srcs[i][:, q]
            wv = [P["tok_attn_tt_c0"]] if (i, q) == (0, 0) else []
            if slot_store[sl]:
                wv.append(slot_store[sl])
            tw = S.dma("scalar", wslot[sl], lsrc, waits=wv, sem="d_wps")
            for b in range(2):
                bsl = slice(b * 1024, (b + 1) * 1024)
                q1 = S.emit("vector", lambda e, sl=sl, bsl=bsl, scol=scol:
                            e.tensor_scalar(
                    out=yb, in0=wslot[sl][:, bsl], scalar1=scol,
                    scalar2=None, op0=ALU.mult), waits=[tw, sc, qprev],
                    inc="tv")
                q2 = S.emit("scalar", lambda e: e.copy(out=tib, in_=yb),
                            waits=[q1], inc="tsc")
                q3 = S.emit("vector", lambda e: e.scalar_tensor_tensor(
                    out=c1b, in0=tib, scalar=0.5, in1=yb, op0=ALU.add,
                    op1=ALU.is_lt), waits=[q2], inc="tv")
                q4 = S.emit("vector", lambda e: e.scalar_tensor_tensor(
                    out=cmb, in0=tib, scalar=-0.5, in1=yb, op0=ALU.add,
                    op1=ALU.is_gt), waits=[q3], inc="tv")
                q5 = S.emit("vector", lambda e: e.tensor_add(
                    out=yb, in0=tib, in1=c1b), waits=[q4], inc="tv")
                q6 = S.emit("vector", lambda e: e.tensor_sub(
                    out=c1b, in0=yb, in1=cmb), waits=[q5], inc="tv")
                q7 = S.emit("vector", lambda e, sl=sl, b=b: e.tensor_scalar(
                    out=ar1b[:, sl * 4096 + b * 2048:
                             sl * 4096 + b * 2048 + 1024],
                    in0=c1b, scalar1=1.0, scalar2=-1.0,
                    op0=ALU.min, op1=ALU.max), waits=[q6], inc="tv")
                qprev = q7
            dsthbm = dsts[i][:, q * 2048:(q + 1) * 2048] if i < 2 else \
                dsts[i][:, q]
            st = S.dma("scalar", dsthbm.rearrange("p (b r) -> p b r", b=2),
                       ar1b[:, sl * 4096:sl * 4096 + 4096].rearrange(
                           "p (b r) -> p b r", b=2)[:, :, 0:1024],
                       waits=[qprev], sem="d_wps")
            slot_store[sl] = st
            if i == 1 and q == 3:
                ag1 = S.emit("gpsimd", lambda e: e.collective_compute(
                    "AllGather", ALU.bypass, replica_groups=G8,
                    ins=[P["h_w1q"]], outs=[P["h_w1g"]]),
                    waits=[st], inc="tpo")
                P["tok_w1g_ffn"] = ag1
    ag2 = S.emit("gpsimd", lambda e: e.collective_compute(
        "AllGather", ALU.bypass, replica_groups=G8,
        ins=[P["h_w2q"]], outs=[P["h_w2g"]]),
        waits=[ag1, slot_store[0], slot_store[1]], inc="tpo")
    P["tok_wq"] = ag2
    P["tok_quant_done"] = S.barrier_token("d_wps")


def emit_wprep_ags(S, P):
    P["tok_ag"] = S.emit("gpsimd", lambda e: e.collective_compute(
        "AllGather", ALU.bypass, replica_groups=G4,
        ins=[P["h_tok2"]], outs=[P["h_tok2g"]]),
        waits=[S.barrier_token("d_t2"), P["tok_wq"]], inc="tpo")
    ag_wq = S.emit("gpsimd", lambda e: e.collective_compute(
        "AllGather", ALU.bypass, replica_groups=G8,
        ins=[P["h_wqcs"]], outs=[P["h_wqc"]]),
        waits=[P["tok_ag"], P["tok_wc_wq"]], inc="tpo")
    ag_wkv = S.emit("gpsimd", lambda e: e.collective_compute(
        "AllGather", ALU.bypass, replica_groups=G8,
        ins=[P["h_wkvcs"]], outs=[P["h_wkvc"]]),
        waits=[ag_wq, P["tok_wc_wkv"]], inc="tpo")
    ag_wo = S.emit("gpsimd", lambda e: e.collective_compute(
        "AllGather", ALU.bypass, replica_groups=G8,
        ins=[P["h_wocs"]], outs=[P["h_woc"]]), waits=[ag_wkv, P["tok_wc_wo"]],
        inc="tpo")
    P["tok_wqg"] = ag_wq
    P["tok_wkvg"] = ag_wkv
    P["tok_wog"] = ag_wo


def emit_l0ffn(S, P):
    ar1b = P["xn_sb"]
    ar2b, ar3b = P["pk"], P["pk2"]
    ar4, ar56 = P["ar4"], P["ar56"]
    ar4b, ar56b = P["ar4b"], P["ar56b"]
    ar56i, ar4i = P["ar56i"], P["ar4"].bitcast(I32)
    smalls, smb = P["smalls"], P["smb"]
    pc = P["pconst_sb"]
    pb = P["pb"]
    pbt_f = P["pbt"].bitcast(F32)
    pbt2_f = P["pbt2"].bitcast(F32)
    banks = P["pb"] + [pbt_f[:, 0:512], pbt2_f[:, 0:512]]
    xn_sb = P["xn_sb"]
    x_sb = P["x_sb"]
    identb = P["identb"]
    s1col, s2col = P["s1col"], P["s2col"]

    ffw = [S.barrier_token("d_t2"), S.barrier_token("tv"),
           S.barrier_token("tt"), S.barrier_token("tsc"),
           S.barrier_token("d_rp"), S.barrier_token("d_gb"),
           S.barrier_token("d_wp"), S.barrier_token("d_wps")]

    amcol = smalls[:, 10:11]
    invcols = smalls[:, 12:20]
    sccol = smalls[:, 11:12]
    xfb = ar4[:, 0:2048]
    yb = ar4[:, 2048:4096]
    tib = ar56i[:, 0:2048]
    c1b = ar56[:, 2048:4096]
    cmb = ar4[:, 0:2048]
    xqb = ar56b[:, 0:2048]

    prev = None
    for t in range(NT):
        w = ffw + ([prev] if prev else [])
        a0 = S.emit("vector", lambda e, t=t: e.tensor_scalar(
            out=xfb, in0=x_sb[:, t * D:(t + 1) * D],
            scalar1=P["mucols"][:, t:t + 1], scalar2=P["rstdcols"][:, t:t + 1],
            op0=ALU.subtract, op1=ALU.mult), waits=w, inc="tv")
        xt = xfb
        a1 = S.emit("vector", lambda e, xt=xt: e.tensor_reduce(
            out=amcol, in_=xt, axis=AX.X, op=ALU.max,
            apply_absolute_value=True), waits=[a0], inc="tv")
        a2 = S.emit("vector", lambda e: e.tensor_scalar(
            out=amcol, in0=amcol, scalar1=EPS, scalar2=None, op0=ALU.max),
            waits=[a1], inc="tv")
        a3 = S.emit("vector", lambda e: e.reciprocal(out=sccol, in_=amcol),
                    waits=[a2], inc="tv")
        a4 = S.emit("vector", lambda e: e.tensor_scalar(
            out=sccol, in0=sccol, scalar1=127.0, scalar2=None, op0=ALU.mult),
            waits=[a3], inc="tv")
        a5 = S.emit("vector", lambda e, t=t: e.tensor_scalar(
            out=invcols[:, t:t + 1], in0=amcol, scalar1=1.0 / 127.0,
            scalar2=None, op0=ALU.mult), waits=[a4], inc="tv")
        a5b = S.emit("vector", lambda e, t=t: e.tensor_tensor(
            out=invcols[:, t:t + 1], in0=invcols[:, t:t + 1], in1=P["m1col"],
            op=ALU.mult), waits=[a5], inc="tv")
        q1 = S.emit("vector", lambda e, xt=xt: e.tensor_scalar(
            out=yb, in0=xt, scalar1=sccol, scalar2=None, op0=ALU.mult),
            waits=[a5b], inc="tv")
        q2 = S.emit("scalar", lambda e: e.copy(out=tib, in_=yb),
                    waits=[q1] + ffw, inc="tsc")
        q3 = S.emit("vector", lambda e: e.scalar_tensor_tensor(
            out=c1b, in0=tib, scalar=0.5, in1=yb, op0=ALU.add,
            op1=ALU.is_lt), waits=[q2], inc="tv")
        q4 = S.emit("vector", lambda e: e.scalar_tensor_tensor(
            out=cmb, in0=tib, scalar=-0.5, in1=yb, op0=ALU.add,
            op1=ALU.is_gt), waits=[q3], inc="tv")
        q5 = S.emit("vector", lambda e: e.tensor_add(
            out=yb, in0=tib, in1=c1b), waits=[q4], inc="tv")
        q6 = S.emit("vector", lambda e: e.tensor_sub(
            out=xqb, in0=yb, in1=cmb), waits=[q5], inc="tv")
        tts = []
        for dc in range(16):
            bank = P["pbt"] if dc < 8 else P["pbt2"]
            kk8 = dc % 8
            w2 = [q6]
            if prev and dc == 0:
                w2 = [q6, prev]
            tts.append(S.emit("tensor", lambda e, dc=dc, bank=bank, kk8=kk8:
                              e.transpose(
                bank[:, kk8 * 128:(kk8 + 1) * 128],
                xqb[:, dc * 128:(dc + 1) * 128], identb),
                waits=w2 if dc == 0 else [], inc="tt"))
        e1 = S.emit("vector", lambda e, t=t: e.tensor_copy(
            out=ar3b.rearrange("p (dc t r) -> p dc t r", dc=16, t=8)[
                :, 0:8, t, :].rearrange("p a b -> p (a b)") if False else
            ar3b.rearrange("p (dc tt r) -> p dc tt r", dc=16, tt=8)[
                :, 0:8, t], in_=P["pbt"].rearrange(
                "p (a r) -> p a r", a=8)), waits=[tts[7]], inc="tv")
        e2 = S.emit("vector", lambda e, t=t: e.tensor_copy(
            out=ar3b.rearrange("p (dc tt r) -> p dc tt r", dc=16, tt=8)[
                :, 8:16, t], in_=P["pbt2"].rearrange(
                "p (a r) -> p a r", a=8)), waits=[tts[15]], inc="tv")
        prev = e2

    xqT_done = prev

    gbc = ar1b[:, 0:8192]
    bbc = ar1b[:, 8192:16384]
    tg = S.dma("gpsimd", gbc, P["vrows"][1:2, :].broadcast_to([128, FF]),
               waits=[xqT_done, P["tok_quant_done"]], sem="d_fg")
    tb = S.dma("gpsimd", bbc, P["vrows"][2:3, :].broadcast_to([128, FF]),
               waits=[xqT_done], sem="d_fg")

    b1bc = ar4[:, 0:512]
    hbou = [ar4[:, 512 + i * 512:1024 + i * 512] for i in range(2)]
    hbou_b = [ar4.bitcast(BF16)[:, 1024 + i * 512:1536 + i * 512]
              for i in range(2)]
    slab = [ar2b[:, 0:8192], ar2b[:, 8192:16384]]
    slab_tok = [None, None]
    evs = {}
    mm_last = {}
    ev_last = {}
    prev_b1 = None
    for fc in range(16):
        sl = fc % 2
        fsl = slice(fc * 512, (fc + 1) * 512)
        w = [P["tok_w1g_ffn"], xqT_done]
        if fc >= 2:
            w.append(mm_last[fc - 2])
        w = [x for x in w if x]
        src = P["h_w1g"][:, fsl].rearrange("(dc p) f -> dc p f", p=128)
        sv = slab[sl].rearrange("p (dc f) -> p dc f", dc=16)
        lta = S.dma("sync", sv[:, 0:8], src[0:8].transpose([1, 0, 2]),
                    waits=w, sem="d_ff")
        ltb = S.dma("scalar", sv[:, 8:16], src[8:16].transpose([1, 0, 2]),
                    waits=w, sem="d_ffs")
        slab_tok[sl] = lta
        bt = S.dma("gpsimd", ar4[:, 0:512],
                   P["vrows"][0:1, fsl].broadcast_to([128, 512]),
                   waits=[prev_b1] if prev_b1 else ffw, sem="d_fg")
        mms = []
        mm = None
        for dc in range(16):
            for t in range(NT):
                w2 = [lta, ltb]
                if fc >= 1 and dc == 0 and t == 0:
                    w2.append(ev_last[fc - 1])
                lhs = ar3b.rearrange("p (dc tt r) -> p dc tt r",
                                     dc=16, tt=8)[:, dc, t]
                mm = S.emit("tensor", lambda e, lhs=lhs, sl=sl, dc=dc, t=t:
                            e.matmul(
                    banks[t][:, :], lhs,
                    slab[sl][:, dc * 512:(dc + 1) * 512],
                    start=(dc == 0), stop=(dc == 15)),
                    waits=w2 if (dc == 0 and t == 0) else [], inc="tt")
                mms.append(mm)
        eprev = None
        if fc == 0:
            slot_last = [None, None]
        for t in range(NT):
            wv = [mms[-(NT - t)], bt] + ([eprev] if eprev else [])
            if slot_last[t % 2]:
                wv.append(slot_last[t % 2])
            eprev = S.emit("vector", lambda e, t=t, fc=fc: e.scalar_tensor_tensor(
                out=hbou_b[t % 2], in0=banks[t][:, :],
                scalar=invcols[:, t:t + 1], in1=ar4[:, 0:512],
                op0=ALU.mult, op1=ALU.add), waits=wv, inc="tv")
            slot_last[t % 2] = S.dma("sync" if t % 2 == 0 else "scalar",
                                     P["h_h"][t * 128:(t + 1) * 128, fsl],
                                     hbou_b[t % 2], waits=[eprev],
                                     sem="d_ff" if t % 2 == 0 else "d_ffs")
            evs[(fc, t)] = slot_last[t % 2]
        mm_last[fc] = mms[-1]
        ev_last[fc] = eprev
        prev_b1 = eprev
    ffn1_done = S.barrier_token("d_ff")
    ffn1s_done = S.barrier_token("d_ffs")

    gtile = P["ar3"]
    htbs = [P["pk"][:, 0:8192], P["pk"][:, 8192:16384]]
    ydb = P["ar4"][:, 0:2048]
    c1db = P["ar4"][:, 2048:4096]
    tidb = P["ar56i"][:, 0:2048]
    cmdb = P["ar56"][:, 2048:4096]
    hqblk = smb[:, 96:2144]
    hbnc = smb[:, 2144:3168]
    mucol = smalls[:, 20:21]
    esqcol = smalls[:, 21:22]
    prev_spill = None
    bounce_free = None
    htb_free = [None, None]
    gtile_free = None
    scratch_free = None
    for t in range(NT):
        sl = t % 2
        htb = htbs[sl]
        w = [ffn1_done, ffn1s_done]
        if htb_free[sl]:
            w.append(htb_free[sl])
        if t < 2:
            w.append(S.barrier_token("tv"))
            w.append(S.barrier_token("tsc"))
        l1a = S.dma("sync", htb[:, 0:4096],
                    P["h_h"][t * 128:(t + 1) * 128, 0:4096],
                    waits=w, sem="d_hp")
        l1b = S.dma("scalar", htb[:, 4096:8192],
                    P["h_h"][t * 128:(t + 1) * 128, 4096:8192],
                    waits=w, sem="d_hps")
        wv = [l1a, l1b]
        if gtile_free:
            wv.append(gtile_free)
        g1 = S.emit("scalar", lambda e, htb=htb: e.activation(
            out=gtile, in_=htb, func=AF.Gelu), waits=wv, inc="tsc")
        htb_free[sl] = g1
        m1 = S.emit("vector", lambda e: e.tensor_reduce(
            out=mucol, in_=gtile, axis=AX.X, op=ALU.add), waits=[g1],
            inc="tv")
        m2 = S.emit("vector", lambda e: e.tensor_scalar(
            out=mucol, in0=mucol, scalar1=1.0 / FF, scalar2=None,
            op0=ALU.mult), waits=[m1], inc="tv")
        sq_ac = []
        for i4 in range(4):
            wv = [g1] + ([sq_ac[-1]] if sq_ac else [])
            if i4 == 0 and scratch_free:
                wv.append(scratch_free)
            sq_ac.append(S.emit("scalar", lambda e, i4=i4: e.activation(
                out=ydb, in_=gtile[:, i4 * 2048:(i4 + 1) * 2048],
                func=AF.Square, scale=float(FF) ** -0.5,
                accum_out=smalls[:, 22 + i4:23 + i4]), waits=wv, inc="tsc"))
        s1_ = S.emit("vector", lambda e: e.tensor_add(
            out=esqcol, in0=smalls[:, 22:23], in1=smalls[:, 23:24]),
            waits=[sq_ac[-1]], inc="tv")
        s2_ = S.emit("vector", lambda e: e.tensor_add(
            out=smalls[:, 22:23], in0=smalls[:, 24:25], in1=smalls[:, 25:26]),
            waits=[s1_], inc="tv")
        s3_ = S.emit("vector", lambda e: e.tensor_add(
            out=esqcol, in0=esqcol, in1=smalls[:, 22:23]), waits=[s2_],
            inc="tv")
        v1 = S.emit("vector", lambda e: e.tensor_mul(
            out=smalls[:, 22:23], in0=mucol, in1=mucol), waits=[s3_, m2],
            inc="tv")
        v2 = S.emit("vector", lambda e: e.tensor_sub(
            out=esqcol, in0=esqcol, in1=smalls[:, 22:23]), waits=[v1],
            inc="tv")
        v3 = S.emit("scalar", lambda e: e.activation(
            out=esqcol, in_=esqcol, func=AF.Sqrt, bias=pc[:, 16:17],
            scale=1.0), waits=[v2], inc="tsc")
        v4 = S.emit("vector", lambda e: e.reciprocal(
            out=esqcol, in_=esqcol), waits=[v3], inc="tv")
        z1 = S.emit("vector", lambda e: e.tensor_scalar(
            out=gtile, in0=gtile, scalar1=mucol, scalar2=esqcol,
            op0=ALU.subtract, op1=ALU.mult),
            waits=[v4, sq_ac[-1], tg, tb], inc="tv")
        z2 = S.emit("vector", lambda e: e.tensor_mul(
            out=gtile, in0=gtile, in1=gbc), waits=[z1], inc="tv")
        z3 = S.emit("vector", lambda e: e.tensor_add(
            out=gtile, in0=gtile, in1=bbc), waits=[z2], inc="tv")
        a1 = S.emit("vector", lambda e: e.tensor_reduce(
            out=amcol, in_=gtile, axis=AX.X, op=ALU.max,
            apply_absolute_value=True), waits=[z3], inc="tv")
        a2 = S.emit("vector", lambda e: e.tensor_scalar(
            out=amcol, in0=amcol, scalar1=EPS, scalar2=None, op0=ALU.max),
            waits=[a1], inc="tv")
        a3 = S.emit("vector", lambda e: e.reciprocal(out=sccol, in_=amcol),
                    waits=[a2], inc="tv")
        a4 = S.emit("vector", lambda e: e.tensor_scalar(
            out=sccol, in0=sccol, scalar1=127.0, scalar2=None, op0=ALU.mult),
            waits=[a3], inc="tv")
        a5 = S.emit("vector", lambda e, t=t: e.tensor_scalar(
            out=invcols[:, t:t + 1], in0=amcol, scalar1=1.0 / 127.0,
            scalar2=None, op0=ALU.mult), waits=[a4], inc="tv")
        a5b = S.emit("vector", lambda e, t=t: e.tensor_tensor(
            out=invcols[:, t:t + 1], in0=invcols[:, t:t + 1], in1=P["m2col"],
            op=ALU.mult), waits=[a5], inc="tv")
        qprev = a5b
        blk_free = None
        ev_prev_grp = None
        for i4 in range(4):
            bsl = slice(i4 * 2048, (i4 + 1) * 2048)
            q1 = S.emit("vector", lambda e, bsl=bsl: e.tensor_scalar(
                out=ydb, in0=gtile[:, bsl], scalar1=sccol, scalar2=None,
                op0=ALU.mult), waits=[qprev], inc="tv")
            if i4 == 3:
                gtile_free = q1
            q2 = S.emit("scalar", lambda e: e.copy(out=tidb, in_=ydb),
                        waits=[q1], inc="tsc")
            q3 = S.emit("vector", lambda e: e.scalar_tensor_tensor(
                out=c1db, in0=tidb, scalar=0.5, in1=ydb, op0=ALU.add,
                op1=ALU.is_lt), waits=[q2], inc="tv")
            q4 = S.emit("vector", lambda e: e.scalar_tensor_tensor(
                out=cmdb, in0=tidb, scalar=-0.5, in1=ydb, op0=ALU.add,
                op1=ALU.is_gt), waits=[q3], inc="tv")
            q5 = S.emit("vector", lambda e: e.tensor_add(
                out=ydb, in0=tidb, in1=c1db), waits=[q4], inc="tv")
            wv = [q5]
            if blk_free:
                wv.append(blk_free)
            q6 = S.emit("vector", lambda e: e.tensor_sub(
                out=hqblk, in0=ydb, in1=cmdb), waits=wv, inc="tv")
            qprev = q6
            if i4 == 3:
                scratch_free = q6
            for gj in range(2):
                g8 = i4 * 2 + gj
                tts = []
                for i8 in range(8):
                    wv = []
                    if i8 == 0:
                        wv = [q6]
                        if ev_prev_grp:
                            wv.append(ev_prev_grp)
                    tts.append(S.emit("tensor", lambda e, gj=gj, i8=i8:
                                      e.transpose(
                        P["pbt"][:, i8 * 128:(i8 + 1) * 128],
                        hqblk[:, (gj * 8 + i8) * 128:
                              (gj * 8 + i8 + 1) * 128],
                        identb), waits=wv, inc="tt"))
                ev = S.emit("vector", lambda e: e.tensor_copy(
                    out=hbnc, in_=P["pbt"][:, :]),
                    waits=[tts[-1]] +
                    ([bounce_free] if bounce_free else []), inc="tv")
                ev_prev_grp = ev
                blk_free = ev
                dst = P["h_hqT"].rearrange("(fc p) tok -> fc p tok", p=128)[
                    g8 * 8:(g8 + 1) * 8, :,
                    t * 128:(t + 1) * 128].transpose([1, 0, 2])
                prev_spill = S.dma("sync", dst,
                                   hbnc.rearrange("p (a r) -> p a r", a=8),
                                   waits=[ev], sem="d_hp")
                bounce_free = prev_spill
    hq_done = S.barrier_token("d_hp")

    w2b = [ar2b[:, 0:512], ar2b[:, 512:1024]]
    hqb = [ar2b[:, 1024 + i * 1024: 2048 + i * 1024] for i in range(2)]
    b2bc = ar4[:, 0:2048]
    tb2 = S.dma("gpsimd", b2bc,
                P["vrows"][3:4, 0:2048].broadcast_to([128, 2048]),
                waits=[hq_done, S.barrier_token("tv")], sem="d_fg")
    x1parts = []
    slot_mm = [None, None]
    prev_dq_ev = None
    for dq in range(4):
        dsl = slice(dq * 512, (dq + 1) * 512)
        mm = None
        for fc in range(64):
            sl = fc % 2
            w = [hq_done, P["tok_wq"]]
            if slot_mm[sl] is not None:
                w.append(slot_mm[sl])
            qa, sa = ("sync", "d_f2") if fc % 2 == 0 else \
                ("scalar", "d_f2s")
            qb, sb2 = ("scalar", "d_f2s") if fc % 2 == 0 else \
                ("sync", "d_f2")
            wt = S.dma(qa, w2b[sl],
                       P["h_w2g"][fc * 128:(fc + 1) * 128, dsl],
                       waits=w, sem=sa)
            ht = S.dma(qb, hqb[sl],
                       P["h_hqT"][fc * 128:(fc + 1) * 128, :],
                       waits=w, sem=sb2)
            for t in range(NT):
                w2_ = []
                if t == 0:
                    w2_ = [wt, ht]
                    if fc == 0 and prev_dq_ev is not None:
                        w2_.append(prev_dq_ev)
                mm = S.emit("tensor", lambda e, sl=sl, t=t, fc=fc: e.matmul(
                    banks[t][:, :], hqb[sl][:, t * 128:(t + 1) * 128],
                    w2b[sl], start=(fc == 0), stop=(fc == 63)),
                    waits=w2_, inc="tt")
            slot_mm[sl] = mm
        eprev = None
        for t in range(NT):
            xsl = x_sb[:, t * D + dq * 512: t * D + (dq + 1) * 512]
            e1 = S.emit("vector", lambda e, t=t, dq=dq: e.scalar_tensor_tensor(
                out=ar4[:, 2048:2560], in0=banks[t][:, :],
                scalar=invcols[:, t:t + 1],
                in1=ar4[:, dq * 512:(dq + 1) * 512],
                op0=ALU.mult, op1=ALU.add),
                waits=[mm, tb2] + ([eprev] if eprev else []), inc="tv")
            eprev = S.emit("vector", lambda e, xsl=xsl: e.tensor_add(
                out=xsl, in0=ar4[:, 2048:2560], in1=xsl), waits=[e1],
                inc="tv")
            x1parts.append(eprev)
        prev_dq_ev = eprev
    P["tok_x1p"] = S.barrier_token("tv")


def emit_x1(S, P):
    x_sb = P["x_sb"]
    prevs = []

    def consume(t, recv, tok):
        w = [tok, P["tok_x1p"]] + prevs[-1:]
        r = S.emit("vector", lambda e, t=t, recv=recv: e.tensor_add(
            out=x_sb[:, t * D:(t + 1) * D], in0=recv,
            in1=x_sb[:, t * D:(t + 1) * D]), waits=w, inc="tv")
        prevs.append(r)
        return r

    emit_attn0_gather(S, P, to_out=False, extra_waits=[P["tok_x1p"]],
                      consume_fn=consume)
    P["tok_x1"] = S.barrier_token("tv")


def emit_l1(S, P):
    x_sb = P["x_sb"]
    ar1, ar4 = P["ar1"], P["ar4"]
    ar1b = P["xn_sb"]
    ar2b, ar3b = P["pk"], P["pk2"]
    ar4b, ar56b = P["ar4b"], P["ar56b"]
    smalls, smb = P["smalls"], P["smb"]
    pc = P["pconst_sb"]
    pb = P["pb"]
    banks = pb + [P["pbt"].bitcast(F32)[:, 0:512],
                  P["pbt2"].bitcast(F32)[:, 0:512]]
    identb = P["identb"]
    ar3v = ar3b.rearrange("p (dc tt r) -> p dc tt r", dc=16, tt=8)

    l1w = [P["tok_x1"], S.barrier_token("tt"), S.barrier_token("tsc"),
           S.barrier_token("d_f2"), S.barrier_token("d_gb"),
           S.barrier_token("d_fg")]
    cut = os.environ.get("KBD_L1CUT", "")

    def cut_here(name):
        if cut == name:
            P["tok_a1_added"] = S.barrier_token("tv")
            return True
        return False

    mu = smalls[:, 0:1]; esq = smalls[:, 1:2]; var = smalls[:, 2:3]
    std = smalls[:, 3:4]; rstd = smalls[:, 4:5]; sc5 = smalls[:, 5:6]
    epsc = pc[:, 16:17]

    prev = None
    for t in range(NT):
        xt = x_sb[:, t * D:(t + 1) * D]
        w = l1w + ([prev] if prev else [])
        t1 = S.emit("vector", lambda e, xt=xt: e.tensor_reduce(
            out=mu, in_=xt, axis=AX.X, op=ALU.add), waits=w, inc="tv")
        t2 = S.emit("scalar", lambda e, xt=xt: e.activation(
            out=P["ar2"][:, 0:D], in_=xt, func=AF.Square,
            scale=float(D) ** -0.5, accum_out=esq), waits=w, inc="tsc")
        t3 = S.emit("vector", lambda e: e.tensor_scalar(
            out=mu, in0=mu, scalar1=1.0 / D, scalar2=None, op0=ALU.mult),
            waits=[t1, t2], inc="tv")
        t5 = S.emit("vector", lambda e: e.tensor_mul(
            out=sc5, in0=mu, in1=mu), waits=[t3], inc="tv")
        t5b = S.emit("vector", lambda e: e.tensor_sub(
            out=var, in0=esq, in1=sc5), waits=[t5], inc="tv")
        t6 = S.emit("scalar", lambda e: e.activation(
            out=std, in_=var, func=AF.Sqrt, bias=epsc, scale=1.0),
            waits=[t5b], inc="tsc")
        t7 = S.emit("vector", lambda e: e.reciprocal(out=rstd, in_=std),
                    waits=[t6], inc="tv")
        prev = S.emit("vector", lambda e, t=t, xt=xt: e.tensor_scalar(
            out=ar1b[:, t * D:(t + 1) * D], in0=xt, scalar1=mu, scalar2=rstd,
            op0=ALU.subtract, op1=ALU.mult), waits=[t7], inc="tv")
    xn1_done = prev

    prevt = None
    for t in range(NT):
        tts = []
        for dc in range(16):
            bank = P["pbt"] if dc < 8 else P["pbt2"]
            kk8 = dc % 8
            w = []
            if dc == 0:
                w = [xn1_done] + ([prevt] if prevt else [])
            tts.append(S.emit("tensor", lambda e, t=t, dc=dc, bank=bank,
                              kk8=kk8: e.transpose(
                bank[:, kk8 * 128:(kk8 + 1) * 128],
                ar1b[:, t * D + dc * 128:t * D + (dc + 1) * 128], identb),
                waits=w, inc="tt"))
        e1 = S.emit("vector", lambda e, t=t: e.tensor_copy(
            out=ar3v[:, 0:8, t], in_=P["pbt"].rearrange(
                "p (a r) -> p a r", a=8)), waits=[tts[7]], inc="tv")
        e2 = S.emit("vector", lambda e, t=t: e.tensor_copy(
            out=ar3v[:, 8:16, t], in_=P["pbt2"].rearrange(
                "p (a r) -> p a r", a=8)), waits=[tts[15], e1], inc="tv")
        prevt = e2
    xn1T_done = prevt
    if cut_here("B"):
        return

    cst0 = P["prod_sb"]
    cst1 = P["pv_sb"]
    dmask_sb = smb[:, 3168:3680]
    tk_c0 = S.dma("sync", cst0, P["cstab_in"][:, 0:512], waits=l1w,
                  sem="d_l1")
    tk_c1 = S.dma("sync", cst1, P["cstab_in"][:, 512:1024], waits=l1w,
                  sem="d_l1")
    tk_dm = S.dma("sync", dmask_sb, P["dmask_in"], waits=l1w, sem="d_l1")

    def costl(t):
        base = (cst0 if t < 4 else cst1)
        return base[:, (t % 4) * 128:(t % 4) * 128 + 64]

    def sintl(t):
        base = (cst0 if t < 4 else cst1)
        return base[:, (t % 4) * 128 + 64:(t % 4) * 128 + 128]

    slabs = [ar1b[:, 0:8192], ar1b[:, 8192:16384]]
    bias_bufs = [ar4[:, 0:512], ar4[:, 512:1024]]
    q_tm = ar2b
    kv_tm = ar56b[:, 0:4096]
    slab_free = [None, None]
    bias_free = [None, None]
    evac_last = None
    mm_last = None
    for ch in range(5):
        sl = ch % 2
        if ch < 4:
            wsrc = P["h_wqc"][:, ch * 512:(ch + 1) * 512]
            agw = P["tok_wqg"]
            brow, bsl = 6, slice(ch * 512, (ch + 1) * 512)
        else:
            wsrc = P["h_wkvc"]
            agw = P["tok_wkvg"]
            brow, bsl = 7, slice(0, 512)
        w = [xn1T_done, agw]
        if slab_free[sl]:
            w.append(slab_free[sl])
        psrc = wsrc.rearrange("(dc p) f -> dc p f", p=128)
        pdst = slabs[sl].rearrange("p (dc f) -> p dc f", dc=16)
        lt_a = S.dma("sync", pdst[:, 0:8], psrc[0:8].transpose([1, 0, 2]),
                     waits=w, sem="d_l1")
        lt = S.dma("scalar", pdst[:, 8:16], psrc[8:16].transpose([1, 0, 2]),
                   waits=w, sem="d_l1s")
        bw = [bias_free[sl]] if bias_free[sl] else l1w
        bt = S.dma("gpsimd", bias_bufs[sl],
                   P["vrows"][brow:brow + 1, bsl].broadcast_to([128, 512]),
                   waits=bw, sem="d_fg")
        mm = None
        for dc in range(16):
            for t in range(NT):
                w2 = []
                if dc == 0 and t == 0:
                    w2 = [lt_a, lt] + ([evac_last] if evac_last else [])
                mm = S.emit("tensor", lambda e, dc=dc, t=t, sl=sl: e.matmul(
                    banks[t][:, :], ar3v[:, dc, t],
                    slabs[sl][:, dc * 512:(dc + 1) * 512],
                    start=(dc == 0), stop=(dc == 15)), waits=w2, inc="tt")
        slab_free[sl] = mm
        mm_last = mm
        eprev = None
        for t in range(NT):
            if ch < 4:
                dst = q_tm[:, t * 2048 + ch * 512:t * 2048 + (ch + 1) * 512]
            else:
                dst = kv_tm[:, t * 512:(t + 1) * 512]
            eprev = S.emit("vector", lambda e, t=t, dst=dst, sl=sl:
                           e.tensor_tensor(
                out=dst, in0=banks[t][:, :], in1=bias_bufs[sl],
                op=ALU.add),
                waits=[mm, bt] + ([eprev] if eprev else []), inc="tv")
        bias_free[sl] = eprev
        evac_last = eprev
    proj_done = evac_last
    if cut_here("D"):
        return

    r1 = ar4b[:, 0:1024]; r2 = ar4b[:, 1024:2048]
    r3 = ar4b[:, 2048:3072]; r4 = ar4b[:, 3072:4096]
    rprev = proj_done
    for t in range(NT):
        for which in (0, 1):
            if which == 0:
                nh = 16
                base = q_tm[:, t * 2048:(t + 1) * 2048]
            else:
                nh = 2
                base = kv_tm[:, t * 512:t * 512 + 256]
            v3 = base.rearrange("p (h d) -> p h d", h=nh)
            d1 = v3[:, :, 0:64]
            d2 = v3[:, :, 64:128]
            cosb = costl(t).rearrange("p (o f) -> p o f", o=1).broadcast_to(
                [128, nh, 64])
            sinb = sintl(t).rearrange("p (o f) -> p o f", o=1).broadcast_to(
                [128, nh, 64])
            n64 = nh * 64
            r1f = r1[:, 0:n64].rearrange("p (h d) -> p h d", h=nh)
            r2f = r2[:, 0:n64].rearrange("p (h d) -> p h d", h=nh)
            r3f = r3[:, 0:n64].rearrange("p (h d) -> p h d", h=nh)
            r4f = r4[:, 0:n64].rearrange("p (h d) -> p h d", h=nh)
            o1 = S.emit("vector", lambda e, d1=d1, cosb=cosb, r1f=r1f:
                        e.tensor_mul(out=r1f, in0=d1, in1=cosb),
                        waits=[rprev, tk_c0, tk_c1], inc="tv")
            o2 = S.emit("vector", lambda e, d2=d2, sinb=sinb, r2f=r2f:
                        e.tensor_mul(out=r2f, in0=d2, in1=sinb),
                        waits=[o1], inc="tv")
            o3 = S.emit("vector", lambda e, d1=d1, sinb=sinb, r3f=r3f:
                        e.tensor_mul(out=r3f, in0=d1, in1=sinb),
                        waits=[o2], inc="tv")
            o4 = S.emit("vector", lambda e, d2=d2, cosb=cosb, r4f=r4f:
                        e.tensor_mul(out=r4f, in0=d2, in1=cosb),
                        waits=[o3], inc="tv")
            o5 = S.emit("vector", lambda e, d1=d1, r1f=r1f, r2f=r2f:
                        e.tensor_sub(out=d1, in0=r1f, in1=r2f),
                        waits=[o4], inc="tv")
            rprev = S.emit("vector", lambda e, d2=d2, r3f=r3f, r4f=r4f:
                           e.tensor_add(out=d2, in0=r4f, in1=r3f),
                           waits=[o5], inc="tv")
    rope_done = rprev
    if cut_here("E"):
        return

    for t in range(NT):
        S.dma("sync", P["h_kv"][t * 128:(t + 1) * 128, :],
              kv_tm[:, t * 512:(t + 1) * 512], waits=[rope_done],
              sem="d_kv")
    ag_kv = S.emit("gpsimd", lambda e: e.collective_compute(
        "AllGather", ALU.bypass, replica_groups=G4,
        ins=[P["h_kv"]], outs=[P["h_kvg"]]),
        waits=[S.barrier_token("d_kv")], inc="tpo")
    if DBG_STAGE >= 5:
        ag_W0 = S.emit("gpsimd", lambda e: e.collective_compute(
            "AllGather", ALU.bypass, replica_groups=G8,
            ins=[P["h_W0cs"]], outs=[P["h_W0g"]]),
            waits=[ag_kv, P["tok_wc_W0"]], inc="tpo")
        ag_W1 = S.emit("gpsimd", lambda e: e.collective_compute(
            "AllGather", ALU.bypass, replica_groups=G8,
            ins=[P["h_W1c"]], outs=[P["h_W1g"]]),
            waits=[ag_W0, P["tok_wc_W1"]], inc="tpo")
        P["tok_W0g"], P["tok_W1g"] = ag_W0, ag_W1

    qf_prev = None
    for t in range(NT):
        tts = []
        for dc in range(16):
            bank = P["pbt"] if dc < 8 else P["pbt2"]
            kk8 = dc % 8
            w = []
            if dc == 0:
                w = [rope_done] + ([qf_prev] if qf_prev else [])
            tts.append(S.emit("tensor", lambda e, t=t, dc=dc, bank=bank,
                              kk8=kk8: e.transpose(
                bank[:, kk8 * 128:(kk8 + 1) * 128],
                q_tm[:, t * 2048 + dc * 128:t * 2048 + (dc + 1) * 128],
                identb), waits=w, inc="tt"))
        ev1 = S.emit("vector", lambda e: e.tensor_copy(
            out=ar4b[:, 0:1024], in_=P["pbt"][:, :]), waits=[tts[7]],
            inc="tv")
        ev2 = S.emit("vector", lambda e: e.tensor_copy(
            out=ar4b[:, 1024:2048], in_=P["pbt2"][:, :]),
            waits=[tts[15], ev1], inc="tv")
        qf_prev = S.dma("sync", P["h_qf"].rearrange(
            "(dc p) tok -> dc p tok", p=128)[
            :, :, t * 128:(t + 1) * 128].transpose([1, 0, 2]),
            ar4b[:, 0:2048].rearrange("p (dc r) -> p dc r", dc=16),
            waits=[ev2], sem="d_qf")
    qf_done = S.barrier_token("d_qf")
    if cut_here("G"):
        return

    kfm = ar56b
    kvsrc = P["h_kvg"].rearrange("(cc tt p) c -> cc tt p c", cc=4, tt=8)
    kb_prev = None
    kfm_prev = None
    tp_prev = [None, None]
    for kvh in range(2):
        w = [ag_kv, qf_done] if kvh == 0 else [kb_prev]
        if kvh == 0:
            w.append(S.barrier_token("tt"))
        for cc in range(4):
            kld = S.dma("sync",
                        ar1b[:, kvh * 4096:(kvh + 1) * 4096].rearrange(
                            "p (tt cc r) -> p tt cc r", tt=8, cc=4)[
                            :, :, cc, :],
                        kvsrc[cc, :, :, kvh * 128:(kvh + 1) * 128].transpose(
                            [1, 0, 2]), waits=(w if cc == 0 else []),
                        sem="d_kf")
        kb_prev = kld
        if cut == "H1" and kvh == 0:
            P["tok_a1_added"] = S.barrier_token("d_kf")
            return
        for kt in range(32):
            sl = kt % 2
            tt_, cc_ = kt // 4, kt % 4
            src = ar1b[:, kvh * 4096 + (tt_ * 4 + cc_) * 128:
                       kvh * 4096 + (tt_ * 4 + cc_ + 1) * 128]
            w2 = [kld]
            if tp_prev[sl]:
                w2.append(tp_prev[sl])
            tbank = P["pbt"] if sl == 0 else P["pbt2"]
            tp = S.emit("tensor", lambda e, tbank=tbank, src=src: e.transpose(
                tbank[:, 0:128], src, identb), waits=w2, inc="tt")
            kfm_prev = S.emit("vector", lambda e, kvh=kvh, kt=kt,
                              tbank=tbank: e.tensor_copy(
                out=kfm[:, kvh * 4096 + kt * 128:kvh * 4096 + (kt + 1) * 128],
                in_=tbank[:, 0:128]),
                waits=[tp] + ([kfm_prev] if kfm_prev else []), inc="tv")
            tp_prev[sl] = kfm_prev
    kfm_done = kfm_prev
    if cut_here("H2"):
        return
    for cc in range(4):
        vload = S.dma("sync",
                      ar4b.rearrange("p (tt cc c) -> p tt cc c",
                                     tt=8, cc=4)[:, :, cc, :],
                      kvsrc[cc, :, :, 256:512].transpose([1, 0, 2]),
                      waits=([ag_kv, qf_done] if cc == 0 else []),
                      sem="d_kf")
    if cut_here("H"):
        return

    qbig = [ar1b[:, 0:8192], ar1b[:, 8192:16384]]
    pts = [smb[:, 2144:2656], smb[:, 2656:3168]]
    rdenb = smalls[:, 456:968]
    rden = smb[0:1, 400:912]
    m_my = None
    scale_s = float(128) ** -0.5
    s_free = [None, None]
    pt_free = [None, None]
    o_free = [None, None]
    d_free = [None, None]
    rden_free = None
    rdenb_free = None
    pb4_free = [None, None]
    exp_prev = None
    te_last = None
    qld = [None, None]
    for jg in range(2):
        w = [qf_done, kfm_done]
        qld[jg] = S.dma("sync", qbig[jg].rearrange("p (h c) -> p h c", h=16),
                        P["h_qf"][:, jg * 512:(jg + 1) * 512].rearrange(
                            "(h p) c -> h p c", p=128).transpose([1, 0, 2]),
                        waits=w, sem="d_qr")
        tiles = [(kb, kt) for kb in range(jg * 4 + 4) for kt in range(4)]
        for h in range(16):
            kvh = h // 8
            hp = h % 2
            obank = pb[2 + 3 * hp]
            dband = (pb[3] if hp == 0 else
                     P["pbt"].bitcast(F32))[0:1, 0:512]
            bcbank = (pb[4] if hp == 0 else
                      P["pbt2"].bitcast(F32))[:, 0:512]
            qrhs = qbig[jg][:, h * 512:(h + 1) * 512]
            omm = None
            dmm = None
            pt_ready = {}

            def emit_s(idx):
                nonlocal te_last
                kb, kt = tiles[idx]
                par = idx % 2
                gk = kb * 4 + kt
                w2 = [qld[jg]]
                if s_free[par]:
                    w2.append(s_free[par])
                if jg == 0 and h == 0 and idx == 0:
                    w2.append(vload)
                tok = S.emit("tensor", lambda e, kvh=kvh, gk=gk, par=par,
                             qrhs=qrhs: e.matmul(
                    pb[par][:, 0:512],
                    kfm[:, kvh * 4096 + gk * 128:kvh * 4096 + (gk + 1) * 128],
                    qrhs, start=True, stop=True), waits=w2, inc="tt")
                te_last = tok
                return tok

            def emit_exp(idx, sm):
                nonlocal exp_prev
                kb, kt = tiles[idx]
                par = idx % 2
                pp = idx % 2
                wv = [sm]
                if pt_free[pp]:
                    wv.append(pt_free[pp])
                if exp_prev:
                    wv.append(exp_prev)
                ex = S.emit("scalar", lambda e, par=par, pp=pp: e.activation(
                    out=pts[pp], in_=pb[par][:, 0:512], func=AF.Exp,
                    scale=scale_s), waits=wv, inc="tsc")
                exp_prev = ex
                s_free[par] = ex
                kbb = kb - jg * 4
                if kbb >= 0:
                    if kbb > 0:
                        ex = S.emit("vector", lambda e, pp=pp, kbb=kbb:
                                    e.memset(pts[pp][:, 0:kbb * 128], 0.0),
                                    waits=[ex], inc="tv")
                    ex = S.emit("vector", lambda e, pp=pp, kbb=kbb, kt=kt:
                                e.tensor_tensor(
                        out=pts[pp][:, kbb * 128:(kbb + 1) * 128],
                        in0=pts[pp][:, kbb * 128:(kbb + 1) * 128],
                        in1=dmask_sb[:, kt * 128:(kt + 1) * 128],
                        op=ALU.mult), waits=[ex, tk_dm], inc="tv")
                return ex

            def emit_ov(idx):
                nonlocal omm, dmm, te_last
                kb, kt = tiles[idx]
                pp = idx % 2
                gk = kb * 4 + kt
                first = idx == 0
                last = idx == len(tiles) - 1
                w2 = [pt_ready[idx]]
                if first and o_free[hp]:
                    w2.append(o_free[hp])
                omm = S.emit("tensor", lambda e, gk=gk, pp=pp, kvh=kvh,
                             obank=obank, first=first, last=last: e.matmul(
                    obank[:, 0:512],
                    ar4b[:, gk * 256 + kvh * 128:gk * 256 + (kvh + 1) * 128],
                    pts[pp][:, 0:512], start=first, stop=last),
                    waits=w2, inc="tt")
                w2 = [pt_ready[idx]]
                if first and d_free[hp]:
                    w2.append(d_free[hp])
                dmm = S.emit("tensor", lambda e, pp=pp, dband=dband,
                             first=first, last=last: e.matmul(
                    dband, P["onescol_b"], pts[pp][:, 0:512],
                    start=first, stop=last), waits=w2, inc="tt")
                te_last = dmm
                pt_free[pp] = dmm

            sm = emit_s(0)
            for idx in range(1, len(tiles)):
                pt_ready[idx - 1] = emit_exp(idx - 1, sm)
                sm = emit_s(idx)
                emit_ov(idx - 1)
            pt_ready[len(tiles) - 1] = emit_exp(len(tiles) - 1, sm)
            emit_ov(len(tiles) - 1)
            wv = [dmm]
            if rden_free:
                wv.append(rden_free)
            rc = S.emit("vector", lambda e, dband=dband: e.reciprocal(
                out=rden, in_=dband), waits=wv, inc="tv")
            d_free[hp] = rc
            wv = [rc, P["tok_constsg"]]
            if pb4_free[hp]:
                wv.append(pb4_free[hp])
            bcmm = S.emit("tensor", lambda e, bcbank=bcbank: e.matmul(
                bcbank, P["onesrow_b"], rden, start=True, stop=True),
                waits=wv, inc="tt")
            te_last = bcmm
            rden_free = bcmm
            wv = [bcmm]
            if rdenb_free:
                wv.append(rdenb_free)
            rbe = S.emit("vector", lambda e, bcbank=bcbank: e.tensor_copy(
                out=rdenb, in_=bcbank), waits=wv, inc="tv")
            pb4_free[hp] = rbe
            on = S.emit("vector", lambda e, h=h, jg=jg, obank=obank:
                        e.tensor_mul(
                out=ar2b[:, (h * 8 + jg * 4) * 128:
                         (h * 8 + jg * 4 + 4) * 128],
                in0=obank[:, 0:512], in1=rdenb), waits=[rbe, omm],
                inc="tv")
            o_free[hp] = on
            rdenb_free = on
    if cut_here("I"):
        return
    flash_dve_done = S.barrier_token("tv")
    flash_te_done = S.barrier_token("tt")

    woslab = [ar1b[:, 0:512], ar1b[:, 512:1024]]
    wo_slab_free = [None, None]
    evac_last = None
    first = True
    for dq in range(4):
        mm = None
        for dc in range(16):
            sl = dc % 2
            w = [P["tok_wog"]]
            if first:
                w += [flash_dve_done, flash_te_done]
                first = False
            if wo_slab_free[sl]:
                w.append(wo_slab_free[sl])
            lt = S.dma("sync", woslab[sl],
                       P["h_woc"][dc * 128:(dc + 1) * 128,
                                  dq * 512:(dq + 1) * 512],
                       waits=w, sem="d_wo")
            for t in range(NT):
                w2 = []
                if t == 0:
                    w2 = [lt]
                    if dc == 0 and evac_last:
                        w2.append(evac_last)
                mm = S.emit("tensor", lambda e, dc=dc, t=t, sl=sl: e.matmul(
                    banks[t][:, :],
                    ar2b[:, (dc * 8 + t) * 128:(dc * 8 + t + 1) * 128],
                    woslab[sl], start=(dc == 0), stop=(dc == 15)),
                    waits=w2, inc="tt")
            wo_slab_free[sl] = mm
        eprev = None
        for t in range(NT):
            xsl = x_sb[:, t * D + dq * 512:t * D + (dq + 1) * 512]
            eprev = S.emit("vector", lambda e, t=t, xsl=xsl: e.tensor_tensor(
                out=xsl, in0=banks[t][:, :], in1=xsl, op=ALU.add),
                waits=[mm] + ([eprev] if eprev else []), inc="tv")
        evac_last = eprev
    tok_wo_mm = S.barrier_token("tt")
    P["tok_a1_added"] = S.barrier_token("tv")
    if DBG_STAGE == 4:
        return

    wsl_bufs = [ar56b[:, 0:2048], ar56b[:, 2048:4096]]
    dbou = [ar4b[:, 0:1024], ar4b[:, 1024:2048]]
    k_slab_free = [None, None]
    k_dbou_free = [None, None]
    gelu_prev = None
    kmm_last = {}
    for ft in range(64):
        sl = ft % 2
        w = [P["tok_W0g"], flash_te_done, flash_dve_done]
        if k_slab_free[sl]:
            w.append(k_slab_free[sl])
        ksrc = P["h_W0g"][:, ft * 128:(ft + 1) * 128].rearrange(
            "(dc p) f -> dc p f", p=128)
        kdst = wsl_bufs[sl].rearrange("p (dc f) -> p dc f", dc=16)
        lt_a = S.dma("sync", kdst[:, 0:8], ksrc[0:8].transpose([1, 0, 2]),
                     waits=w, sem="d_w0")
        lt = S.dma("scalar", kdst[:, 8:16], ksrc[8:16].transpose([1, 0, 2]),
                   waits=w, sem="d_w0s")
        mm = None
        for half in range(2):
            bank = pb[sl * 2 + half]
            for dc in range(16):
                w2 = []
                if dc == 0:
                    w2 = [lt_a, lt]
                    if kmm_last.get((sl, half)):
                        w2.append(kmm_last[(sl, half)])
                    else:
                        w2.append(P["tok_a1_added"])
                mm = S.emit("tensor", lambda e, sl=sl, half=half, dc=dc,
                            bank=bank: e.matmul(
                    bank[:, :], wsl_bufs[sl][:, dc * 128:(dc + 1) * 128],
                    ar3b[:, dc * 1024 + half * 512:
                         dc * 1024 + (half + 1) * 512],
                    start=(dc == 0), stop=(dc == 15)), waits=w2, inc="tt")
            wg = [mm, P["tok_consts"]]
            if gelu_prev:
                wg.append(gelu_prev)
            if k_dbou_free[sl] and half == 0:
                wg.append(k_dbou_free[sl])
            gelu_prev = S.emit("scalar", lambda e, sl=sl, half=half, ft=ft,
                               bank=bank: e.activation(
                out=dbou[sl][:, half * 512:(half + 1) * 512], in_=bank[:, :],
                func=AF.Gelu, bias=P["b0c_sb"][:, ft:ft + 1], scale=1.0),
                waits=wg, inc="tsc")
            kmm_last[(sl, half)] = gelu_prev
        k_dbou_free[sl] = S.dma("sync", P["h_DqT"][ft * 128:(ft + 1) * 128, :],
                                dbou[sl], waits=[gelu_prev], sem="d_d1")
    dqt_done = S.barrier_token("d_d1")
    gelu_done = S.barrier_token("tsc")

    w1b = [ar2b[:, 0:512], ar2b[:, 512:1024]]
    dqb = [ar2b[:, 1024 + i * 1024:2048 + i * 1024] for i in range(2)]
    b1bc = ar4[:, 0:2048]
    tb1 = S.dma("gpsimd", b1bc,
                P["vrows"][5:6, 0:2048].broadcast_to([128, 2048]),
                waits=[dqt_done, gelu_done], sem="d_fg")
    slot_mm = [None, None]
    prev_dq_ev = P["tok_a1_added"]
    for dq in range(4):
        mm = None
        for fc in range(64):
            sl = fc % 2
            w = [dqt_done, P["tok_W1g"], tok_wo_mm]
            if slot_mm[sl] is not None:
                w.append(slot_mm[sl])
            qa, sa = ("sync", "d_f2b") if fc % 2 == 0 else \
                ("scalar", "d_f2bs")
            qb, sb2 = ("scalar", "d_f2bs") if fc % 2 == 0 else \
                ("sync", "d_f2b")
            wt = S.dma(qa, w1b[sl],
                       P["h_W1g"][fc * 128:(fc + 1) * 128,
                                  dq * 512:(dq + 1) * 512],
                       waits=w, sem=sa)
            ht = S.dma(qb, dqb[sl],
                       P["h_DqT"][fc * 128:(fc + 1) * 128, :],
                       waits=w, sem=sb2)
            for t in range(NT):
                w2 = []
                if t == 0:
                    w2 = [wt, ht]
                    if fc == 0:
                        w2 += [prev_dq_ev, gelu_done]
                mm = S.emit("tensor", lambda e, sl=sl, t=t, fc=fc: e.matmul(
                    banks[t][:, :], dqb[sl][:, t * 128:(t + 1) * 128],
                    w1b[sl], start=(fc == 0), stop=(fc == 63)),
                    waits=w2, inc="tt")
            slot_mm[sl] = mm
        eprev = None
        for t in range(NT):
            xsl = x_sb[:, t * D + dq * 512:t * D + (dq + 1) * 512]
            e1 = S.emit("vector", lambda e, t=t, dq=dq: e.tensor_tensor(
                out=ar4[:, 2048:2560], in0=banks[t][:, :],
                in1=b1bc[:, dq * 512:(dq + 1) * 512], op=ALU.add),
                waits=[mm, tb1, P["tok_a1_added"]] +
                ([eprev] if eprev else []), inc="tv")
            eprev = S.emit("vector", lambda e, xsl=xsl: e.tensor_add(
                out=xsl, in0=ar4[:, 2048:2560], in1=xsl), waits=[e1],
                inc="tv")
        prev_dq_ev = eprev
    P["tok_final"] = S.barrier_token("tv")


def emit_debug_out(S, P):
    out_t, x_sb = P["out_t"], P["x_sb"]
    if DBG_STAGE == 0:
        xn_sb = P["xn_sb"]
        for t in range(NT):
            S.dma("gpsimd", out_t[t * 128:(t + 1) * 128, :],
                  xn_sb[:, t * D:(t + 1) * D], waits=[P["tok_xn"]], sem="d_out")
        S.emit("gpsimd", lambda e: None, waits=[S.barrier_token("d_out")],
               inc="done")
        return
    if DBG_STAGE == 1:
        emit_attn0_gather(S, P, to_out=True)
        S.emit("gpsimd", lambda e: None, waits=[S.barrier_token("d_out")],
               inc="done")
        return
    if DBG_STAGE in (2, 3, 4, 99):
        tok = {2: P.get("tok_x1p"), 3: P.get("tok_x1"),
               4: P.get("tok_a1_added"), 99: P.get("tok_final")}[DBG_STAGE]
        for t in range(NT):
            S.dma("sync", P["out_t"][t * 128:(t + 1) * 128, :],
                  P["x_sb"][:, t * D:(t + 1) * D], waits=[tok], sem="d_out")
        S.emit("sync", lambda e: None, waits=[S.barrier_token("d_out")],
               inc="done")
        return
    raise NotImplementedError(f"stage {DBG_STAGE}")


def emit_attn0_gather(S, P, to_out=False, extra_waits=(), consume_fn=None):
    h_tok2g = P["h_tok2g"]
    out_t = P["out_t"]
    recv = P["ar4b"]
    rows_view = h_tok2g.rearrange("a (b c) -> (a b) c", b=64, c=2048)
    gi = P["gidx_sb"]
    res = []
    slot_done = [None] * 4
    for t in range(NT):
        dst = recv[:, (t % 4) * 2048:(t % 4) * 2048 + 2048]
        w = [P["tok_ag"], P["tok_gidx"]] + list(extra_waits)
        if slot_done[t % 4]:
            w.append(slot_done[t % 4])
        w.append(S.barrier_token("d_gb"))
        tk = S.emit("gpsimd", lambda e, t=t, dst=dst: e.indirect_dma_start(
            out=dst, out_offset=None, in_=rows_view,
            in_offset=IndirectOffsetOnAxis(gi[:, t:t + 1], 0)),
            waits=w, inc="d_gb", inc_amt=16)
        if to_out:
            slot_done[t % 4] = S.dma("gpsimd",
                                     out_t[t * 128:(t + 1) * 128, :], dst,
                                     waits=[tk], sem="d_out")
        elif consume_fn is not None:
            slot_done[t % 4] = consume_fn(t, dst, tk)
        res.append((dst, tk))
    return res



def host_prep(inputs):
    x = np.ascontiguousarray(inputs["L0_attn_x"].reshape(B * S, D))
    w1T = np.ascontiguousarray(inputs["L0_ffn_w1"].T)
    w2T = np.ascontiguousarray(inputs["L0_ffn_w2"].T)
    W0T = np.ascontiguousarray(inputs["L1_ffn_W0"].T)
    W1T = np.ascontiguousarray(inputs["L1_ffn_W1"].T)
    onesrow = np.ones((1, 128), np.float32)
    wqT = np.ascontiguousarray(inputs["L1_attn_wq"].T)
    wkvT = np.ascontiguousarray(
        np.concatenate([inputs["L1_attn_wk"].T, inputs["L1_attn_wv"].T], axis=1))
    woT = np.ascontiguousarray(inputs["L1_attn_wo"].T)

    qkv_w = inputs["L0_attn_qkv_w"]
    qkvbd3 = np.zeros((3, 128, 128), np.float32)
    for g in range(3):
        for d8 in range(8):
            qkvbd3[g, d8 * 16:(d8 + 1) * 16, d8 * 16:(d8 + 1) * 16] = \
                qkv_w[:, g * 16:(g + 1) * 16]
    qkvbd = np.ascontiguousarray(qkvbd3.transpose(1, 0, 2).reshape(128, 3 * 128))

    proj_w = inputs["L0_attn_proj_w"]
    projbd4 = np.zeros((4, 128, 128), np.float32)
    for br in range(2):
        for oct_ in range(2):
            m = projbd4[br * 2 + oct_]
            for c in range(8):
                for s8 in range(8):
                    s = 8 * oct_ + s8
                    for o in range(16):
                        m[c * 16 + s, o * 8 + s8] = proj_w[c + 8 * br, o]

    projbd = np.ascontiguousarray(projbd4.transpose(1, 0, 2).reshape(128, 4 * 128))
    ident = np.eye(128, dtype=np.float32)

    pconst = np.zeros((128, 32), np.float32)
    cw0 = inputs["L0_attn_conv_w0"]
    cw1 = inputs["L0_attn_conv_w1"]
    cb0 = inputs["L0_attn_conv_b0"]
    cb1 = inputs["L0_attn_conv_b1"]
    for c in range(8):
        for s in range(16):
            p = c * 16 + s
            pconst[p, 0:3] = cw0[c, 0, 1, 0:3]
            pconst[p, 3:6] = cw1[c, 0, 0:3, 1]
            pconst[p, 6] = cb0[c]
            pconst[p, 7] = cb1[c]
    ln_w = inputs["L0_attn_ln_w"]
    ln_b = inputs["L0_attn_ln_b"]
    proj_b = inputs["L0_attn_proj_b"]
    for o in range(16):
        for so in range(8):
            p = o * 8 + so
            pconst[p, 8] = ln_w[o]
            pconst[p, 9] = ln_b[o]
            pconst[p, 10] = proj_b[o]
    pconst[:, 11] = 1.0
    pconst[:, 16] = EPS

    vrows = np.zeros((10, FF), np.float32)
    vrows[0] = inputs["L0_ffn_b1"]
    vrows[1] = inputs["L0_ffn_gamma"]
    vrows[2] = inputs["L0_ffn_beta"]
    vrows[3, :D] = inputs["L0_ffn_b2"]
    vrows[4] = inputs["L1_ffn_bias0"]
    vrows[5, :D] = inputs["L1_ffn_bias1"]
    vrows[6, :D] = inputs["L1_attn_bq"]
    vrows[7, :256] = inputs["L1_attn_bk"]
    vrows[7, 256:512] = inputs["L1_attn_bv"]

    cmaskv = np.where(np.tril(np.ones((128, 128), bool)), 0.0, -30000.0
                      ).astype(np.float32)

    gselv = np.zeros((128, 16), np.float32)
    for c in range(8):
        for s_ in range(16):
            gselv[c * 16 + s_, s_] = 1.0
    ebcv = np.zeros((16, 128), np.float32)
    for c in range(8):
        for s_ in range(16):
            ebcv[s_, c * 16 + s_] = 1.0
    i16v = np.eye(16, dtype=np.float32)
    glnv = np.zeros((128, 8), np.float32)
    for o in range(16):
        for so in range(8):
            glnv[o * 8 + so, so] = 1.0 / 16.0
    ebc8v = np.zeros((8, 128), np.float32)
    for o in range(16):
        for so in range(8):
            ebc8v[so, o * 8 + so] = 1.0

    in_maps = []
    for p in range(8):
        rows = core_rows(p)
        postab = np.zeros((128, NT), np.float32)
        gidx = np.zeros((128, NT), np.int32)
        for t in range(NT):
            postab[:, t] = (rows[t * 128:(t + 1) * 128] % S).astype(np.float32)
            g = core_tiles(p)[t]
            for pp in range(128):
                h_b = g * 128 + pp
                c = h_b // 256
                tval = h_b % 256
                g_s = tval // 8
                snd = g_s % 4
                j_s = g_s // 4
                gidx[pp, t] = (snd * 16 + c) * 64 + j_s * 8 + (tval % 8)
        inv = (10000.0 ** (-np.arange(64, dtype=np.float64) / 64.0))
        cstab = np.zeros((128, NT * 128), np.float32)
        for t in range(NT):
            pos = (rows[t * 128:(t + 1) * 128] % S).astype(np.float64)
            ang = pos[:, None] * inv[None, :]
            cstab[:, t * 128:t * 128 + 64] = np.cos(ang)
            cstab[:, t * 128 + 64:t * 128 + 128] = np.sin(ang)
        dmaskv = np.zeros((128, 512), np.float32)
        m = p % 4
        rr, cc2 = np.arange(128)[:, None], np.arange(128)[None, :]
        for kt in range(4):
            if kt < m:
                dmaskv[:, kt * 128:(kt + 1) * 128] = 1.0
            elif kt == m:
                dmaskv[:, kt * 128:(kt + 1) * 128] = (rr <= cc2)
        b0colsv = np.ascontiguousarray(
            inputs["L1_ffn_bias0"].reshape(64, 128).T.astype(np.float32))
        im = {
            "x_in": np.ascontiguousarray(x[rows]),
            "w1s": np.ascontiguousarray(w1T[256 * p:256 * (p + 1)]),
            "w2s": np.ascontiguousarray(w2T[1024 * p:1024 * (p + 1)]),
            "W0s": np.ascontiguousarray(W0T[256 * p:256 * (p + 1)]),
            "W1s": np.ascontiguousarray(W1T[1024 * p:1024 * (p + 1)]),
            "onesrow": onesrow,
            "wqs": np.ascontiguousarray(wqT[256 * p:256 * (p + 1)]),
            "wkvs": np.ascontiguousarray(wkvT[256 * p:256 * (p + 1)]),
            "wos": np.ascontiguousarray(woT[256 * p:256 * (p + 1)]),
            "cstab_in": cstab.astype(BF),
            "dmask_in": dmaskv.astype(BF),
            "b0cols": b0colsv,
            "qkvbd": qkvbd, "projbd": projbd, "ident_in": ident,
            "pconst": pconst, "vrows": vrows, "cmask": cmaskv,
            "postab": postab, "gidx": gidx,
            "gsel_in": gselv, "ebc_in": ebcv, "i16_in": i16v,
            "glnsel_in": glnv, "ebc8_in": ebc8v,
        }
        in_maps.append(im)
    return in_maps


def kernel(**inputs):
    in_maps = host_prep(inputs)
    nc = build_program()
    res = run_bass_kernel_spmd(nc, in_maps, core_ids=list(range(8)))
    out = np.zeros((B * S, D), np.float32)
    for p in range(8):
        out[core_rows(p)] = np.asarray(res.results[p]["out"])
    return out.reshape(B, S, D)


# version 09
# speedup vs baseline: 1.1492x; 1.1492x over previous
import numpy as np
import ml_dtypes
from contextlib import ExitStack

import concourse.bass as bass
import concourse.mybir as mybir
from concourse.bass_utils import run_bass_kernel_spmd
from concourse.bass_core_rust import IndirectOffsetOnAxis

F32 = mybir.dt.float32
BF16 = mybir.dt.bfloat16
I32 = mybir.dt.int32
AF = mybir.ActivationFunctionType
ALU = mybir.AluOpType
AX = mybir.AxisListType
BF = ml_dtypes.bfloat16

B = 2
S = 4096
D = 2048
FF = 8192
RPC = 1024
NT = 8
TPC = RPC * 128
EPS = 1e-5
G4 = [[0, 1, 2, 3], [4, 5, 6, 7]]
G8 = [[0, 1, 2, 3, 4, 5, 6, 7]]

import os
DBG_STAGE = int(os.environ.get("KBD_STAGE", "99"))

ENGS = ("sync", "gpsimd", "vector", "scalar", "tensor")


def core_tiles(p):
    return [4 * j + (p % 4) for j in range(NT)]


def core_rows(p):
    b = p // 4
    out = []
    for g in core_tiles(p):
        st = b * S + g * 128
        out.extend(range(st, st + 128))
    return np.array(out)


class Sched:
    def __init__(self):
        self.ops = {e: [] for e in ENGS}
        self.semnames = []
        self.counts = {}

    def sem(self, name):
        if name not in self.counts:
            self.semnames.append(name)
            self.counts[name] = 0
        return name

    def emit(self, eng, fn, waits=(), inc=None, inc_amt=1):
        tok = None
        if inc is not None:
            self.sem(inc)
            self.counts[inc] += inc_amt
            tok = (inc, self.counts[inc])
        self.ops[eng].append((fn, [tuple(w) for w in waits], inc, inc_amt))
        return tok

    def dma(self, queue, out, in_, waits=(), sem=None, **kw):
        if sem is None:
            sem = "d_" + queue
        self.sem(sem)
        gate = [(sem, self.counts[sem])] if self.counts[sem] else []
        return self.emit(queue, lambda e: e.dma_start(out=out, in_=in_, **kw),
                         waits=list(waits) + gate, inc=sem, inc_amt=16)

    def barrier_token(self, sem):
        return (sem, self.counts.get(sem, 0))


def replay(block, sch, sems):
    def mk(eng_name):
        def body(eng):
            for fn, waits, inc, amt in sch.ops[eng_name]:
                for s, v in waits:
                    if v > 0:
                        eng.wait_ge(sems[s], v)
                ins = fn(eng)
                if inc is not None:
                    if ins is None:
                        eng.nop().then_inc(sems[inc], amt)
                    else:
                        ins.then_inc(sems[inc], amt)
        return body
    block.sync(mk("sync"))
    block.gpsimd(mk("gpsimd"))
    block.vector(mk("vector"))
    block.scalar(mk("scalar"))
    block.tensor(mk("tensor"))



def build_program():
    nc = bass.Bass("TRN2", target_bir_lowering=False, debug=False)
    S_ = Sched()
    P = {}

    def din(name, shape, dt=F32):
        return nc.dram_tensor(name, shape, dt, kind="ExternalInput").ap()

    def dint(name, shape, dt=F32, **kw):
        return nc.dram_tensor(name, shape, dt, kind="Internal", **kw).ap()

    x_in = din("x_in", [RPC, D])
    out_t = nc.dram_tensor("out", [RPC, D], F32, kind="ExternalOutput").ap()
    w1s = din("w1s", [256, FF])
    w2s = din("w2s", [1024, D])
    W0s = din("W0s", [256, FF])
    W1s = din("W1s", [1024, D])
    onesrow = din("onesrow", [1, 128])
    wqs = din("wqs", [256, D])
    wkvs = din("wkvs", [256, 512])
    wos = din("wos", [256, D])
    cstab_in = din("cstab_in", [128, NT * 128], BF16)
    dmask_in = din("dmask_in", [128, 512], BF16)
    b0cols = din("b0cols", [128, 64])
    qkvbd = din("qkvbd", [128, 3 * 128])
    projbd = din("projbd", [128, 4 * 128])
    ident_in = din("ident_in", [128, 128])
    pconst = din("pconst", [128, 32])
    vrows = din("vrows", [10, FF])
    cmask = din("cmask", [128, 128])
    postab = din("postab", [128, NT])
    gsel_in = din("gsel_in", [128, 16])
    ebc_in = din("ebc_in", [16, 128])
    i16_in = din("i16_in", [16, 16])
    glnsel_in = din("glnsel_in", [128, 8])
    ebc8_in = din("ebc8_in", [8, 128])
    gidx = din("gidx", [128, NT], I32)

    w1slab = dint("w1slab", [256, FF])
    h_w1q = dint("h_w1q", [256, FF], BF16)
    h_w1g = dint("h_w1g", [D, FF], BF16)
    h_w2q = dint("h_w2q", [1024, D], BF16)
    h_w2g = dint("h_w2g", [FF, D], BF16)
    h_W1c = dint("h_W1c", [1024, D], BF16)
    h_W1g = dint("h_W1g", [FF, D], BF16)
    h_stat_i = dint("h_stat_i", [1, 2])
    h_stat_o = dint("h_stat_o", [1, 2])
    h_qkvt = dint("h_qkvt", [NT, 16, 128, 384], BF16)
    h_Tt = dint("h_Tt", [NT, 16, 128, 128], BF16)
    h_tok2 = dint("h_tok2", [16, TPC], BF16)
    h_tok2g = dint("h_tok2g", [4 * 16, TPC], BF16)
    h_kv = dint("h_kv", [RPC, 512], BF16)
    h_kvg = dint("h_kvg", [4 * RPC, 512], BF16)
    h_h = dint("h_h", [RPC, FF], BF16)
    h_hqT = dint("h_hqT", [FF, RPC], BF16)
    h_x1 = dint("h_x1", [RPC, D])
    h_qf = dint("h_qf", [16 * 128, RPC], BF16)
    h_DqT = dint("h_DqT", [FF, RPC], BF16)
    h_dbg = dint("h_dbg", [RPC, D])
    h_wqcs = dint("h_wqcs", [256, D], BF16)
    h_wqc = dint("h_wqc", [D, D], BF16)
    h_wkvcs = dint("h_wkvcs", [256, 512], BF16)
    h_wkvc = dint("h_wkvc", [D, 512], BF16)
    h_wocs = dint("h_wocs", [256, D], BF16)
    h_woc = dint("h_woc", [D, D], BF16)
    h_W0cs = dint("h_W0cs", [256, FF], BF16)
    h_W0g = dint("h_W0g", [D, FF], BF16)

    ctx = ExitStack()
    with ctx:
        def sb(name, shape, dt=F32):
            return ctx.enter_context(nc.sbuf_tensor(name, shape, dt)).ap()

        def psum(name, shape, dt=F32):
            return ctx.enter_context(nc.psum_tensor(name, shape, dt)).ap()

        x_sb = sb("x_sb", [128, NT * D])
        ar1 = sb("ar1", [128, FF])
        ar2 = sb("ar2", [128, FF])
        ar3 = sb("ar3", [128, FF])
        ar4 = sb("ar4", [128, 4096])
        ar56 = sb("ar56", [128, 4096])
        smalls = sb("smalls", [128, 1840])
        prod_f = sb("prod_f", [128, 256])
        pv_f = sb("pv_f", [128, 256])
        pconst_sb = sb("pconst_sb", [128, 32])
        identb = sb("identb", [128, 128], BF16)
        qkvbd_sb = sb("qkvbd_sb", [128, 3 * 128], BF16)
        projbd_sb = sb("projbd_sb", [128, 4 * 128], BF16)
        cmask_sb = sb("cmask_sb", [128, 128], BF16)
        gsel_sb = sb("gsel_sb", [128, 16], BF16)
        ebc_sb = sb("ebc_sb", [16, 128], BF16)
        i16_sb = sb("i16_sb", [16, 16], BF16)
        glnsel_sb = sb("glnsel_sb", [128, 8], BF16)
        ebc8_sb = sb("ebc8_sb", [8, 128], BF16)
        gidx_sb = sb("gidx_sb", [128, NT], I32)
        onesrow_sb = sb("onesrow_sb", [1, 128])
        onesrow_b = sb("onesrow_b", [1, 128], BF16)
        onescol_b = sb("onescol_b", [128, 1], BF16)
        b0c_sb = sb("b0c_sb", [128, 64])

        pb = [psum(f"pb{i}", [128, 512], F32) for i in range(6)]
        pbt = psum("pbt", [128, 1024], BF16)
        pbt2 = psum("pbt2", [128, 1024], BF16)

        xn_sb = ar1.bitcast(BF16)
        pk = ar2.bitcast(BF16)
        pk2 = ar3.bitcast(BF16)
        ar4b = ar4.bitcast(BF16)
        ar56b = ar56.bitcast(BF16)
        ar56i = ar56.bitcast(I32)
        prod_sb = prod_f.bitcast(BF16)
        pv_sb = pv_f.bitcast(BF16)
        smb = smalls.bitcast(BF16)
        smb_base = 2144

        P.update(locals())

        emit_phase0(S_, P)
        if DBG_STAGE >= 2:
            emit_wprep_stats(S_, P)
            P["attn_midhook"] = lambda: emit_wprep_quant(S_, P)
        if DBG_STAGE >= 1:
            emit_l0attn(S_, P)
        if DBG_STAGE >= 2:
            emit_wprep_ags(S_, P)
        if DBG_STAGE >= 2:
            emit_l0ffn(S_, P)
        if DBG_STAGE >= 3:
            emit_x1(S_, P)
        if DBG_STAGE >= 4:
            emit_l1(S_, P)
        emit_debug_out(S_, P)

        sems = {}
        for name in S_.semnames:
            sems[name] = ctx.enter_context(nc.semaphore(name))
        block = ctx.enter_context(nc.Block())
        with nc.allow_low_precision(reason="bf16 pipeline; tolerance spec "
                                    "permits bf16 matmuls + bf16 stores"):
            replay(block, S_, sems)
    return nc



def emit_phase0(S, P):
    x_in, x_sb = P["x_in"], P["x_sb"]
    ar1, smalls = P["ar1"], P["smalls"]
    xn_sb = P["xn_sb"]

    S.dma("sync", P["pconst_sb"], P["pconst"], sem="d_cn")
    S.dma("gpsimd", P["identb"], P["ident_in"])
    S.dma("gpsimd", P["qkvbd_sb"], P["qkvbd"])
    S.dma("gpsimd", P["projbd_sb"], P["projbd"])
    S.dma("gpsimd", P["cmask_sb"], P["cmask"])
    S.dma("gpsimd", P["gsel_sb"], P["gsel_in"])
    S.dma("gpsimd", P["ebc_sb"], P["ebc_in"])
    S.dma("gpsimd", P["i16_sb"], P["i16_in"])
    S.dma("gpsimd", P["glnsel_sb"], P["glnsel_in"])
    S.dma("gpsimd", P["ebc8_sb"], P["ebc8_in"])
    P["tok_gidx"] = S.dma("gpsimd", P["gidx_sb"], P["gidx"])
    S.dma("gpsimd", P["onesrow_b"], P["onesrow"])
    S.dma("gpsimd", P["onescol_b"], P["onesrow"].transpose([1, 0]))
    S.dma("sync", P["onesrow_sb"], P["onesrow"], sem="d_cn")
    S.dma("sync", P["b0c_sb"], P["b0cols"], sem="d_cn")
    P["tok_consts"] = S.barrier_token("d_cn")
    P["tok_constsg"] = S.barrier_token("d_gpsimd")

    for t in range(NT):
        S.dma("sync", x_sb[:, t * D:(t + 1) * D], x_in[t * 128:(t + 1) * 128, :])
    tok_x = S.barrier_token("d_sync")
    P["tok_x"] = tok_x

    mu = P["smalls"][:, 0:1]
    esq = P["smalls"][:, 1:2]
    var = P["smalls"][:, 2:3]
    std = P["smalls"][:, 3:4]
    rstd = P["smalls"][:, 4:5]
    epsc = P["pconst_sb"][:, 16:17]
    ftmp = P["ar3"]
    sc5 = P["smalls"][:, 5:6]
    t8 = None
    t7 = None
    for t in range(NT):
        xt = x_sb[:, t * D:(t + 1) * D]
        head = [tok_x] + ([t8] if t8 else [])
        t1 = S.emit("vector", lambda e, xt=xt: e.tensor_reduce(
            out=mu, in_=xt, axis=AX.X, op=ALU.add), waits=head, inc="tv")
        t2 = S.emit("scalar", lambda e, xt=xt: e.activation(
            out=ftmp[:, 0:D], in_=xt, func=AF.Square, scale=float(D) ** -0.5,
            accum_out=esq), waits=head, inc="tsc")
        t3 = S.emit("vector", lambda e: e.tensor_scalar(
            out=mu, in0=mu, scalar1=1.0 / D, scalar2=None, op0=ALU.mult),
            waits=[t1, t2], inc="tv")
        t5 = S.emit("vector", lambda e: e.tensor_mul(
            out=sc5, in0=mu, in1=mu), waits=[t3], inc="tv")
        t5b = S.emit("vector", lambda e: e.tensor_sub(out=var, in0=esq, in1=sc5),
                     waits=[t5], inc="tv")
        t6 = S.emit("scalar", lambda e: e.activation(
            out=std, in_=var, func=AF.Sqrt, bias=epsc, scale=1.0),
            waits=[t5b, P["tok_consts"]] + ([t7] if t7 else []), inc="tsc")
        t7 = S.emit("vector", lambda e: e.reciprocal(out=rstd, in_=std),
                    waits=[t6], inc="tv")
        t7b = S.emit("vector", lambda e, t=t: e.tensor_copy(
            out=P["smalls"][:, 24 + t:25 + t], in_=mu), waits=[t7], inc="tv")
        t7c = S.emit("vector", lambda e, t=t: e.tensor_copy(
            out=P["smalls"][:, 32 + t:33 + t], in_=rstd), waits=[t7],
            inc="tv")
        t8 = S.emit("vector", lambda e, t=t, xt=xt: e.tensor_scalar(
            out=xn_sb[:, t * D:(t + 1) * D], in0=xt, scalar1=mu, scalar2=rstd,
            op0=ALU.subtract, op1=ALU.mult), waits=[t7c], inc="tv")
    P["tok_xn"] = S.barrier_token("tv")
    P["mucols"] = P["smalls"][:, 24:32]
    P["rstdcols"] = P["smalls"][:, 32:40]


def emit_l0attn(S, P):
    xn_sb = P["xn_sb"]
    identb, qkvbd_sb, projbd_sb = P["identb"], P["qkvbd_sb"], P["projbd_sb"]
    pbt, pbt2, pb = P["pbt"], P["pbt2"], P["pb"]
    ar4b = P["ar4b"]
    ar56b = P["ar56b"]
    pk, pk2 = P["pk"], P["pk2"]
    smalls = P["smalls"]
    h_tok2 = P["h_tok2"]
    pc = P["pconst_sb"]
    prod, pv = P["prod_sb"], P["pv_sb"]
    gsel, i16b, ebc = P["gsel_sb"], P["i16_sb"], P["ebc_sb"]
    glnsel, ebc8 = P["glnsel_sb"], P["ebc8_sb"]

    Q1 = pk[:, 0:4096]; K1 = pk[:, 4096:8192]
    V1 = pk[:, 8192:12288]; Q2 = pk[:, 12288:16384]
    K2 = pk2[:, 0:4096]; V2 = pk2[:, 4096:8192]
    tokb = [pk2[:, 8192:12288], pk2[:, 12288:16384]]
    QKV = [(Q1, K1, V1), (Q2, K2, V2)]
    On = [ar4b[:, 0:4096], ar56b[:, 0:4096]]
    lep = [ar4b[:, 4096:8192], ar56b[:, 4096:8192]]
    tok2b = [ar4b[:, 0:4096], ar4b[:, 4096:8192]]
    qt = ar56b

    smb = P["smb"]
    pbase = P["smb_base"]
    Pslot = [smb[0:16, pbase:pbase + 512], smb[0:16, pbase + 512:pbase + 1024]]
    rden = smb[0:16, pbase + 1024:pbase + 1536]
    attsb = smb[:, pbase:pbase + 512]
    sqsb = smb[:, pbase + 512:pbase + 1024]
    musb = smb[0:8, 96:608]
    varsb = smb[0:8, 608:1120]
    stdsb = smb[0:8, 1120:1632]
    rstdsb = smb[0:8, 1632:2144]

    def vsl(buf, brr):
        if brr == 0:
            v = buf.rearrange("p (kk d r) -> p kk d r", kk=4, d=8, r=128)
            return lambda t: v[:, :, t, :]
        v = buf.rearrange("p (kk d i j rr) -> p kk d i j rr",
                          kk=4, d=8, i=4, j=8, rr=4)
        return lambda t: v[:, :, :, :, t, :]

    def flatlike(flat, brr):
        if brr == 0:
            return flat.rearrange("p (a b) -> p a b", a=4, b=128)
        return flat.rearrange("p (a b c e) -> p a b c e", a=4, b=8, c=4, e=4)

    tok_need = [P["tok_xn"], P["tok_constsg"], P["tok_consts"]]

    for chunk in range(2):
        chb = [S.barrier_token(x) for x in ("tv", "tt", "tsc", "d_t2")]
        last_rp = None
        for rtl in range(4):
            rt = chunk * 4 + rtl
            prev_evac = P.get("_t_evac_prev")
            ttoks = []
            for k in range(16):
                bank = pbt if k < 8 else pbt2
                kk8 = k % 8
                w = []
                if k == 0:
                    w = list(tok_need) + chb + \
                        ([prev_evac] if prev_evac else []) + \
                        ([last_rp] if last_rp else [])
                ttoks.append(S.emit("tensor", lambda e, rt=rt, k=k, bank=bank,
                                    kk8=kk8: e.transpose(
                    bank[:, kk8 * 128:(kk8 + 1) * 128],
                    xn_sb[:, rt * D + k * 128: rt * D + (k + 1) * 128],
                    identb), waits=w, inc="tt"))
            ev1 = S.emit("vector", lambda e: e.tensor_copy(
                out=qt[:, 6144:7168], in_=pbt[:, :]),
                waits=[ttoks[7]] + chb + ([last_rp] if last_rp else []),
                inc="tv")
            ev2 = S.emit("vector", lambda e: e.tensor_copy(
                out=qt[:, 7168:8192], in_=pbt2[:, :]), waits=[ttoks[15]],
                inc="tv")
            P["_t_evac_prev"] = ev2
            qk_evacs = []
            for k in range(16):
                bank = pb[k % 2]
                mm = None
                for g in range(3):
                    w = []
                    if g == 0:
                        w = [ev1 if k < 8 else ev2]
                        if len(qk_evacs) >= 2:
                            w.append(qk_evacs[-2])
                    mm = S.emit("tensor", lambda e, k=k, g=g, bank=bank:
                                e.matmul(
                        bank[:, g * 128:(g + 1) * 128],
                        qkvbd_sb[:, g * 128:(g + 1) * 128],
                        qt[:, 6144 + k * 128:6144 + (k + 1) * 128],
                        start=True, stop=True), waits=w, inc="tt")
                qk_evacs.append(S.emit("vector", lambda e, k=k, bank=bank:
                                       e.tensor_copy(
                    out=qt[:, k * 384:(k + 1) * 384], in_=bank[:, 0:384]),
                    waits=[mm], inc="tv"))
            wdma = [qk_evacs[-1]]
            last_rp = S.dma("sync", P["h_qkvt"][rt].transpose([1, 0, 2]),
                qt[:, 0:6144].rearrange("p (k gr) -> p k gr", k=16),
                waits=wdma, sem="d_rp")
            last_rp = S.dma("sync", P["h_Tt"][rt].transpose([1, 0, 2]),
                qt[:, 6144:8192].rearrange("p (k r) -> p k r", k=16),
                waits=wdma, sem="d_rp")
        spill_tok = S.barrier_token("d_rp")
        if chunk == 0:
            P["tok_attn_tt_c0"] = S.barrier_token("tt")
        if chunk == 1 and "attn_midhook" in P:
            P["attn_midhook"]()
            del P["attn_midhook"]

        qv = P["h_qkvt"].rearrange(
            "rt (kq kk) (d c) (g r) -> rt kq kk d c g r",
            kq=4, kk=4, d=8, c=16, g=3, r=128)
        tv = P["h_Tt"].rearrange(
            "rt (kq kk) (d c) r -> rt kq kk d c r", kq=4, kk=4, d=8, c=16)
        rts = slice(chunk * 4, chunk * 4 + 4)
        rp2 = []
        for br in range(2):
            Qb, Kb, Vb = QKV[br]
            cs = slice(br * 8, br * 8 + 8)
            for g, dstb in ((0, Qb), (1, Kb), (2, Vb)):
                src = qv[rts, :, :, :, cs, g, :].transpose([4, 0, 1, 2, 3, 5])
                rp2.append(S.dma("sync", dstb, src,
                                 waits=[spill_tok] +
                                 ([P["tok_wprep_v"], P["tok_wprep_bufs_free"]]
                                  if "tok_wprep_v" in P else []),
                                 sem="d_rp"))
        for oct_ in range(2):
            rto = slice(chunk * 4 + oct_ * 2, chunk * 4 + oct_ * 2 + 2)
            src = tv[rto].transpose([4, 0, 1, 2, 3, 5])
            rp2.append(S.dma("sync", tokb[oct_], src,
                             waits=[spill_tok] +
                             ([P["tok_wprep_v"], P["tok_wprep_bufs_free"]]
                              if "tok_wprep_v" in P else []), sem="d_rp"))
        tok_rp = S.barrier_token("d_rp")

        for br in range(2):
            Qb, Kb, Vb = QKV[br]
            sq_ = vsl(Qb, br); sk_ = vsl(Kb, br); sv_ = vsl(Vb, br)
            lepb, Onb = lep[br], On[br]
            so_ = vsl(Onb, br); sL_ = vsl(lepb, br)
            l1 = S.emit("vector", lambda e, Vb=Vb, lepb=lepb, br=br:
                        e.tensor_scalar(
                out=lepb, in0=Vb, scalar1=pc[:, br * 3 + 1:br * 3 + 2],
                scalar2=pc[:, 6 + br:7 + br], op0=ALU.mult, op1=ALU.add),
                waits=[tok_rp, S.barrier_token("tt")], inc="tv")
            lprev = l1
            for tap, dst_t, src_t in ((2, slice(0, 7), slice(1, 8)),
                                      (0, slice(1, 8), slice(0, 7))):
                if br == 0:
                    Vv = Vb.rearrange("p (kk d r) -> p kk d r", kk=4, d=8, r=128)
                    Lv = lepb.rearrange("p (kk d r) -> p kk d r",
                                        kk=4, d=8, r=128)
                    vsrc = Vv[:, :, src_t, :]
                    ldst = Lv[:, :, dst_t, :]
                else:
                    Vv = Vb.rearrange("p (kk d i j rr) -> p kk d i j rr",
                                      kk=4, d=8, i=4, j=8, rr=4)
                    Lv = lepb.rearrange("p (kk d i j rr) -> p kk d i j rr",
                                        kk=4, d=8, i=4, j=8, rr=4)
                    vsrc = Vv[:, :, :, :, src_t]
                    ldst = Lv[:, :, :, :, dst_t]
                lprev = S.emit("vector", lambda e, vsrc=vsrc, ldst=ldst,
                               tap=tap, br=br: e.scalar_tensor_tensor(
                    out=ldst, in0=vsrc, scalar=pc[:, br * 3 + tap:br * 3 + tap + 1],
                    in1=ldst, op0=ALU.mult, op1=ALU.add), waits=[lprev],
                    inc="tv")

            prev_on = None
            prev_rc = None
            for i in range(8):
                den_mm = None
                o_mm = None
                prev_gmm = None
                prev_den = {}
                prev_pb = {}
                for j in range(8):
                    slot = Pslot[j % 2]
                    pw = [tok_rp]
                    if prev_gmm:
                        pw.append(prev_gmm)
                    if prev_on:
                        pw.append(prev_on)
                    pr = S.emit("vector", lambda e, a=sq_(i), b=sk_(j),
                                o=flatlike(prod, br): e.tensor_mul(
                        out=o, in0=a, in1=b), waits=pw, inc="tv")
                    g_mm = S.emit("tensor", lambda e: e.matmul(
                        pb[2][0:16, :], gsel, prod, start=True, stop=True),
                        waits=[pr], inc="tt")
                    prev_gmm = g_mm
                    exw = [g_mm]
                    if j >= 2:
                        exw += [prev_den[j - 2], prev_pb[j - 2]]
                    ex = S.emit("scalar", lambda e, slot=slot: e.activation(
                        out=slot, in_=pb[2][0:16, :], func=AF.Exp,
                        scale=float(8) ** -0.5), waits=exw, inc="tsc")
                    dw = [ex]
                    if j == 0 and prev_rc:
                        dw.append(prev_rc)
                    den_mm = S.emit("tensor", lambda e, j=j, slot=slot:
                                    e.matmul(
                        pb[3][0:16, :], i16b, slot,
                        start=(j == 0), stop=(j == 7)), waits=dw, inc="tt")
                    prev_den[j] = den_mm
                    pb_mm = S.emit("tensor", lambda e, slot=slot: e.matmul(
                        pb[4][:, :], ebc, slot, start=True, stop=True),
                        waits=[ex], inc="tt")
                    prev_pb[j] = pb_mm
                    mu_ = S.emit("vector", lambda e, b=sv_(j),
                                 o=flatlike(pv, br),
                                 a=flatlike(pb[4][:, :], br): e.tensor_mul(
                        out=o, in0=a, in1=b),
                        waits=[pb_mm] + ([o_mm] if o_mm else []), inc="tv")
                    o_mm = S.emit("tensor", lambda e, j=j: e.matmul(
                        pb[5][:, :], identb, pv,
                        start=(j == 0), stop=(j == 7)), waits=[mu_], inc="tt")
                rc = S.emit("vector", lambda e: e.reciprocal(
                    out=rden, in_=pb[3][0:16, :]), waits=[den_mm], inc="tv")
                prev_rc = rc
                rb_mm = S.emit("tensor", lambda e: e.matmul(
                    pb[2][:, :], ebc, rden, start=True, stop=True),
                    waits=[rc], inc="tt")
                rb_ev = S.emit("vector", lambda e: e.tensor_copy(
                    out=pv, in_=pb[2][:, :]), waits=[rb_mm, o_mm], inc="tv")
                prev_on = S.emit("vector", lambda e, oi=so_(i),
                                 a=flatlike(pb[5][:, :], br),
                                 b=flatlike(pv, br): e.tensor_mul(
                    out=oi, in0=a, in1=b), waits=[rb_ev], inc="tv")

        brc = [S.barrier_token("tv"), S.barrier_token("tt"),
               S.barrier_token("tsc")]

        apply_done = []
        for oct_ in range(2):
            for nch in range(8):
                csl = slice(nch * 512, (nch + 1) * 512)
                bank = pb[4 + (nch % 2)]
                mm = None
                first = True
                for br in range(2):
                    lhs = projbd_sb[:, (br * 2 + oct_) * 128:
                                    (br * 2 + oct_ + 1) * 128]
                    for buf in (On[br], lep[br]):
                        w = []
                        if first:
                            w = brc + ([apply_done[-2]] if len(apply_done) >= 2
                                       else [])
                        mm = S.emit("tensor", lambda e, lhs=lhs, buf=buf,
                                    csl=csl, bank=bank, first=first,
                                    last=(br == 1 and buf is lep[1]): e.matmul(
                            bank[:, :], lhs, buf[:, csl], start=first,
                            stop=last), waits=w, inc="tt")
                        first = False
                aev = S.emit("vector", lambda e, bank=bank: e.tensor_scalar(
                    out=attsb, in0=bank[:, :], scalar1=pc[:, 10:11],
                    scalar2=None, op0=ALU.add),
                    waits=[mm] + ([apply_done[-1]] if apply_done else []),
                    inc="tv")
                sq = S.emit("scalar", lambda e: e.activation(
                    out=sqsb, in_=attsb, func=AF.Square), waits=[aev],
                    inc="tsc")
                mu_mm = S.emit("tensor", lambda e: e.matmul(
                    pb[2][0:8, :], glnsel, attsb, start=True, stop=True),
                    waits=[aev], inc="tt")
                sq_mm = S.emit("tensor", lambda e: e.matmul(
                    pb[3][0:8, :], glnsel, sqsb, start=True, stop=True),
                    waits=[sq], inc="tt")
                muev = S.emit("vector", lambda e: e.tensor_copy(
                    out=musb, in_=pb[2][0:8, :]), waits=[mu_mm], inc="tv")
                msq = S.emit("vector", lambda e: e.tensor_mul(
                    out=varsb, in0=musb, in1=musb), waits=[muev], inc="tv")
                vv = S.emit("vector", lambda e: e.tensor_sub(
                    out=varsb, in0=pb[3][0:8, :], in1=varsb),
                    waits=[msq, sq_mm], inc="tv")
                sd = S.emit("scalar", lambda e: e.activation(
                    out=stdsb, in_=varsb, func=AF.Sqrt,
                    bias=pc[0:8, 16:17], scale=1.0), waits=[vv], inc="tsc")
                rs = S.emit("vector", lambda e: e.reciprocal(
                    out=rstdsb, in_=stdsb), waits=[sd], inc="tv")
                mb_mm = S.emit("tensor", lambda e: e.matmul(
                    pb[2][:, :], ebc8, musb, start=True, stop=True),
                    waits=[rs], inc="tt")
                rb_mm2 = S.emit("tensor", lambda e: e.matmul(
                    pb[3][:, :], ebc8, rstdsb, start=True, stop=True),
                    waits=[rs], inc="tt")
                t1 = S.emit("vector", lambda e: e.tensor_sub(
                    out=sqsb, in0=attsb, in1=pb[2][:, :]),
                    waits=[mb_mm], inc="tv")
                t2 = S.emit("vector", lambda e: e.tensor_tensor(
                    out=attsb, in0=sqsb, in1=pb[3][:, :], op=ALU.mult),
                    waits=[t1, rb_mm2], inc="tv")
                t3 = S.emit("vector", lambda e: e.tensor_scalar(
                    out=sqsb, in0=attsb, scalar1=pc[:, 8:9],
                    scalar2=pc[:, 9:10], op0=ALU.mult, op1=ALU.add),
                    waits=[t2], inc="tv")
                apply_done.append(S.emit("vector",
                    lambda e, oct_=oct_, csl=csl: e.tensor_tensor(
                    out=tokb[oct_][:, csl], in0=sqsb,
                    in1=tokb[oct_][:, csl], op=ALU.add), waits=[t3], inc="tv"))
        apb = S.barrier_token("tv")

        for oct_ in range(2):
            ro_src = tokb[oct_].rearrange("p (kk d r) -> p kk d r",
                                          kk=4, d=8, r=128)
            ro_src = ro_src.transpose([0, 3, 1, 2])
            ro_dst = tok2b[oct_].rearrange("p (r kk d) -> p r kk d",
                                           r=128, kk=4, d=8)
            ro = S.emit("vector", lambda e, ro_dst=ro_dst, ro_src=ro_src:
                        e.tensor_copy(out=ro_dst, in_=ro_src),
                        waits=[apb], inc="tv")
            dstv = h_tok2.rearrange(
                "o (rt r kq m) -> o rt kq r m", rt=8, r=128, kq=4, m=32)
            for quad in range(2):
                rtl = oct_ * 2 + quad
                rt = chunk * 4 + rtl
                for o in range(16):
                    src = tok2b[oct_][o * 8 + quad * 4:o * 8 + quad * 4 + 4, :]
                    src2 = src.rearrange("so (r m) -> so r m", r=128, m=32)
                    S.dma("sync", dstv[o, rt], src2, waits=[ro], sem="d_t2")



def emit_wprep_stats(S, P):
    ar2, ar3 = P["ar2"], P["ar3"]
    smalls = P["smalls"]
    pc = P["pconst_sb"]
    pb = P["pb"]
    w0 = [P["tok_consts"], P["tok_constsg"], P["tok_xn"],
          S.barrier_token("tsc")]

    acc = smalls[:, 0:1]
    acc2 = smalls[:, 1:2]
    tmpc = smalls[:, 2:3]

    w2v = P["w2s"].rearrange("(a p) d -> a p d", p=128)
    srcs = [P["w1s"][0:128], P["w1s"][128:256],
            w2v[0:4].transpose([1, 0, 2]), w2v[4:8].transpose([1, 0, 2])]
    P["wp_srcs"] = srcs
    prev_red = None
    reds = []
    for i, src in enumerate(srcs):
        buf = ar2 if i % 2 == 0 else ar3
        bufv = buf if i < 2 else buf.rearrange("p (a d) -> p a d", a=4)
        tw = S.dma("scalar", bufv, src, waits=w0 + ([reds[i - 2]] if i >= 2
                                                    else []), sem="d_wps")
        pr = S.emit("vector", lambda e, buf=buf: e.tensor_reduce(
            out=tmpc, in_=buf, axis=AX.X, op=ALU.add,
            apply_absolute_value=True),
            waits=[tw] + ([prev_red] if prev_red else []), inc="tv")
        dst = acc if i < 2 else acc2
        prev_red = S.emit("vector", lambda e, dst=dst, first=(i % 2 == 0):
                          e.tensor_copy(out=dst, in_=tmpc) if first else
                          e.tensor_add(out=dst, in0=dst, in1=tmpc),
                          waits=[pr], inc="tv")
        reds.append(prev_red)
    pm1 = S.emit("tensor", lambda e: e.matmul(
        pb[0][0:1, 0:2], pc[:, 11:12], smalls[:, 0:2],
        start=True, stop=True), waits=[prev_red], inc="tt")
    ev = S.emit("vector", lambda e: e.tensor_copy(
        out=smalls[0:1, 4:6], in_=pb[0][0:1, 0:2]), waits=[pm1], inc="tv")
    st1 = S.dma("sync", P["h_stat_i"], smalls[0:1, 4:6], waits=[ev],
                sem="d_wp")
    wc = S.dma("gpsimd", P["h_W1c"], P["W1s"], waits=[], sem="d_wc")
    wc_wq = S.dma("gpsimd", P["h_wqcs"], P["wqs"], waits=[], sem="d_wc")
    wc_wkv = S.dma("gpsimd", P["h_wkvcs"], P["wkvs"], waits=[], sem="d_wc")
    wc_wo = S.dma("gpsimd", P["h_wocs"], P["wos"], waits=[], sem="d_wc")
    wc_w0 = S.dma("gpsimd", P["h_W0cs"], P["W0s"], waits=[], sem="d_wc")
    P["tok_wc_W1"] = wc
    P["tok_wc_W0"] = wc_w0
    P["tok_wc_wq"], P["tok_wc_wkv"], P["tok_wc_wo"] = wc_wq, wc_wkv, wc_wo
    ar_tok = S.emit("gpsimd", lambda e: e.collective_compute(
        "AllReduce", ALU.add, replica_groups=G8,
        ins=[P["h_stat_i"]], outs=[P["h_stat_o"]]), waits=[st1], inc="tpo")
    P["tok_stat_ar"] = ar_tok
    P["tok_stat_ld"] = S.dma("gpsimd", smalls[0:1, 5:7], P["h_stat_o"],
                             waits=[ar_tok], sem="d_wc")
    P["tok_wprep_bufs_free"] = S.barrier_token("d_wps")
    P["tok_wprep_v"] = S.barrier_token("tv")


def emit_wprep_quant(S, P):
    ar1 = P["ar1"]
    ar1b = P["xn_sb"]
    ar1i = ar1.bitcast(I32)
    smalls = P["smalls"]
    pb = P["pb"]
    srcs = P["wp_srcs"]
    head = [P["tok_stat_ld"], S.barrier_token("tt"), P["tok_wprep_v"]]

    c1_ = S.emit("vector", lambda e: e.tensor_scalar(
        out=smalls[0:1, 5:7], in0=smalls[0:1, 5:7],
        scalar1=1.0 / (D * FF), scalar2=EPS, op0=ALU.mult, op1=ALU.max),
        waits=head, inc="tv")
    c1b_ = S.emit("vector", lambda e: e.tensor_copy(
        out=smalls[0:1, 3:5], in_=smalls[0:1, 5:7]), waits=[c1_], inc="tv")
    c2_ = S.emit("vector", lambda e: e.reciprocal(
        out=smalls[0:1, 5:7], in_=smalls[0:1, 5:7]), waits=[c1b_], inc="tv")
    bc = S.emit("tensor", lambda e: e.matmul(
        pb[0][:, 0:4], P["onesrow_sb"], smalls[0:1, 3:7],
        start=True, stop=True), waits=[c2_, S.barrier_token("tt")],
        inc="tt")
    sc = S.emit("vector", lambda e: e.tensor_copy(
        out=smalls[:, 4:8], in_=pb[0][:, 0:4]), waits=[bc], inc="tv")
    s1col = smalls[:, 6:7]
    s2col = smalls[:, 7:8]
    P["s1col"], P["s2col"] = s1col, s2col
    P["m1col"] = smalls[:, 4:5]
    P["m2col"] = smalls[:, 5:6]

    wslot = [ar1[:, 0:2048], ar1[:, 2048:4096]]
    yb = ar1[:, 4096:5120]
    tib = ar1i[:, 5120:6144]
    c1b = ar1[:, 6144:7168]
    cmb = ar1[:, 7168:8192]
    w2qv = P["h_w2q"].rearrange("(a p) d -> a p d", p=128)
    dsts = [P["h_w1q"][0:128], P["h_w1q"][128:256],
            w2qv[0:4].transpose([1, 0, 2]), w2qv[4:8].transpose([1, 0, 2])]
    slot_store = [None, None]
    slot_lastread = [None, None]
    qprev = sc
    ag1 = None
    for i in range(4):
        scol = s1col if i < 2 else s2col
        for q in range(4):
            sl = (i * 4 + q) % 2
            lsrc = srcs[i][:, q * 2048:(q + 1) * 2048] if i < 2 else \
                srcs[i][:, q]
            wv = [P["tok_attn_tt_c0"]] if (i, q) == (0, 0) else []
            if slot_store[sl]:
                wv.append(slot_store[sl])
            tw = S.dma("scalar", wslot[sl], lsrc, waits=wv, sem="d_wps")
            for b in range(2):
                bsl = slice(b * 1024, (b + 1) * 1024)
                q1 = S.emit("vector", lambda e, sl=sl, bsl=bsl, scol=scol:
                            e.tensor_scalar(
                    out=yb, in0=wslot[sl][:, bsl], scalar1=scol,
                    scalar2=None, op0=ALU.mult), waits=[tw, sc, qprev],
                    inc="tv")
                q2 = S.emit("scalar", lambda e: e.copy(out=tib, in_=yb),
                            waits=[q1], inc="tsc")
                q3 = S.emit("vector", lambda e: e.scalar_tensor_tensor(
                    out=c1b, in0=tib, scalar=0.5, in1=yb, op0=ALU.add,
                    op1=ALU.is_lt), waits=[q2], inc="tv")
                q4 = S.emit("vector", lambda e: e.scalar_tensor_tensor(
                    out=cmb, in0=tib, scalar=-0.5, in1=yb, op0=ALU.add,
                    op1=ALU.is_gt), waits=[q3], inc="tv")
                q5 = S.emit("vector", lambda e: e.tensor_add(
                    out=yb, in0=tib, in1=c1b), waits=[q4], inc="tv")
                q6 = S.emit("vector", lambda e: e.tensor_sub(
                    out=c1b, in0=yb, in1=cmb), waits=[q5], inc="tv")
                q7 = S.emit("vector", lambda e, sl=sl, b=b: e.tensor_scalar(
                    out=ar1b[:, sl * 4096 + b * 2048:
                             sl * 4096 + b * 2048 + 1024],
                    in0=c1b, scalar1=1.0, scalar2=-1.0,
                    op0=ALU.min, op1=ALU.max), waits=[q6], inc="tv")
                qprev = q7
            dsthbm = dsts[i][:, q * 2048:(q + 1) * 2048] if i < 2 else \
                dsts[i][:, q]
            st = S.dma("scalar", dsthbm.rearrange("p (b r) -> p b r", b=2),
                       ar1b[:, sl * 4096:sl * 4096 + 4096].rearrange(
                           "p (b r) -> p b r", b=2)[:, :, 0:1024],
                       waits=[qprev], sem="d_wps")
            slot_store[sl] = st
            if i == 1 and q == 3:
                ag1 = S.emit("gpsimd", lambda e: e.collective_compute(
                    "AllGather", ALU.bypass, replica_groups=G8,
                    ins=[P["h_w1q"]], outs=[P["h_w1g"]]),
                    waits=[st], inc="tpo")
                P["tok_w1g_ffn"] = ag1
    ag2 = S.emit("gpsimd", lambda e: e.collective_compute(
        "AllGather", ALU.bypass, replica_groups=G8,
        ins=[P["h_w2q"]], outs=[P["h_w2g"]]),
        waits=[ag1, slot_store[0], slot_store[1]], inc="tpo")
    P["tok_wq"] = ag2
    P["tok_quant_done"] = S.barrier_token("d_wps")


def emit_wprep_ags(S, P):
    P["tok_ag"] = S.emit("gpsimd", lambda e: e.collective_compute(
        "AllGather", ALU.bypass, replica_groups=G4,
        ins=[P["h_tok2"]], outs=[P["h_tok2g"]]),
        waits=[S.barrier_token("d_t2"), P["tok_wq"]], inc="tpo")
    ag_wq = S.emit("gpsimd", lambda e: e.collective_compute(
        "AllGather", ALU.bypass, replica_groups=G8,
        ins=[P["h_wqcs"]], outs=[P["h_wqc"]]),
        waits=[P["tok_ag"], P["tok_wc_wq"]], inc="tpo")
    ag_wkv = S.emit("gpsimd", lambda e: e.collective_compute(
        "AllGather", ALU.bypass, replica_groups=G8,
        ins=[P["h_wkvcs"]], outs=[P["h_wkvc"]]),
        waits=[ag_wq, P["tok_wc_wkv"]], inc="tpo")
    ag_wo = S.emit("gpsimd", lambda e: e.collective_compute(
        "AllGather", ALU.bypass, replica_groups=G8,
        ins=[P["h_wocs"]], outs=[P["h_woc"]]), waits=[ag_wkv, P["tok_wc_wo"]],
        inc="tpo")
    P["tok_wqg"] = ag_wq
    P["tok_wkvg"] = ag_wkv
    P["tok_wog"] = ag_wo


def emit_l0ffn(S, P):
    ar1b = P["xn_sb"]
    ar2b, ar3b = P["pk"], P["pk2"]
    ar4, ar56 = P["ar4"], P["ar56"]
    ar4b, ar56b = P["ar4b"], P["ar56b"]
    ar56i, ar4i = P["ar56i"], P["ar4"].bitcast(I32)
    smalls, smb = P["smalls"], P["smb"]
    pc = P["pconst_sb"]
    pb = P["pb"]
    pbt_f = P["pbt"].bitcast(F32)
    pbt2_f = P["pbt2"].bitcast(F32)
    banks = P["pb"] + [pbt_f[:, 0:512], pbt2_f[:, 0:512]]
    xn_sb = P["xn_sb"]
    x_sb = P["x_sb"]
    identb = P["identb"]
    s1col, s2col = P["s1col"], P["s2col"]

    ffw = [S.barrier_token("d_t2"), S.barrier_token("tv"),
           S.barrier_token("tt"), S.barrier_token("tsc"),
           S.barrier_token("d_rp"), S.barrier_token("d_gb"),
           S.barrier_token("d_wp"), S.barrier_token("d_wps")]

    amcol = smalls[:, 10:11]
    invcols = smalls[:, 12:20]
    sccol = smalls[:, 11:12]
    xfb = ar4[:, 0:2048]
    yb = ar4[:, 2048:4096]
    tib = ar56i[:, 0:2048]
    c1b = ar56[:, 2048:4096]
    cmb = ar4[:, 0:2048]
    xqb = ar56b[:, 0:2048]

    prev = None
    for t in range(NT):
        w = ffw + ([prev] if prev else [])
        a0 = S.emit("vector", lambda e, t=t: e.tensor_scalar(
            out=xfb, in0=x_sb[:, t * D:(t + 1) * D],
            scalar1=P["mucols"][:, t:t + 1], scalar2=P["rstdcols"][:, t:t + 1],
            op0=ALU.subtract, op1=ALU.mult), waits=w, inc="tv")
        xt = xfb
        a1 = S.emit("vector", lambda e, xt=xt: e.tensor_reduce(
            out=amcol, in_=xt, axis=AX.X, op=ALU.max,
            apply_absolute_value=True), waits=[a0], inc="tv")
        a2 = S.emit("vector", lambda e: e.tensor_scalar(
            out=amcol, in0=amcol, scalar1=EPS, scalar2=None, op0=ALU.max),
            waits=[a1], inc="tv")
        a3 = S.emit("vector", lambda e: e.reciprocal(out=sccol, in_=amcol),
                    waits=[a2], inc="tv")
        a4 = S.emit("vector", lambda e: e.tensor_scalar(
            out=sccol, in0=sccol, scalar1=127.0, scalar2=None, op0=ALU.mult),
            waits=[a3], inc="tv")
        a5 = S.emit("vector", lambda e, t=t: e.tensor_scalar(
            out=invcols[:, t:t + 1], in0=amcol, scalar1=1.0 / 127.0,
            scalar2=None, op0=ALU.mult), waits=[a4], inc="tv")
        a5b = S.emit("vector", lambda e, t=t: e.tensor_tensor(
            out=invcols[:, t:t + 1], in0=invcols[:, t:t + 1], in1=P["m1col"],
            op=ALU.mult), waits=[a5], inc="tv")
        q1 = S.emit("vector", lambda e, xt=xt: e.tensor_scalar(
            out=yb, in0=xt, scalar1=sccol, scalar2=None, op0=ALU.mult),
            waits=[a5b], inc="tv")
        q2 = S.emit("scalar", lambda e: e.copy(out=tib, in_=yb),
                    waits=[q1] + ffw, inc="tsc")
        q3 = S.emit("vector", lambda e: e.scalar_tensor_tensor(
            out=c1b, in0=tib, scalar=0.5, in1=yb, op0=ALU.add,
            op1=ALU.is_lt), waits=[q2], inc="tv")
        q4 = S.emit("vector", lambda e: e.scalar_tensor_tensor(
            out=cmb, in0=tib, scalar=-0.5, in1=yb, op0=ALU.add,
            op1=ALU.is_gt), waits=[q3], inc="tv")
        q5 = S.emit("vector", lambda e: e.tensor_add(
            out=yb, in0=tib, in1=c1b), waits=[q4], inc="tv")
        q6 = S.emit("vector", lambda e: e.tensor_sub(
            out=xqb, in0=yb, in1=cmb), waits=[q5], inc="tv")
        tts = []
        for dc in range(16):
            bank = P["pbt"] if dc < 8 else P["pbt2"]
            kk8 = dc % 8
            w2 = [q6]
            if prev and dc == 0:
                w2 = [q6, prev]
            tts.append(S.emit("tensor", lambda e, dc=dc, bank=bank, kk8=kk8:
                              e.transpose(
                bank[:, kk8 * 128:(kk8 + 1) * 128],
                xqb[:, dc * 128:(dc + 1) * 128], identb),
                waits=w2 if dc == 0 else [], inc="tt"))
        e1 = S.emit("vector", lambda e, t=t: e.tensor_copy(
            out=ar3b.rearrange("p (dc t r) -> p dc t r", dc=16, t=8)[
                :, 0:8, t, :].rearrange("p a b -> p (a b)") if False else
            ar3b.rearrange("p (dc tt r) -> p dc tt r", dc=16, tt=8)[
                :, 0:8, t], in_=P["pbt"].rearrange(
                "p (a r) -> p a r", a=8)), waits=[tts[7]], inc="tv")
        e2 = S.emit("vector", lambda e, t=t: e.tensor_copy(
            out=ar3b.rearrange("p (dc tt r) -> p dc tt r", dc=16, tt=8)[
                :, 8:16, t], in_=P["pbt2"].rearrange(
                "p (a r) -> p a r", a=8)), waits=[tts[15]], inc="tv")
        prev = e2

    xqT_done = prev

    gbc = ar1b[:, 0:8192]
    bbc = ar1b[:, 8192:16384]
    tg = S.dma("gpsimd", gbc, P["vrows"][1:2, :].broadcast_to([128, FF]),
               waits=[xqT_done, P["tok_quant_done"]], sem="d_fg")
    tb = S.dma("gpsimd", bbc, P["vrows"][2:3, :].broadcast_to([128, FF]),
               waits=[xqT_done], sem="d_fg")

    b1bc = ar4[:, 0:512]
    hbou = [ar4[:, 512 + i * 512:1024 + i * 512] for i in range(2)]
    hbou_b = [ar4.bitcast(BF16)[:, 1024 + i * 512:1536 + i * 512]
              for i in range(2)]
    slab = [ar2b[:, 0:8192], ar2b[:, 8192:16384]]
    slab_tok = [None, None]
    evs = {}
    mm_last = {}
    ev_last = {}
    prev_b1 = None
    for fc in range(16):
        sl = fc % 2
        fsl = slice(fc * 512, (fc + 1) * 512)
        w = [P["tok_w1g_ffn"], xqT_done]
        if fc >= 2:
            w.append(mm_last[fc - 2])
        w = [x for x in w if x]
        src = P["h_w1g"][:, fsl].rearrange("(dc p) f -> dc p f", p=128)
        sv = slab[sl].rearrange("p (dc f) -> p dc f", dc=16)
        lta = S.dma("sync", sv[:, 0:8], src[0:8].transpose([1, 0, 2]),
                    waits=w, sem="d_ff")
        ltb = S.dma("scalar", sv[:, 8:16], src[8:16].transpose([1, 0, 2]),
                    waits=w, sem="d_ffs")
        slab_tok[sl] = lta
        bt = S.dma("gpsimd", ar4[:, 0:512],
                   P["vrows"][0:1, fsl].broadcast_to([128, 512]),
                   waits=[prev_b1] if prev_b1 else ffw, sem="d_fg")
        mms = []
        mm = None
        for dc in range(16):
            for t in range(NT):
                w2 = [lta, ltb]
                if fc >= 1 and dc == 0 and t == 0:
                    w2.append(ev_last[fc - 1])
                lhs = ar3b.rearrange("p (dc tt r) -> p dc tt r",
                                     dc=16, tt=8)[:, dc, t]
                mm = S.emit("tensor", lambda e, lhs=lhs, sl=sl, dc=dc, t=t:
                            e.matmul(
                    banks[t][:, :], lhs,
                    slab[sl][:, dc * 512:(dc + 1) * 512],
                    start=(dc == 0), stop=(dc == 15)),
                    waits=w2 if (dc == 0 and t == 0) else [], inc="tt")
                mms.append(mm)
        eprev = None
        if fc == 0:
            slot_last = [None, None]
        for t in range(NT):
            wv = [mms[-(NT - t)], bt] + ([eprev] if eprev else [])
            if slot_last[t % 2]:
                wv.append(slot_last[t % 2])
            eprev = S.emit("vector", lambda e, t=t, fc=fc: e.scalar_tensor_tensor(
                out=hbou_b[t % 2], in0=banks[t][:, :],
                scalar=invcols[:, t:t + 1], in1=ar4[:, 0:512],
                op0=ALU.mult, op1=ALU.add), waits=wv, inc="tv")
            slot_last[t % 2] = S.dma("sync" if t % 2 == 0 else "scalar",
                                     P["h_h"][t * 128:(t + 1) * 128, fsl],
                                     hbou_b[t % 2], waits=[eprev],
                                     sem="d_ff" if t % 2 == 0 else "d_ffs")
            evs[(fc, t)] = slot_last[t % 2]
        mm_last[fc] = mms[-1]
        ev_last[fc] = eprev
        prev_b1 = eprev
    ffn1_done = S.barrier_token("d_ff")
    ffn1s_done = S.barrier_token("d_ffs")

    gtile = P["ar3"]
    htbs = [P["pk"][:, 0:8192], P["pk"][:, 8192:16384]]
    ydb = P["ar4"][:, 0:2048]
    c1db = P["ar4"][:, 2048:4096]
    tidb = P["ar56i"][:, 0:2048]
    cmdb = P["ar56"][:, 2048:4096]
    hqblk = smb[:, 96:2144]
    hbnc = smb[:, 2144:3168]
    mucol = smalls[:, 20:21]
    esqcol = smalls[:, 21:22]
    prev_spill = None
    bounce_free = None
    htb_free = [None, None]
    gtile_free = None
    scratch_free = None
    for t in range(NT):
        sl = t % 2
        htb = htbs[sl]
        w = [ffn1_done, ffn1s_done]
        if htb_free[sl]:
            w.append(htb_free[sl])
        if t < 2:
            w.append(S.barrier_token("tv"))
            w.append(S.barrier_token("tsc"))
        l1a = S.dma("sync", htb[:, 0:4096],
                    P["h_h"][t * 128:(t + 1) * 128, 0:4096],
                    waits=w, sem="d_hp")
        l1b = S.dma("scalar", htb[:, 4096:8192],
                    P["h_h"][t * 128:(t + 1) * 128, 4096:8192],
                    waits=w, sem="d_hps")
        wv = [l1a, l1b]
        if gtile_free:
            wv.append(gtile_free)
        g1 = S.emit("scalar", lambda e, htb=htb: e.activation(
            out=gtile, in_=htb, func=AF.Gelu), waits=wv, inc="tsc")
        htb_free[sl] = g1
        m1 = S.emit("vector", lambda e: e.tensor_reduce(
            out=mucol, in_=gtile, axis=AX.X, op=ALU.add), waits=[g1],
            inc="tv")
        m2 = S.emit("vector", lambda e: e.tensor_scalar(
            out=mucol, in0=mucol, scalar1=1.0 / FF, scalar2=None,
            op0=ALU.mult), waits=[m1], inc="tv")
        sq_ac = []
        for i4 in range(4):
            wv = [g1] + ([sq_ac[-1]] if sq_ac else [])
            if i4 == 0 and scratch_free:
                wv.append(scratch_free)
            sq_ac.append(S.emit("scalar", lambda e, i4=i4: e.activation(
                out=ydb, in_=gtile[:, i4 * 2048:(i4 + 1) * 2048],
                func=AF.Square, scale=float(FF) ** -0.5,
                accum_out=smalls[:, 22 + i4:23 + i4]), waits=wv, inc="tsc"))
        s1_ = S.emit("vector", lambda e: e.tensor_add(
            out=esqcol, in0=smalls[:, 22:23], in1=smalls[:, 23:24]),
            waits=[sq_ac[-1]], inc="tv")
        s2_ = S.emit("vector", lambda e: e.tensor_add(
            out=smalls[:, 22:23], in0=smalls[:, 24:25], in1=smalls[:, 25:26]),
            waits=[s1_], inc="tv")
        s3_ = S.emit("vector", lambda e: e.tensor_add(
            out=esqcol, in0=esqcol, in1=smalls[:, 22:23]), waits=[s2_],
            inc="tv")
        v1 = S.emit("vector", lambda e: e.tensor_mul(
            out=smalls[:, 22:23], in0=mucol, in1=mucol), waits=[s3_, m2],
            inc="tv")
        v2 = S.emit("vector", lambda e: e.tensor_sub(
            out=esqcol, in0=esqcol, in1=smalls[:, 22:23]), waits=[v1],
            inc="tv")
        v3 = S.emit("scalar", lambda e: e.activation(
            out=esqcol, in_=esqcol, func=AF.Sqrt, bias=pc[:, 16:17],
            scale=1.0), waits=[v2], inc="tsc")
        v4 = S.emit("vector", lambda e: e.reciprocal(
            out=esqcol, in_=esqcol), waits=[v3], inc="tv")
        z1 = S.emit("vector", lambda e: e.tensor_scalar(
            out=gtile, in0=gtile, scalar1=mucol, scalar2=esqcol,
            op0=ALU.subtract, op1=ALU.mult),
            waits=[v4, sq_ac[-1], tg, tb], inc="tv")
        z2 = S.emit("vector", lambda e: e.tensor_mul(
            out=gtile, in0=gtile, in1=gbc), waits=[z1], inc="tv")
        z3 = S.emit("vector", lambda e: e.tensor_add(
            out=gtile, in0=gtile, in1=bbc), waits=[z2], inc="tv")
        a1 = S.emit("vector", lambda e: e.tensor_reduce(
            out=amcol, in_=gtile, axis=AX.X, op=ALU.max,
            apply_absolute_value=True), waits=[z3], inc="tv")
        a2 = S.emit("vector", lambda e: e.tensor_scalar(
            out=amcol, in0=amcol, scalar1=EPS, scalar2=None, op0=ALU.max),
            waits=[a1], inc="tv")
        a3 = S.emit("vector", lambda e: e.reciprocal(out=sccol, in_=amcol),
                    waits=[a2], inc="tv")
        a4 = S.emit("vector", lambda e: e.tensor_scalar(
            out=sccol, in0=sccol, scalar1=127.0, scalar2=None, op0=ALU.mult),
            waits=[a3], inc="tv")
        a5 = S.emit("vector", lambda e, t=t: e.tensor_scalar(
            out=invcols[:, t:t + 1], in0=amcol, scalar1=1.0 / 127.0,
            scalar2=None, op0=ALU.mult), waits=[a4], inc="tv")
        a5b = S.emit("vector", lambda e, t=t: e.tensor_tensor(
            out=invcols[:, t:t + 1], in0=invcols[:, t:t + 1], in1=P["m2col"],
            op=ALU.mult), waits=[a5], inc="tv")
        qprev = a5b
        blk_free = None
        ev_prev_grp = None
        for i4 in range(4):
            bsl = slice(i4 * 2048, (i4 + 1) * 2048)
            q1 = S.emit("vector", lambda e, bsl=bsl: e.tensor_scalar(
                out=ydb, in0=gtile[:, bsl], scalar1=sccol, scalar2=None,
                op0=ALU.mult), waits=[qprev], inc="tv")
            if i4 == 3:
                gtile_free = q1
            q2 = S.emit("scalar", lambda e: e.copy(out=tidb, in_=ydb),
                        waits=[q1], inc="tsc")
            q3 = S.emit("vector", lambda e: e.scalar_tensor_tensor(
                out=c1db, in0=tidb, scalar=0.5, in1=ydb, op0=ALU.add,
                op1=ALU.is_lt), waits=[q2], inc="tv")
            q4 = S.emit("vector", lambda e: e.scalar_tensor_tensor(
                out=cmdb, in0=tidb, scalar=-0.5, in1=ydb, op0=ALU.add,
                op1=ALU.is_gt), waits=[q3], inc="tv")
            q5 = S.emit("vector", lambda e: e.tensor_add(
                out=ydb, in0=tidb, in1=c1db), waits=[q4], inc="tv")
            wv = [q5]
            if blk_free:
                wv.append(blk_free)
            q6 = S.emit("vector", lambda e: e.tensor_sub(
                out=hqblk, in0=ydb, in1=cmdb), waits=wv, inc="tv")
            qprev = q6
            if i4 == 3:
                scratch_free = q6
            for gj in range(2):
                g8 = i4 * 2 + gj
                tts = []
                for i8 in range(8):
                    wv = []
                    if i8 == 0:
                        wv = [q6]
                        if ev_prev_grp:
                            wv.append(ev_prev_grp)
                    tts.append(S.emit("tensor", lambda e, gj=gj, i8=i8:
                                      e.transpose(
                        P["pbt"][:, i8 * 128:(i8 + 1) * 128],
                        hqblk[:, (gj * 8 + i8) * 128:
                              (gj * 8 + i8 + 1) * 128],
                        identb), waits=wv, inc="tt"))
                ev = S.emit("vector", lambda e: e.tensor_copy(
                    out=hbnc, in_=P["pbt"][:, :]),
                    waits=[tts[-1]] +
                    ([bounce_free] if bounce_free else []), inc="tv")
                ev_prev_grp = ev
                blk_free = ev
                dst = P["h_hqT"].rearrange("(fc p) tok -> fc p tok", p=128)[
                    g8 * 8:(g8 + 1) * 8, :,
                    t * 128:(t + 1) * 128].transpose([1, 0, 2])
                prev_spill = S.dma("sync", dst,
                                   hbnc.rearrange("p (a r) -> p a r", a=8),
                                   waits=[ev], sem="d_hp")
                bounce_free = prev_spill
    hq_done = S.barrier_token("d_hp")

    w2b = [ar2b[:, 0:512], ar2b[:, 512:1024]]
    hqb = [ar2b[:, 1024 + i * 1024: 2048 + i * 1024] for i in range(2)]
    b2bc = ar4[:, 0:2048]
    tb2 = S.dma("gpsimd", b2bc,
                P["vrows"][3:4, 0:2048].broadcast_to([128, 2048]),
                waits=[hq_done, S.barrier_token("tv")], sem="d_fg")
    x1parts = []
    slot_mm = [None, None]
    prev_dq_ev = None
    for dq in range(4):
        dsl = slice(dq * 512, (dq + 1) * 512)
        mm = None
        for fc in range(64):
            sl = fc % 2
            w = [hq_done, P["tok_wq"]]
            if slot_mm[sl] is not None:
                w.append(slot_mm[sl])
            qa, sa = ("sync", "d_f2") if fc % 2 == 0 else \
                ("scalar", "d_f2s")
            qb, sb2 = ("scalar", "d_f2s") if fc % 2 == 0 else \
                ("sync", "d_f2")
            wt = S.dma(qa, w2b[sl],
                       P["h_w2g"][fc * 128:(fc + 1) * 128, dsl],
                       waits=w, sem=sa)
            ht = S.dma(qb, hqb[sl],
                       P["h_hqT"][fc * 128:(fc + 1) * 128, :],
                       waits=w, sem=sb2)
            for t in range(NT):
                w2_ = []
                if t == 0:
                    w2_ = [wt, ht]
                    if fc == 0 and prev_dq_ev is not None:
                        w2_.append(prev_dq_ev)
                mm = S.emit("tensor", lambda e, sl=sl, t=t, fc=fc: e.matmul(
                    banks[t][:, :], hqb[sl][:, t * 128:(t + 1) * 128],
                    w2b[sl], start=(fc == 0), stop=(fc == 63)),
                    waits=w2_, inc="tt")
            slot_mm[sl] = mm
        eprev = None
        for t in range(NT):
            xsl = x_sb[:, t * D + dq * 512: t * D + (dq + 1) * 512]
            e1 = S.emit("vector", lambda e, t=t, dq=dq: e.scalar_tensor_tensor(
                out=ar4[:, 2048:2560], in0=banks[t][:, :],
                scalar=invcols[:, t:t + 1],
                in1=ar4[:, dq * 512:(dq + 1) * 512],
                op0=ALU.mult, op1=ALU.add),
                waits=[mm, tb2] + ([eprev] if eprev else []), inc="tv")
            eprev = S.emit("vector", lambda e, xsl=xsl: e.tensor_add(
                out=xsl, in0=ar4[:, 2048:2560], in1=xsl), waits=[e1],
                inc="tv")
            x1parts.append(eprev)
        prev_dq_ev = eprev
    P["tok_x1p"] = S.barrier_token("tv")


def emit_x1(S, P):
    x_sb = P["x_sb"]
    prevs = []

    def consume(t, recv, tok):
        w = [tok, P["tok_x1p"]] + prevs[-1:]
        r = S.emit("vector", lambda e, t=t, recv=recv: e.tensor_add(
            out=x_sb[:, t * D:(t + 1) * D], in0=recv,
            in1=x_sb[:, t * D:(t + 1) * D]), waits=w, inc="tv")
        prevs.append(r)
        return r

    emit_attn0_gather(S, P, to_out=False, extra_waits=[P["tok_x1p"]],
                      consume_fn=consume)
    P["tok_x1"] = S.barrier_token("tv")


def emit_l1(S, P):
    x_sb = P["x_sb"]
    ar1, ar4 = P["ar1"], P["ar4"]
    ar1b = P["xn_sb"]
    ar2b, ar3b = P["pk"], P["pk2"]
    ar4b, ar56b = P["ar4b"], P["ar56b"]
    smalls, smb = P["smalls"], P["smb"]
    pc = P["pconst_sb"]
    pb = P["pb"]
    banks = pb + [P["pbt"].bitcast(F32)[:, 0:512],
                  P["pbt2"].bitcast(F32)[:, 0:512]]
    identb = P["identb"]
    ar3v = ar3b.rearrange("p (dc tt r) -> p dc tt r", dc=16, tt=8)

    l1w = [P["tok_x1"], S.barrier_token("tt"), S.barrier_token("tsc"),
           S.barrier_token("d_f2"), S.barrier_token("d_gb"),
           S.barrier_token("d_fg")]
    cut = os.environ.get("KBD_L1CUT", "")

    def cut_here(name):
        if cut == name:
            P["tok_a1_added"] = S.barrier_token("tv")
            return True
        return False

    mu = smalls[:, 0:1]; esq = smalls[:, 1:2]; var = smalls[:, 2:3]
    std = smalls[:, 3:4]; rstd = smalls[:, 4:5]; sc5 = smalls[:, 5:6]
    epsc = pc[:, 16:17]

    prev = None
    for t in range(NT):
        xt = x_sb[:, t * D:(t + 1) * D]
        w = l1w + ([prev] if prev else [])
        t1 = S.emit("vector", lambda e, xt=xt: e.tensor_reduce(
            out=mu, in_=xt, axis=AX.X, op=ALU.add), waits=w, inc="tv")
        t2 = S.emit("scalar", lambda e, xt=xt: e.activation(
            out=P["ar2"][:, 0:D], in_=xt, func=AF.Square,
            scale=float(D) ** -0.5, accum_out=esq), waits=w, inc="tsc")
        t3 = S.emit("vector", lambda e: e.tensor_scalar(
            out=mu, in0=mu, scalar1=1.0 / D, scalar2=None, op0=ALU.mult),
            waits=[t1, t2], inc="tv")
        t5 = S.emit("vector", lambda e: e.tensor_mul(
            out=sc5, in0=mu, in1=mu), waits=[t3], inc="tv")
        t5b = S.emit("vector", lambda e: e.tensor_sub(
            out=var, in0=esq, in1=sc5), waits=[t5], inc="tv")
        t6 = S.emit("scalar", lambda e: e.activation(
            out=std, in_=var, func=AF.Sqrt, bias=epsc, scale=1.0),
            waits=[t5b], inc="tsc")
        t7 = S.emit("vector", lambda e: e.reciprocal(out=rstd, in_=std),
                    waits=[t6], inc="tv")
        prev = S.emit("vector", lambda e, t=t, xt=xt: e.tensor_scalar(
            out=ar1b[:, t * D:(t + 1) * D], in0=xt, scalar1=mu, scalar2=rstd,
            op0=ALU.subtract, op1=ALU.mult), waits=[t7], inc="tv")
    xn1_done = prev

    prevt = None
    for t in range(NT):
        tts = []
        for dc in range(16):
            bank = P["pbt"] if dc < 8 else P["pbt2"]
            kk8 = dc % 8
            w = []
            if dc == 0:
                w = [xn1_done] + ([prevt] if prevt else [])
            tts.append(S.emit("tensor", lambda e, t=t, dc=dc, bank=bank,
                              kk8=kk8: e.transpose(
                bank[:, kk8 * 128:(kk8 + 1) * 128],
                ar1b[:, t * D + dc * 128:t * D + (dc + 1) * 128], identb),
                waits=w, inc="tt"))
        e1 = S.emit("vector", lambda e, t=t: e.tensor_copy(
            out=ar3v[:, 0:8, t], in_=P["pbt"].rearrange(
                "p (a r) -> p a r", a=8)), waits=[tts[7]], inc="tv")
        e2 = S.emit("vector", lambda e, t=t: e.tensor_copy(
            out=ar3v[:, 8:16, t], in_=P["pbt2"].rearrange(
                "p (a r) -> p a r", a=8)), waits=[tts[15], e1], inc="tv")
        prevt = e2
    xn1T_done = prevt
    if cut_here("B"):
        return

    cst0 = P["prod_sb"]
    cst1 = P["pv_sb"]
    dmask_sb = smb[:, 3168:3680]
    tk_c0 = S.dma("sync", cst0, P["cstab_in"][:, 0:512], waits=l1w,
                  sem="d_l1")
    tk_c1 = S.dma("sync", cst1, P["cstab_in"][:, 512:1024], waits=l1w,
                  sem="d_l1")
    tk_dm = S.dma("sync", dmask_sb, P["dmask_in"], waits=l1w, sem="d_l1")

    def costl(t):
        base = (cst0 if t < 4 else cst1)
        return base[:, (t % 4) * 128:(t % 4) * 128 + 64]

    def sintl(t):
        base = (cst0 if t < 4 else cst1)
        return base[:, (t % 4) * 128 + 64:(t % 4) * 128 + 128]

    slabs = [ar1b[:, 0:8192], ar1b[:, 8192:16384]]
    bias_bufs = [ar4[:, 0:512], ar4[:, 512:1024]]
    q_tm = ar2b
    kv_tm = ar56b[:, 0:4096]
    slab_free = [None, None]
    bias_free = [None, None]
    evac_last = None
    mm_last = None
    for ch in range(5):
        sl = ch % 2
        if ch < 4:
            wsrc = P["h_wqc"][:, ch * 512:(ch + 1) * 512]
            agw = P["tok_wqg"]
            brow, bsl = 6, slice(ch * 512, (ch + 1) * 512)
        else:
            wsrc = P["h_wkvc"]
            agw = P["tok_wkvg"]
            brow, bsl = 7, slice(0, 512)
        w = [xn1T_done, agw]
        if slab_free[sl]:
            w.append(slab_free[sl])
        psrc = wsrc.rearrange("(dc p) f -> dc p f", p=128)
        pdst = slabs[sl].rearrange("p (dc f) -> p dc f", dc=16)
        lt_a = S.dma("sync", pdst[:, 0:8], psrc[0:8].transpose([1, 0, 2]),
                     waits=w, sem="d_l1")
        lt = S.dma("scalar", pdst[:, 8:16], psrc[8:16].transpose([1, 0, 2]),
                   waits=w, sem="d_l1s")
        bw = [bias_free[sl]] if bias_free[sl] else l1w
        bt = S.dma("gpsimd", bias_bufs[sl],
                   P["vrows"][brow:brow + 1, bsl].broadcast_to([128, 512]),
                   waits=bw, sem="d_fg")
        mm = None
        for dc in range(16):
            for t in range(NT):
                w2 = []
                if dc == 0 and t == 0:
                    w2 = [lt_a, lt] + ([evac_last] if evac_last else [])
                mm = S.emit("tensor", lambda e, dc=dc, t=t, sl=sl: e.matmul(
                    banks[t][:, :], ar3v[:, dc, t],
                    slabs[sl][:, dc * 512:(dc + 1) * 512],
                    start=(dc == 0), stop=(dc == 15)), waits=w2, inc="tt")
        slab_free[sl] = mm
        mm_last = mm
        eprev = None
        for t in range(NT):
            if ch < 4:
                dst = q_tm[:, t * 2048 + ch * 512:t * 2048 + (ch + 1) * 512]
            else:
                dst = kv_tm[:, t * 512:(t + 1) * 512]
            eprev = S.emit("vector", lambda e, t=t, dst=dst, sl=sl:
                           e.tensor_tensor(
                out=dst, in0=banks[t][:, :], in1=bias_bufs[sl],
                op=ALU.add),
                waits=[mm, bt] + ([eprev] if eprev else []), inc="tv")
        bias_free[sl] = eprev
        evac_last = eprev
    proj_done = evac_last
    if cut_here("D"):
        return

    r1 = ar4b[:, 0:1024]; r2 = ar4b[:, 1024:2048]
    r3 = ar4b[:, 2048:3072]; r4 = ar4b[:, 3072:4096]
    rprev = proj_done
    for t in range(NT):
        for which in (0, 1):
            if which == 0:
                nh = 16
                base = q_tm[:, t * 2048:(t + 1) * 2048]
            else:
                nh = 2
                base = kv_tm[:, t * 512:t * 512 + 256]
            v3 = base.rearrange("p (h d) -> p h d", h=nh)
            d1 = v3[:, :, 0:64]
            d2 = v3[:, :, 64:128]
            cosb = costl(t).rearrange("p (o f) -> p o f", o=1).broadcast_to(
                [128, nh, 64])
            sinb = sintl(t).rearrange("p (o f) -> p o f", o=1).broadcast_to(
                [128, nh, 64])
            n64 = nh * 64
            r1f = r1[:, 0:n64].rearrange("p (h d) -> p h d", h=nh)
            r2f = r2[:, 0:n64].rearrange("p (h d) -> p h d", h=nh)
            r3f = r3[:, 0:n64].rearrange("p (h d) -> p h d", h=nh)
            r4f = r4[:, 0:n64].rearrange("p (h d) -> p h d", h=nh)
            o1 = S.emit("vector", lambda e, d1=d1, cosb=cosb, r1f=r1f:
                        e.tensor_mul(out=r1f, in0=d1, in1=cosb),
                        waits=[rprev, tk_c0, tk_c1], inc="tv")
            o2 = S.emit("vector", lambda e, d2=d2, sinb=sinb, r2f=r2f:
                        e.tensor_mul(out=r2f, in0=d2, in1=sinb),
                        waits=[o1], inc="tv")
            o3 = S.emit("vector", lambda e, d1=d1, sinb=sinb, r3f=r3f:
                        e.tensor_mul(out=r3f, in0=d1, in1=sinb),
                        waits=[o2], inc="tv")
            o4 = S.emit("vector", lambda e, d2=d2, cosb=cosb, r4f=r4f:
                        e.tensor_mul(out=r4f, in0=d2, in1=cosb),
                        waits=[o3], inc="tv")
            o5 = S.emit("vector", lambda e, d1=d1, r1f=r1f, r2f=r2f:
                        e.tensor_sub(out=d1, in0=r1f, in1=r2f),
                        waits=[o4], inc="tv")
            rprev = S.emit("vector", lambda e, d2=d2, r3f=r3f, r4f=r4f:
                           e.tensor_add(out=d2, in0=r4f, in1=r3f),
                           waits=[o5], inc="tv")
    rope_done = rprev
    if cut_here("E"):
        return

    for t in range(NT):
        S.dma("sync", P["h_kv"][t * 128:(t + 1) * 128, :],
              kv_tm[:, t * 512:(t + 1) * 512], waits=[rope_done],
              sem="d_kv")
    ag_kv = S.emit("gpsimd", lambda e: e.collective_compute(
        "AllGather", ALU.bypass, replica_groups=G4,
        ins=[P["h_kv"]], outs=[P["h_kvg"]]),
        waits=[S.barrier_token("d_kv")], inc="tpo")
    if DBG_STAGE >= 5:
        ag_W0 = S.emit("gpsimd", lambda e: e.collective_compute(
            "AllGather", ALU.bypass, replica_groups=G8,
            ins=[P["h_W0cs"]], outs=[P["h_W0g"]]),
            waits=[ag_kv, P["tok_wc_W0"]], inc="tpo")
        ag_W1 = S.emit("gpsimd", lambda e: e.collective_compute(
            "AllGather", ALU.bypass, replica_groups=G8,
            ins=[P["h_W1c"]], outs=[P["h_W1g"]]),
            waits=[ag_W0, P["tok_wc_W1"]], inc="tpo")
        P["tok_W0g"], P["tok_W1g"] = ag_W0, ag_W1

    qf_prev = None
    for t in range(NT):
        tts = []
        for dc in range(16):
            bank = P["pbt"] if dc < 8 else P["pbt2"]
            kk8 = dc % 8
            w = []
            if dc == 0:
                w = [rope_done] + ([qf_prev] if qf_prev else [])
            tts.append(S.emit("tensor", lambda e, t=t, dc=dc, bank=bank,
                              kk8=kk8: e.transpose(
                bank[:, kk8 * 128:(kk8 + 1) * 128],
                q_tm[:, t * 2048 + dc * 128:t * 2048 + (dc + 1) * 128],
                identb), waits=w, inc="tt"))
        ev1 = S.emit("vector", lambda e: e.tensor_copy(
            out=ar4b[:, 0:1024], in_=P["pbt"][:, :]), waits=[tts[7]],
            inc="tv")
        ev2 = S.emit("vector", lambda e: e.tensor_copy(
            out=ar4b[:, 1024:2048], in_=P["pbt2"][:, :]),
            waits=[tts[15], ev1], inc="tv")
        qf_prev = S.dma("sync", P["h_qf"].rearrange(
            "(dc p) tok -> dc p tok", p=128)[
            :, :, t * 128:(t + 1) * 128].transpose([1, 0, 2]),
            ar4b[:, 0:2048].rearrange("p (dc r) -> p dc r", dc=16),
            waits=[ev2], sem="d_qf")
    qf_done = S.barrier_token("d_qf")
    if cut_here("G"):
        return

    kfm = ar56b
    kvsrc = P["h_kvg"].rearrange("(cc tt p) c -> cc tt p c", cc=4, tt=8)
    kb_prev = None
    kfm_prev = None
    tp_prev = [None, None]
    for kvh in range(2):
        w = [ag_kv, qf_done] if kvh == 0 else [kb_prev]
        if kvh == 0:
            w.append(S.barrier_token("tt"))
        for cc in range(4):
            kld = S.dma("sync",
                        ar1b[:, kvh * 4096:(kvh + 1) * 4096].rearrange(
                            "p (tt cc r) -> p tt cc r", tt=8, cc=4)[
                            :, :, cc, :],
                        kvsrc[cc, :, :, kvh * 128:(kvh + 1) * 128].transpose(
                            [1, 0, 2]), waits=(w if cc == 0 else []),
                        sem="d_kf")
        kb_prev = kld
        if cut == "H1" and kvh == 0:
            P["tok_a1_added"] = S.barrier_token("d_kf")
            return
        for kt in range(32):
            sl = kt % 2
            tt_, cc_ = kt // 4, kt % 4
            src = ar1b[:, kvh * 4096 + (tt_ * 4 + cc_) * 128:
                       kvh * 4096 + (tt_ * 4 + cc_ + 1) * 128]
            w2 = [kld]
            if tp_prev[sl]:
                w2.append(tp_prev[sl])
            tbank = P["pbt"] if sl == 0 else P["pbt2"]
            tp = S.emit("tensor", lambda e, tbank=tbank, src=src: e.transpose(
                tbank[:, 0:128], src, identb), waits=w2, inc="tt")
            kfm_prev = S.emit("vector", lambda e, kvh=kvh, kt=kt,
                              tbank=tbank: e.tensor_copy(
                out=kfm[:, kvh * 4096 + kt * 128:kvh * 4096 + (kt + 1) * 128],
                in_=tbank[:, 0:128]),
                waits=[tp] + ([kfm_prev] if kfm_prev else []), inc="tv")
            tp_prev[sl] = kfm_prev
    kfm_done = kfm_prev
    if cut_here("H2"):
        return
    for cc in range(4):
        vload = S.dma("sync",
                      ar4b.rearrange("p (tt cc c) -> p tt cc c",
                                     tt=8, cc=4)[:, :, cc, :],
                      kvsrc[cc, :, :, 256:512].transpose([1, 0, 2]),
                      waits=([ag_kv, qf_done] if cc == 0 else []),
                      sem="d_kf")
    if cut_here("H"):
        return

    qbig = [ar1b[:, 0:8192], ar1b[:, 8192:16384]]
    pts = [smb[:, 96 + i * 512:608 + i * 512] for i in range(4)]
    rdenb = smalls[:, 1072:1584]
    rden = P["prod_sb"][0:1, 0:512]
    m_my = None
    scale_s = float(128) ** -0.5
    s_free = [None, None, None]
    pt_free = [None, None, None, None]
    o_free = [None, None]
    d_free = [None, None]
    rden_free = None
    rdenb_free = None
    pb4_free = [None, None]
    exp_prev = None
    te_last = None
    qld = [None, None]
    for jg in range(2):
        w = [qf_done, kfm_done]
        qld[jg] = S.dma("sync", qbig[jg].rearrange("p (h c) -> p h c", h=16),
                        P["h_qf"][:, jg * 512:(jg + 1) * 512].rearrange(
                            "(h p) c -> h p c", p=128).transpose([1, 0, 2]),
                        waits=w, sem="d_qr")
        tiles = [(kb, kt) for kb in range(jg * 4 + 4) for kt in range(4)]
        for h in range(16):
            kvh = h // 8
            hp = h % 2
            obank = pb[2 + 3 * hp]
            dband = (pb[3] if hp == 0 else
                     P["pbt"].bitcast(F32))[0:1, 0:512]
            bcbank = P["pbt2"].bitcast(F32)[:, 0:512]
            qrhs = qbig[jg][:, h * 512:(h + 1) * 512]
            omm = None
            dmm = None
            pt_ready = {}

            def emit_s(idx):
                nonlocal te_last
                kb, kt = tiles[idx]
                par = idx % 3
                sbank = pb[par if par < 2 else 4]
                gk = kb * 4 + kt
                w2 = [qld[jg]]
                if s_free[par]:
                    w2.append(s_free[par])
                if jg == 0 and h == 0 and idx == 0:
                    w2.append(vload)
                tok = S.emit("tensor", lambda e, kvh=kvh, gk=gk,
                             sbank=sbank, qrhs=qrhs: e.matmul(
                    sbank[:, 0:512],
                    kfm[:, kvh * 4096 + gk * 128:kvh * 4096 + (gk + 1) * 128],
                    qrhs, start=True, stop=True), waits=w2, inc="tt")
                te_last = tok
                return tok

            def emit_exp(idx, sm):
                nonlocal exp_prev
                kb, kt = tiles[idx]
                par = idx % 3
                sbank = pb[par if par < 2 else 4]
                pp = idx % 4
                wv = [sm]
                if pt_free[pp]:
                    wv.append(pt_free[pp])
                if exp_prev:
                    wv.append(exp_prev)
                ex = S.emit("scalar", lambda e, sbank=sbank, pp=pp:
                            e.activation(
                    out=pts[pp], in_=sbank[:, 0:512], func=AF.Exp,
                    scale=scale_s), waits=wv, inc="tsc")
                exp_prev = ex
                s_free[par] = ex
                kbb = kb - jg * 4
                if kbb >= 0:
                    if kbb > 0:
                        ex = S.emit("vector", lambda e, pp=pp, kbb=kbb:
                                    e.memset(pts[pp][:, 0:kbb * 128], 0.0),
                                    waits=[ex], inc="tv")
                    ex = S.emit("vector", lambda e, pp=pp, kbb=kbb, kt=kt:
                                e.tensor_tensor(
                        out=pts[pp][:, kbb * 128:(kbb + 1) * 128],
                        in0=pts[pp][:, kbb * 128:(kbb + 1) * 128],
                        in1=dmask_sb[:, kt * 128:(kt + 1) * 128],
                        op=ALU.mult), waits=[ex, tk_dm], inc="tv")
                return ex

            def emit_ov(idx):
                nonlocal omm, dmm, te_last
                kb, kt = tiles[idx]
                pp = idx % 4
                gk = kb * 4 + kt
                first = idx == 0
                last = idx == len(tiles) - 1
                w2 = [pt_ready[idx]]
                if first and o_free[hp]:
                    w2.append(o_free[hp])
                omm = S.emit("tensor", lambda e, gk=gk, pp=pp, kvh=kvh,
                             obank=obank, first=first, last=last: e.matmul(
                    obank[:, 0:512],
                    ar4b[:, gk * 256 + kvh * 128:gk * 256 + (kvh + 1) * 128],
                    pts[pp][:, 0:512], start=first, stop=last),
                    waits=w2, inc="tt")
                w2 = [pt_ready[idx]]
                if first and d_free[hp]:
                    w2.append(d_free[hp])
                dmm = S.emit("tensor", lambda e, pp=pp, dband=dband,
                             first=first, last=last: e.matmul(
                    dband, P["onescol_b"], pts[pp][:, 0:512],
                    start=first, stop=last), waits=w2, inc="tt")
                te_last = dmm
                pt_free[pp] = dmm

            n_t = len(tiles)
            stoks = {0: emit_s(0)}
            if n_t > 1:
                stoks[1] = emit_s(1)
            for idx in range(2, n_t):
                pt_ready[idx - 2] = emit_exp(idx - 2, stoks[idx - 2])
                stoks[idx] = emit_s(idx)
                emit_ov(idx - 2)
            for idx in range(max(n_t - 2, 0), n_t):
                pt_ready[idx] = emit_exp(idx, stoks[idx])
                emit_ov(idx)
            wv = [dmm]
            if rden_free:
                wv.append(rden_free)
            rc = S.emit("vector", lambda e, dband=dband: e.reciprocal(
                out=rden, in_=dband), waits=wv, inc="tv")
            d_free[hp] = rc
            wv = [rc, P["tok_constsg"]]
            if pb4_free[0]:
                wv.append(pb4_free[0])
            bcmm = S.emit("tensor", lambda e, bcbank=bcbank: e.matmul(
                bcbank, P["onesrow_b"], rden, start=True, stop=True),
                waits=wv, inc="tt")
            te_last = bcmm
            rden_free = bcmm
            wv = [bcmm]
            if rdenb_free:
                wv.append(rdenb_free)
            rbe = S.emit("vector", lambda e, bcbank=bcbank: e.tensor_copy(
                out=rdenb, in_=bcbank), waits=wv, inc="tv")
            pb4_free[0] = rbe
            on = S.emit("vector", lambda e, h=h, jg=jg, obank=obank:
                        e.tensor_mul(
                out=ar2b[:, (h * 8 + jg * 4) * 128:
                         (h * 8 + jg * 4 + 4) * 128],
                in0=obank[:, 0:512], in1=rdenb), waits=[rbe, omm],
                inc="tv")
            o_free[hp] = on
            rdenb_free = on
    if cut_here("I"):
        return
    flash_dve_done = S.barrier_token("tv")
    flash_te_done = S.barrier_token("tt")

    woslab = [ar1b[:, 0:512], ar1b[:, 512:1024]]
    wo_slab_free = [None, None]
    evac_last = None
    first = True
    for dq in range(4):
        mm = None
        for dc in range(16):
            sl = dc % 2
            w = [P["tok_wog"]]
            if first:
                w += [flash_dve_done, flash_te_done]
                first = False
            if wo_slab_free[sl]:
                w.append(wo_slab_free[sl])
            lt = S.dma("sync", woslab[sl],
                       P["h_woc"][dc * 128:(dc + 1) * 128,
                                  dq * 512:(dq + 1) * 512],
                       waits=w, sem="d_wo")
            for t in range(NT):
                w2 = []
                if t == 0:
                    w2 = [lt]
                    if dc == 0 and evac_last:
                        w2.append(evac_last)
                mm = S.emit("tensor", lambda e, dc=dc, t=t, sl=sl: e.matmul(
                    banks[t][:, :],
                    ar2b[:, (dc * 8 + t) * 128:(dc * 8 + t + 1) * 128],
                    woslab[sl], start=(dc == 0), stop=(dc == 15)),
                    waits=w2, inc="tt")
            wo_slab_free[sl] = mm
        eprev = None
        for t in range(NT):
            xsl = x_sb[:, t * D + dq * 512:t * D + (dq + 1) * 512]
            eprev = S.emit("vector", lambda e, t=t, xsl=xsl: e.tensor_tensor(
                out=xsl, in0=banks[t][:, :], in1=xsl, op=ALU.add),
                waits=[mm] + ([eprev] if eprev else []), inc="tv")
        evac_last = eprev
    tok_wo_mm = S.barrier_token("tt")
    P["tok_a1_added"] = S.barrier_token("tv")
    if DBG_STAGE == 4:
        return

    wsl_bufs = [ar56b[:, 0:2048], ar56b[:, 2048:4096]]
    dbou = [ar4b[:, 0:1024], ar4b[:, 1024:2048]]
    k_slab_free = [None, None]
    k_dbou_free = [None, None]
    gelu_prev = None
    kmm_last = {}
    for ft in range(64):
        sl = ft % 2
        w = [P["tok_W0g"], flash_te_done, flash_dve_done]
        if k_slab_free[sl]:
            w.append(k_slab_free[sl])
        ksrc = P["h_W0g"][:, ft * 128:(ft + 1) * 128].rearrange(
            "(dc p) f -> dc p f", p=128)
        kdst = wsl_bufs[sl].rearrange("p (dc f) -> p dc f", dc=16)
        lt_a = S.dma("sync", kdst[:, 0:8], ksrc[0:8].transpose([1, 0, 2]),
                     waits=w, sem="d_w0")
        lt = S.dma("scalar", kdst[:, 8:16], ksrc[8:16].transpose([1, 0, 2]),
                   waits=w, sem="d_w0s")
        mm = None
        for half in range(2):
            bank = pb[sl * 2 + half]
            for dc in range(16):
                w2 = []
                if dc == 0:
                    w2 = [lt_a, lt]
                    if kmm_last.get((sl, half)):
                        w2.append(kmm_last[(sl, half)])
                    else:
                        w2.append(P["tok_a1_added"])
                mm = S.emit("tensor", lambda e, sl=sl, half=half, dc=dc,
                            bank=bank: e.matmul(
                    bank[:, :], wsl_bufs[sl][:, dc * 128:(dc + 1) * 128],
                    ar3b[:, dc * 1024 + half * 512:
                         dc * 1024 + (half + 1) * 512],
                    start=(dc == 0), stop=(dc == 15)), waits=w2, inc="tt")
            wg = [mm, P["tok_consts"]]
            if gelu_prev:
                wg.append(gelu_prev)
            if k_dbou_free[sl] and half == 0:
                wg.append(k_dbou_free[sl])
            gelu_prev = S.emit("scalar", lambda e, sl=sl, half=half, ft=ft,
                               bank=bank: e.activation(
                out=dbou[sl][:, half * 512:(half + 1) * 512], in_=bank[:, :],
                func=AF.Gelu, bias=P["b0c_sb"][:, ft:ft + 1], scale=1.0),
                waits=wg, inc="tsc")
            kmm_last[(sl, half)] = gelu_prev
        k_dbou_free[sl] = S.dma("sync", P["h_DqT"][ft * 128:(ft + 1) * 128, :],
                                dbou[sl], waits=[gelu_prev], sem="d_d1")
    dqt_done = S.barrier_token("d_d1")
    gelu_done = S.barrier_token("tsc")

    w1b = [ar2b[:, 0:512], ar2b[:, 512:1024]]
    dqb = [ar2b[:, 1024 + i * 1024:2048 + i * 1024] for i in range(2)]
    b1bc = ar4[:, 0:2048]
    tb1 = S.dma("gpsimd", b1bc,
                P["vrows"][5:6, 0:2048].broadcast_to([128, 2048]),
                waits=[dqt_done, gelu_done], sem="d_fg")
    slot_mm = [None, None]
    prev_dq_ev = P["tok_a1_added"]
    for dq in range(4):
        mm = None
        for fc in range(64):
            sl = fc % 2
            w = [dqt_done, P["tok_W1g"], tok_wo_mm]
            if slot_mm[sl] is not None:
                w.append(slot_mm[sl])
            qa, sa = ("sync", "d_f2b") if fc % 2 == 0 else \
                ("scalar", "d_f2bs")
            qb, sb2 = ("scalar", "d_f2bs") if fc % 2 == 0 else \
                ("sync", "d_f2b")
            wt = S.dma(qa, w1b[sl],
                       P["h_W1g"][fc * 128:(fc + 1) * 128,
                                  dq * 512:(dq + 1) * 512],
                       waits=w, sem=sa)
            ht = S.dma(qb, dqb[sl],
                       P["h_DqT"][fc * 128:(fc + 1) * 128, :],
                       waits=w, sem=sb2)
            for t in range(NT):
                w2 = []
                if t == 0:
                    w2 = [wt, ht]
                    if fc == 0:
                        w2 += [prev_dq_ev, gelu_done]
                mm = S.emit("tensor", lambda e, sl=sl, t=t, fc=fc: e.matmul(
                    banks[t][:, :], dqb[sl][:, t * 128:(t + 1) * 128],
                    w1b[sl], start=(fc == 0), stop=(fc == 63)),
                    waits=w2, inc="tt")
            slot_mm[sl] = mm
        eprev = None
        for t in range(NT):
            xsl = x_sb[:, t * D + dq * 512:t * D + (dq + 1) * 512]
            e1 = S.emit("vector", lambda e, t=t, dq=dq: e.tensor_tensor(
                out=ar4[:, 2048:2560], in0=banks[t][:, :],
                in1=b1bc[:, dq * 512:(dq + 1) * 512], op=ALU.add),
                waits=[mm, tb1, P["tok_a1_added"]] +
                ([eprev] if eprev else []), inc="tv")
            eprev = S.emit("vector", lambda e, xsl=xsl: e.tensor_add(
                out=xsl, in0=ar4[:, 2048:2560], in1=xsl), waits=[e1],
                inc="tv")
        prev_dq_ev = eprev
    P["tok_final"] = S.barrier_token("tv")


def emit_debug_out(S, P):
    out_t, x_sb = P["out_t"], P["x_sb"]
    if DBG_STAGE == 0:
        xn_sb = P["xn_sb"]
        for t in range(NT):
            S.dma("gpsimd", out_t[t * 128:(t + 1) * 128, :],
                  xn_sb[:, t * D:(t + 1) * D], waits=[P["tok_xn"]], sem="d_out")
        S.emit("gpsimd", lambda e: None, waits=[S.barrier_token("d_out")],
               inc="done")
        return
    if DBG_STAGE == 1:
        emit_attn0_gather(S, P, to_out=True)
        S.emit("gpsimd", lambda e: None, waits=[S.barrier_token("d_out")],
               inc="done")
        return
    if DBG_STAGE in (2, 3, 4, 99):
        tok = {2: P.get("tok_x1p"), 3: P.get("tok_x1"),
               4: P.get("tok_a1_added"), 99: P.get("tok_final")}[DBG_STAGE]
        for t in range(NT):
            S.dma("sync", P["out_t"][t * 128:(t + 1) * 128, :],
                  P["x_sb"][:, t * D:(t + 1) * D], waits=[tok], sem="d_out")
        S.emit("sync", lambda e: None, waits=[S.barrier_token("d_out")],
               inc="done")
        return
    raise NotImplementedError(f"stage {DBG_STAGE}")


def emit_attn0_gather(S, P, to_out=False, extra_waits=(), consume_fn=None):
    h_tok2g = P["h_tok2g"]
    out_t = P["out_t"]
    recv = P["ar4b"]
    rows_view = h_tok2g.rearrange("a (b c) -> (a b) c", b=64, c=2048)
    gi = P["gidx_sb"]
    res = []
    slot_done = [None] * 4
    for t in range(NT):
        dst = recv[:, (t % 4) * 2048:(t % 4) * 2048 + 2048]
        w = [P["tok_ag"], P["tok_gidx"]] + list(extra_waits)
        if slot_done[t % 4]:
            w.append(slot_done[t % 4])
        w.append(S.barrier_token("d_gb"))
        tk = S.emit("gpsimd", lambda e, t=t, dst=dst: e.indirect_dma_start(
            out=dst, out_offset=None, in_=rows_view,
            in_offset=IndirectOffsetOnAxis(gi[:, t:t + 1], 0)),
            waits=w, inc="d_gb", inc_amt=16)
        if to_out:
            slot_done[t % 4] = S.dma("gpsimd",
                                     out_t[t * 128:(t + 1) * 128, :], dst,
                                     waits=[tk], sem="d_out")
        elif consume_fn is not None:
            slot_done[t % 4] = consume_fn(t, dst, tk)
        res.append((dst, tk))
    return res



def host_prep(inputs):
    x = np.ascontiguousarray(inputs["L0_attn_x"].reshape(B * S, D))
    w1T = np.ascontiguousarray(inputs["L0_ffn_w1"].T)
    w2T = np.ascontiguousarray(inputs["L0_ffn_w2"].T)
    W0T = np.ascontiguousarray(inputs["L1_ffn_W0"].T)
    W1T = np.ascontiguousarray(inputs["L1_ffn_W1"].T)
    onesrow = np.ones((1, 128), np.float32)
    wqT = np.ascontiguousarray(inputs["L1_attn_wq"].T)
    wkvT = np.ascontiguousarray(
        np.concatenate([inputs["L1_attn_wk"].T, inputs["L1_attn_wv"].T], axis=1))
    woT = np.ascontiguousarray(inputs["L1_attn_wo"].T)

    qkv_w = inputs["L0_attn_qkv_w"]
    qkvbd3 = np.zeros((3, 128, 128), np.float32)
    for g in range(3):
        for d8 in range(8):
            qkvbd3[g, d8 * 16:(d8 + 1) * 16, d8 * 16:(d8 + 1) * 16] = \
                qkv_w[:, g * 16:(g + 1) * 16]
    qkvbd = np.ascontiguousarray(qkvbd3.transpose(1, 0, 2).reshape(128, 3 * 128))

    proj_w = inputs["L0_attn_proj_w"]
    projbd4 = np.zeros((4, 128, 128), np.float32)
    for br in range(2):
        for oct_ in range(2):
            m = projbd4[br * 2 + oct_]
            for c in range(8):
                for s8 in range(8):
                    s = 8 * oct_ + s8
                    for o in range(16):
                        m[c * 16 + s, o * 8 + s8] = proj_w[c + 8 * br, o]

    projbd = np.ascontiguousarray(projbd4.transpose(1, 0, 2).reshape(128, 4 * 128))
    ident = np.eye(128, dtype=np.float32)

    pconst = np.zeros((128, 32), np.float32)
    cw0 = inputs["L0_attn_conv_w0"]
    cw1 = inputs["L0_attn_conv_w1"]
    cb0 = inputs["L0_attn_conv_b0"]
    cb1 = inputs["L0_attn_conv_b1"]
    for c in range(8):
        for s in range(16):
            p = c * 16 + s
            pconst[p, 0:3] = cw0[c, 0, 1, 0:3]
            pconst[p, 3:6] = cw1[c, 0, 0:3, 1]
            pconst[p, 6] = cb0[c]
            pconst[p, 7] = cb1[c]
    ln_w = inputs["L0_attn_ln_w"]
    ln_b = inputs["L0_attn_ln_b"]
    proj_b = inputs["L0_attn_proj_b"]
    for o in range(16):
        for so in range(8):
            p = o * 8 + so
            pconst[p, 8] = ln_w[o]
            pconst[p, 9] = ln_b[o]
            pconst[p, 10] = proj_b[o]
    pconst[:, 11] = 1.0
    pconst[:, 16] = EPS

    vrows = np.zeros((10, FF), np.float32)
    vrows[0] = inputs["L0_ffn_b1"]
    vrows[1] = inputs["L0_ffn_gamma"]
    vrows[2] = inputs["L0_ffn_beta"]
    vrows[3, :D] = inputs["L0_ffn_b2"]
    vrows[4] = inputs["L1_ffn_bias0"]
    vrows[5, :D] = inputs["L1_ffn_bias1"]
    vrows[6, :D] = inputs["L1_attn_bq"]
    vrows[7, :256] = inputs["L1_attn_bk"]
    vrows[7, 256:512] = inputs["L1_attn_bv"]

    cmaskv = np.where(np.tril(np.ones((128, 128), bool)), 0.0, -30000.0
                      ).astype(np.float32)

    gselv = np.zeros((128, 16), np.float32)
    for c in range(8):
        for s_ in range(16):
            gselv[c * 16 + s_, s_] = 1.0
    ebcv = np.zeros((16, 128), np.float32)
    for c in range(8):
        for s_ in range(16):
            ebcv[s_, c * 16 + s_] = 1.0
    i16v = np.eye(16, dtype=np.float32)
    glnv = np.zeros((128, 8), np.float32)
    for o in range(16):
        for so in range(8):
            glnv[o * 8 + so, so] = 1.0 / 16.0
    ebc8v = np.zeros((8, 128), np.float32)
    for o in range(16):
        for so in range(8):
            ebc8v[so, o * 8 + so] = 1.0

    in_maps = []
    for p in range(8):
        rows = core_rows(p)
        postab = np.zeros((128, NT), np.float32)
        gidx = np.zeros((128, NT), np.int32)
        for t in range(NT):
            postab[:, t] = (rows[t * 128:(t + 1) * 128] % S).astype(np.float32)
            g = core_tiles(p)[t]
            for pp in range(128):
                h_b = g * 128 + pp
                c = h_b // 256
                tval = h_b % 256
                g_s = tval // 8
                snd = g_s % 4
                j_s = g_s // 4
                gidx[pp, t] = (snd * 16 + c) * 64 + j_s * 8 + (tval % 8)
        inv = (10000.0 ** (-np.arange(64, dtype=np.float64) / 64.0))
        cstab = np.zeros((128, NT * 128), np.float32)
        for t in range(NT):
            pos = (rows[t * 128:(t + 1) * 128] % S).astype(np.float64)
            ang = pos[:, None] * inv[None, :]
            cstab[:, t * 128:t * 128 + 64] = np.cos(ang)
            cstab[:, t * 128 + 64:t * 128 + 128] = np.sin(ang)
        dmaskv = np.zeros((128, 512), np.float32)
        m = p % 4
        rr, cc2 = np.arange(128)[:, None], np.arange(128)[None, :]
        for kt in range(4):
            if kt < m:
                dmaskv[:, kt * 128:(kt + 1) * 128] = 1.0
            elif kt == m:
                dmaskv[:, kt * 128:(kt + 1) * 128] = (rr <= cc2)
        b0colsv = np.ascontiguousarray(
            inputs["L1_ffn_bias0"].reshape(64, 128).T.astype(np.float32))
        im = {
            "x_in": np.ascontiguousarray(x[rows]),
            "w1s": np.ascontiguousarray(w1T[256 * p:256 * (p + 1)]),
            "w2s": np.ascontiguousarray(w2T[1024 * p:1024 * (p + 1)]),
            "W0s": np.ascontiguousarray(W0T[256 * p:256 * (p + 1)]),
            "W1s": np.ascontiguousarray(W1T[1024 * p:1024 * (p + 1)]),
            "onesrow": onesrow,
            "wqs": np.ascontiguousarray(wqT[256 * p:256 * (p + 1)]),
            "wkvs": np.ascontiguousarray(wkvT[256 * p:256 * (p + 1)]),
            "wos": np.ascontiguousarray(woT[256 * p:256 * (p + 1)]),
            "cstab_in": cstab.astype(BF),
            "dmask_in": dmaskv.astype(BF),
            "b0cols": b0colsv,
            "qkvbd": qkvbd, "projbd": projbd, "ident_in": ident,
            "pconst": pconst, "vrows": vrows, "cmask": cmaskv,
            "postab": postab, "gidx": gidx,
            "gsel_in": gselv, "ebc_in": ebcv, "i16_in": i16v,
            "glnsel_in": glnv, "ebc8_in": ebc8v,
        }
        in_maps.append(im)
    return in_maps


def kernel(**inputs):
    in_maps = host_prep(inputs)
    nc = build_program()
    res = run_bass_kernel_spmd(nc, in_maps, core_ids=list(range(8)))
    out = np.zeros((B * S, D), np.float32)
    for p in range(8):
        out[core_rows(p)] = np.asarray(res.results[p]["out"])
    return out.reshape(B, S, D)


# version 11
# speedup vs baseline: 1.1757x; 1.0231x over previous
import numpy as np
import ml_dtypes
from contextlib import ExitStack

import concourse.bass as bass
import concourse.mybir as mybir
from concourse.bass_utils import run_bass_kernel_spmd
from concourse.bass_core_rust import IndirectOffsetOnAxis

F32 = mybir.dt.float32
BF16 = mybir.dt.bfloat16
I32 = mybir.dt.int32
AF = mybir.ActivationFunctionType
ALU = mybir.AluOpType
AX = mybir.AxisListType
BF = ml_dtypes.bfloat16

B = 2
S = 4096
D = 2048
FF = 8192
RPC = 1024
NT = 8
TPC = RPC * 128
EPS = 1e-5
G4 = [[0, 1, 2, 3], [4, 5, 6, 7]]
G8 = [[0, 1, 2, 3, 4, 5, 6, 7]]

import os
DBG_STAGE = int(os.environ.get("KBD_STAGE", "99"))

ENGS = ("sync", "gpsimd", "vector", "scalar", "tensor")


def core_tiles(p):
    return [4 * j + (p % 4) for j in range(NT)]


def core_rows(p):
    b = p // 4
    out = []
    for g in core_tiles(p):
        st = b * S + g * 128
        out.extend(range(st, st + 128))
    return np.array(out)


class Sched:
    def __init__(self):
        self.ops = {e: [] for e in ENGS}
        self.semnames = []
        self.counts = {}

    def sem(self, name):
        if name not in self.counts:
            self.semnames.append(name)
            self.counts[name] = 0
        return name

    def emit(self, eng, fn, waits=(), inc=None, inc_amt=1):
        tok = None
        if inc is not None:
            self.sem(inc)
            self.counts[inc] += inc_amt
            tok = (inc, self.counts[inc])
        self.ops[eng].append((fn, [tuple(w) for w in waits], inc, inc_amt))
        return tok

    def dma(self, queue, out, in_, waits=(), sem=None, **kw):
        if sem is None:
            sem = "d_" + queue
        self.sem(sem)
        gate = [(sem, self.counts[sem])] if self.counts[sem] else []
        return self.emit(queue, lambda e: e.dma_start(out=out, in_=in_, **kw),
                         waits=list(waits) + gate, inc=sem, inc_amt=16)

    def barrier_token(self, sem):
        return (sem, self.counts.get(sem, 0))


def replay(block, sch, sems):
    def mk(eng_name):
        def body(eng):
            for fn, waits, inc, amt in sch.ops[eng_name]:
                for s, v in waits:
                    if v > 0:
                        eng.wait_ge(sems[s], v)
                ins = fn(eng)
                if inc is not None:
                    if ins is None:
                        eng.nop().then_inc(sems[inc], amt)
                    else:
                        ins.then_inc(sems[inc], amt)
        return body
    block.sync(mk("sync"))
    block.gpsimd(mk("gpsimd"))
    block.vector(mk("vector"))
    block.scalar(mk("scalar"))
    block.tensor(mk("tensor"))



def build_program():
    nc = bass.Bass("TRN2", target_bir_lowering=False, debug=False)
    S_ = Sched()
    P = {}

    def din(name, shape, dt=F32):
        return nc.dram_tensor(name, shape, dt, kind="ExternalInput").ap()

    def dint(name, shape, dt=F32, **kw):
        return nc.dram_tensor(name, shape, dt, kind="Internal", **kw).ap()

    x_in = din("x_in", [RPC, D])
    out_t = nc.dram_tensor("out", [RPC, D], F32, kind="ExternalOutput").ap()
    w1s = din("w1s", [256, FF])
    w2s = din("w2s", [1024, D])
    W0s = din("W0s", [256, FF])
    W1s = din("W1s", [1024, D])
    onesrow = din("onesrow", [1, 128])
    wqs = din("wqs", [256, D])
    wkvs = din("wkvs", [256, 512])
    wos = din("wos", [256, D])
    cstab_in = din("cstab_in", [128, NT * 128], BF16)
    dmask_in = din("dmask_in", [128, 512], BF16)
    b0cols = din("b0cols", [128, 64])
    qkvbd = din("qkvbd", [128, 3 * 128])
    projbd = din("projbd", [128, 4 * 128])
    ident_in = din("ident_in", [128, 128])
    pconst = din("pconst", [128, 32])
    vrows = din("vrows", [10, FF])
    cmask = din("cmask", [128, 128])
    postab = din("postab", [128, NT])
    gsel_in = din("gsel_in", [128, 16])
    ebc_in = din("ebc_in", [16, 128])
    i16_in = din("i16_in", [16, 16])
    glnsel_in = din("glnsel_in", [128, 8])
    ebc8_in = din("ebc8_in", [8, 128])
    gidx = din("gidx", [128, NT], I32)

    w1slab = dint("w1slab", [256, FF])
    h_w1q = dint("h_w1q", [256, FF], BF16)
    h_w1g = dint("h_w1g", [D, FF], BF16)
    h_w2q = dint("h_w2q", [1024, D], BF16)
    h_w2g = dint("h_w2g", [FF, D], BF16)
    h_W1c = dint("h_W1c", [1024, D], BF16)
    h_W1g = dint("h_W1g", [FF, D], BF16)
    h_stat_i = dint("h_stat_i", [1, 2])
    h_stat_o = dint("h_stat_o", [1, 2])
    h_qkvt = dint("h_qkvt", [NT, 16, 128, 384], BF16)
    h_Tt = dint("h_Tt", [NT, 16, 128, 128], BF16)
    h_tok2 = dint("h_tok2", [16, TPC], BF16)
    h_tok2g = dint("h_tok2g", [4 * 16, TPC], BF16)
    h_kv = dint("h_kv", [RPC, 512], BF16)
    h_kvg = dint("h_kvg", [4 * RPC, 512], BF16)
    h_h = dint("h_h", [RPC, FF], BF16)
    h_hqT = dint("h_hqT", [FF, RPC], BF16)
    h_x1 = dint("h_x1", [RPC, D])
    h_qf = dint("h_qf", [16 * 128, RPC], BF16)
    h_DqT = dint("h_DqT", [FF, RPC], BF16)
    h_dbg = dint("h_dbg", [RPC, D])
    h_wqcs = dint("h_wqcs", [256, D], BF16)
    h_wqc = dint("h_wqc", [D, D], BF16)
    h_wkvcs = dint("h_wkvcs", [256, 512], BF16)
    h_wkvc = dint("h_wkvc", [D, 512], BF16)
    h_wocs = dint("h_wocs", [256, D], BF16)
    h_woc = dint("h_woc", [D, D], BF16)
    h_W0cs = dint("h_W0cs", [256, FF], BF16)
    h_W0g = dint("h_W0g", [D, FF], BF16)

    ctx = ExitStack()
    with ctx:
        def sb(name, shape, dt=F32):
            return ctx.enter_context(nc.sbuf_tensor(name, shape, dt)).ap()

        def psum(name, shape, dt=F32):
            return ctx.enter_context(nc.psum_tensor(name, shape, dt)).ap()

        x_sb = sb("x_sb", [128, NT * D])
        ar1 = sb("ar1", [128, FF])
        ar2 = sb("ar2", [128, FF])
        ar3 = sb("ar3", [128, FF])
        ar4 = sb("ar4", [128, 4096])
        ar56 = sb("ar56", [128, 4096])
        smalls = sb("smalls", [128, 1840])
        prod_f = sb("prod_f", [128, 256])
        pv_f = sb("pv_f", [128, 256])
        pconst_sb = sb("pconst_sb", [128, 32])
        identb = sb("identb", [128, 128], BF16)
        qkvbd_sb = sb("qkvbd_sb", [128, 3 * 128], BF16)
        projbd_sb = sb("projbd_sb", [128, 4 * 128], BF16)
        cmask_sb = sb("cmask_sb", [128, 128], BF16)
        gsel_sb = sb("gsel_sb", [128, 16], BF16)
        ebc_sb = sb("ebc_sb", [16, 128], BF16)
        i16_sb = sb("i16_sb", [16, 16], BF16)
        glnsel_sb = sb("glnsel_sb", [128, 8], BF16)
        ebc8_sb = sb("ebc8_sb", [8, 128], BF16)
        gidx_sb = sb("gidx_sb", [128, NT], I32)
        onesrow_sb = sb("onesrow_sb", [1, 128])
        onesrow_b = sb("onesrow_b", [1, 128], BF16)
        onescol_b = sb("onescol_b", [128, 1], BF16)
        b0c_sb = sb("b0c_sb", [128, 64])

        pb = [psum(f"pb{i}", [128, 512], F32) for i in range(6)]
        pbt = psum("pbt", [128, 1024], BF16)
        pbt2 = psum("pbt2", [128, 1024], BF16)

        xn_sb = ar1.bitcast(BF16)
        pk = ar2.bitcast(BF16)
        pk2 = ar3.bitcast(BF16)
        ar4b = ar4.bitcast(BF16)
        ar56b = ar56.bitcast(BF16)
        ar56i = ar56.bitcast(I32)
        prod_sb = prod_f.bitcast(BF16)
        pv_sb = pv_f.bitcast(BF16)
        smb = smalls.bitcast(BF16)
        smb_base = 2144

        P.update(locals())

        emit_phase0(S_, P)
        if DBG_STAGE >= 2:
            emit_wprep_stats(S_, P)
            P["attn_midhook"] = lambda: emit_wprep_quant(S_, P)
        if DBG_STAGE >= 1:
            emit_l0attn(S_, P)
        if DBG_STAGE >= 2:
            emit_wprep_ags(S_, P)
        if DBG_STAGE >= 2:
            emit_l0ffn(S_, P)
        if DBG_STAGE >= 3:
            emit_x1(S_, P)
        if DBG_STAGE >= 4:
            emit_l1(S_, P)
        emit_debug_out(S_, P)

        sems = {}
        for name in S_.semnames:
            sems[name] = ctx.enter_context(nc.semaphore(name))
        block = ctx.enter_context(nc.Block())
        with nc.allow_low_precision(reason="bf16 pipeline; tolerance spec "
                                    "permits bf16 matmuls + bf16 stores"):
            replay(block, S_, sems)
    return nc



def emit_phase0(S, P):
    x_in, x_sb = P["x_in"], P["x_sb"]
    ar1, smalls = P["ar1"], P["smalls"]
    xn_sb = P["xn_sb"]

    S.dma("sync", P["pconst_sb"], P["pconst"], sem="d_cn")
    S.dma("gpsimd", P["identb"], P["ident_in"])
    S.dma("gpsimd", P["qkvbd_sb"], P["qkvbd"])
    S.dma("gpsimd", P["projbd_sb"], P["projbd"])
    S.dma("gpsimd", P["cmask_sb"], P["cmask"])
    S.dma("gpsimd", P["gsel_sb"], P["gsel_in"])
    S.dma("gpsimd", P["ebc_sb"], P["ebc_in"])
    S.dma("gpsimd", P["i16_sb"], P["i16_in"])
    S.dma("gpsimd", P["glnsel_sb"], P["glnsel_in"])
    S.dma("gpsimd", P["ebc8_sb"], P["ebc8_in"])
    P["tok_gidx"] = S.dma("gpsimd", P["gidx_sb"], P["gidx"])
    S.dma("gpsimd", P["onesrow_b"], P["onesrow"])
    S.dma("gpsimd", P["onescol_b"], P["onesrow"].transpose([1, 0]))
    S.dma("sync", P["onesrow_sb"], P["onesrow"], sem="d_cn")
    S.dma("sync", P["b0c_sb"], P["b0cols"], sem="d_cn")
    P["tok_consts"] = S.barrier_token("d_cn")
    P["tok_constsg"] = S.barrier_token("d_gpsimd")

    for t in range(NT):
        S.dma("sync", x_sb[:, t * D:(t + 1) * D], x_in[t * 128:(t + 1) * 128, :])
    tok_x = S.barrier_token("d_sync")
    P["tok_x"] = tok_x

    mu = P["smalls"][:, 0:1]
    esq = P["smalls"][:, 1:2]
    var = P["smalls"][:, 2:3]
    std = P["smalls"][:, 3:4]
    rstd = P["smalls"][:, 4:5]
    epsc = P["pconst_sb"][:, 16:17]
    ftmp = P["ar3"]
    sc5 = P["smalls"][:, 5:6]
    t8 = None
    t7 = None
    for t in range(NT):
        xt = x_sb[:, t * D:(t + 1) * D]
        head = [tok_x] + ([t8] if t8 else [])
        t1 = S.emit("vector", lambda e, xt=xt: e.tensor_reduce(
            out=mu, in_=xt, axis=AX.X, op=ALU.add), waits=head, inc="tv")
        t2 = S.emit("scalar", lambda e, xt=xt: e.activation(
            out=ftmp[:, 0:D], in_=xt, func=AF.Square, scale=float(D) ** -0.5,
            accum_out=esq), waits=head, inc="tsc")
        t3 = S.emit("vector", lambda e: e.tensor_scalar(
            out=mu, in0=mu, scalar1=1.0 / D, scalar2=None, op0=ALU.mult),
            waits=[t1, t2], inc="tv")
        t5 = S.emit("vector", lambda e: e.tensor_mul(
            out=sc5, in0=mu, in1=mu), waits=[t3], inc="tv")
        t5b = S.emit("vector", lambda e: e.tensor_sub(out=var, in0=esq, in1=sc5),
                     waits=[t5], inc="tv")
        t6 = S.emit("scalar", lambda e: e.activation(
            out=std, in_=var, func=AF.Sqrt, bias=epsc, scale=1.0),
            waits=[t5b, P["tok_consts"]] + ([t7] if t7 else []), inc="tsc")
        t7 = S.emit("vector", lambda e: e.reciprocal(out=rstd, in_=std),
                    waits=[t6], inc="tv")
        t7b = S.emit("vector", lambda e, t=t: e.tensor_copy(
            out=P["smalls"][:, 24 + t:25 + t], in_=mu), waits=[t7], inc="tv")
        t7c = S.emit("vector", lambda e, t=t: e.tensor_copy(
            out=P["smalls"][:, 32 + t:33 + t], in_=rstd), waits=[t7],
            inc="tv")
        t8 = S.emit("vector", lambda e, t=t, xt=xt: e.tensor_scalar(
            out=xn_sb[:, t * D:(t + 1) * D], in0=xt, scalar1=mu, scalar2=rstd,
            op0=ALU.subtract, op1=ALU.mult), waits=[t7c], inc="tv")
    P["tok_xn"] = S.barrier_token("tv")
    P["mucols"] = P["smalls"][:, 24:32]
    P["rstdcols"] = P["smalls"][:, 32:40]


def emit_l0attn(S, P):
    xn_sb = P["xn_sb"]
    identb, qkvbd_sb, projbd_sb = P["identb"], P["qkvbd_sb"], P["projbd_sb"]
    pbt, pbt2, pb = P["pbt"], P["pbt2"], P["pb"]
    ar4b = P["ar4b"]
    ar56b = P["ar56b"]
    pk, pk2 = P["pk"], P["pk2"]
    smalls = P["smalls"]
    h_tok2 = P["h_tok2"]
    pc = P["pconst_sb"]
    prod, pv = P["prod_sb"], P["pv_sb"]
    gsel, i16b, ebc = P["gsel_sb"], P["i16_sb"], P["ebc_sb"]
    glnsel, ebc8 = P["glnsel_sb"], P["ebc8_sb"]

    Q1 = pk[:, 0:4096]; K1 = pk[:, 4096:8192]
    V1 = pk[:, 8192:12288]; Q2 = pk[:, 12288:16384]
    K2 = pk2[:, 0:4096]; V2 = pk2[:, 4096:8192]
    tokb = [pk2[:, 8192:12288], pk2[:, 12288:16384]]
    QKV = [(Q1, K1, V1), (Q2, K2, V2)]
    On = [ar4b[:, 0:4096], ar56b[:, 0:4096]]
    lep = [ar4b[:, 4096:8192], ar56b[:, 4096:8192]]
    tok2b = [ar4b[:, 0:4096], ar4b[:, 4096:8192]]
    qt = ar56b

    smb = P["smb"]
    pbase = P["smb_base"]
    Pslot = [smb[0:16, pbase:pbase + 512], smb[0:16, pbase + 512:pbase + 1024]]
    rden = smb[0:16, pbase + 1024:pbase + 1536]
    attsb = smb[:, pbase:pbase + 512]
    sqsb = smb[:, pbase + 512:pbase + 1024]
    musb = smb[0:8, 96:608]
    varsb = smb[0:8, 608:1120]
    stdsb = smb[0:8, 1120:1632]
    rstdsb = smb[0:8, 1632:2144]

    def vsl(buf, brr):
        if brr == 0:
            v = buf.rearrange("p (kk d r) -> p kk d r", kk=4, d=8, r=128)
            return lambda t: v[:, :, t, :]
        v = buf.rearrange("p (kk d i j rr) -> p kk d i j rr",
                          kk=4, d=8, i=4, j=8, rr=4)
        return lambda t: v[:, :, :, :, t, :]

    def flatlike(flat, brr):
        if brr == 0:
            return flat.rearrange("p (a b) -> p a b", a=4, b=128)
        return flat.rearrange("p (a b c e) -> p a b c e", a=4, b=8, c=4, e=4)

    tok_need = [P["tok_xn"], P["tok_constsg"], P["tok_consts"]]

    for chunk in range(2):
        chb = [S.barrier_token(x) for x in ("tv", "tt", "tsc", "d_t2")]
        last_rp = None
        for rtl in range(4):
            rt = chunk * 4 + rtl
            prev_evac = P.get("_t_evac_prev")
            ttoks = []
            for k in range(16):
                bank = pbt if k < 8 else pbt2
                kk8 = k % 8
                w = []
                if k == 0:
                    w = list(tok_need) + chb + \
                        ([prev_evac] if prev_evac else []) + \
                        (list(last_rp) if last_rp else [])
                ttoks.append(S.emit("tensor", lambda e, rt=rt, k=k, bank=bank,
                                    kk8=kk8: e.transpose(
                    bank[:, kk8 * 128:(kk8 + 1) * 128],
                    xn_sb[:, rt * D + k * 128: rt * D + (k + 1) * 128],
                    identb), waits=w, inc="tt"))
            ev1 = S.emit("vector", lambda e: e.tensor_copy(
                out=qt[:, 6144:7168], in_=pbt[:, :]),
                waits=[ttoks[7]] + chb + (list(last_rp) if last_rp else []),
                inc="tv")
            ev2 = S.emit("vector", lambda e: e.tensor_copy(
                out=qt[:, 7168:8192], in_=pbt2[:, :]), waits=[ttoks[15]],
                inc="tv")
            P["_t_evac_prev"] = ev2
            qk_evacs = []
            for k in range(16):
                bank = pb[k % 2]
                mm = None
                for g in range(3):
                    w = []
                    if g == 0:
                        w = [ev1 if k < 8 else ev2]
                        if len(qk_evacs) >= 2:
                            w.append(qk_evacs[-2])
                    mm = S.emit("tensor", lambda e, k=k, g=g, bank=bank:
                                e.matmul(
                        bank[:, g * 128:(g + 1) * 128],
                        qkvbd_sb[:, g * 128:(g + 1) * 128],
                        qt[:, 6144 + k * 128:6144 + (k + 1) * 128],
                        start=True, stop=True), waits=w, inc="tt")
                qk_evacs.append(S.emit("vector", lambda e, k=k, bank=bank:
                                       e.tensor_copy(
                    out=qt[:, k * 384:(k + 1) * 384], in_=bank[:, 0:384]),
                    waits=[mm], inc="tv"))
            wdma = [qk_evacs[-1]]
            rp_a = S.dma("sync", P["h_qkvt"][rt].transpose([1, 0, 2]),
                qt[:, 0:6144].rearrange("p (k gr) -> p k gr", k=16),
                waits=wdma, sem="d_rp")
            rp_b = S.dma("scalar", P["h_Tt"][rt].transpose([1, 0, 2]),
                qt[:, 6144:8192].rearrange("p (k r) -> p k r", k=16),
                waits=wdma, sem="d_rps")
            last_rp = (rp_a, rp_b)
        spill_tok = S.barrier_token("d_rp")
        spill_tok2 = S.barrier_token("d_rps")
        if chunk == 0:
            P["tok_attn_tt_c0"] = S.barrier_token("tt")
        if chunk == 1 and "attn_midhook" in P:
            P["attn_midhook"]()
            del P["attn_midhook"]

        qv = P["h_qkvt"].rearrange(
            "rt (kq kk) (d c) (g r) -> rt kq kk d c g r",
            kq=4, kk=4, d=8, c=16, g=3, r=128)
        tv = P["h_Tt"].rearrange(
            "rt (kq kk) (d c) r -> rt kq kk d c r", kq=4, kk=4, d=8, c=16)
        rts = slice(chunk * 4, chunk * 4 + 4)
        rp2 = []
        nrb = 0
        for br in range(2):
            Qb, Kb, Vb = QKV[br]
            cs = slice(br * 8, br * 8 + 8)
            for g, dstb in ((0, Qb), (1, Kb), (2, Vb)):
                src = qv[rts, :, :, :, cs, g, :].transpose([4, 0, 1, 2, 3, 5])
                qx, sx = (("scalar", "d_rps") if (chunk == 0 and nrb % 2)
                          else ("sync", "d_rp"))
                nrb += 1
                rp2.append(S.dma(qx, dstb, src,
                                 waits=[spill_tok, spill_tok2] +
                                 ([P["tok_wprep_v"], P["tok_wprep_bufs_free"]]
                                  if "tok_wprep_v" in P else []),
                                 sem=sx))
        for oct_ in range(2):
            rto = slice(chunk * 4 + oct_ * 2, chunk * 4 + oct_ * 2 + 2)
            src = tv[rto].transpose([4, 0, 1, 2, 3, 5])
            rp2.append(S.dma("sync", tokb[oct_], src,
                             waits=[spill_tok, spill_tok2] +
                             ([P["tok_wprep_v"], P["tok_wprep_bufs_free"]]
                              if "tok_wprep_v" in P else []), sem="d_rp"))
        tok_rp = S.barrier_token("d_rp")
        tok_rp2 = S.barrier_token("d_rps")

        for br in range(2):
            Qb, Kb, Vb = QKV[br]
            sq_ = vsl(Qb, br); sk_ = vsl(Kb, br); sv_ = vsl(Vb, br)
            lepb, Onb = lep[br], On[br]
            so_ = vsl(Onb, br); sL_ = vsl(lepb, br)
            l1 = S.emit("vector", lambda e, Vb=Vb, lepb=lepb, br=br:
                        e.tensor_scalar(
                out=lepb, in0=Vb, scalar1=pc[:, br * 3 + 1:br * 3 + 2],
                scalar2=pc[:, 6 + br:7 + br], op0=ALU.mult, op1=ALU.add),
                waits=[tok_rp, tok_rp2, S.barrier_token("tt")], inc="tv")
            lprev = l1
            for tap, dst_t, src_t in ((2, slice(0, 7), slice(1, 8)),
                                      (0, slice(1, 8), slice(0, 7))):
                if br == 0:
                    Vv = Vb.rearrange("p (kk d r) -> p kk d r", kk=4, d=8, r=128)
                    Lv = lepb.rearrange("p (kk d r) -> p kk d r",
                                        kk=4, d=8, r=128)
                    vsrc = Vv[:, :, src_t, :]
                    ldst = Lv[:, :, dst_t, :]
                else:
                    Vv = Vb.rearrange("p (kk d i j rr) -> p kk d i j rr",
                                      kk=4, d=8, i=4, j=8, rr=4)
                    Lv = lepb.rearrange("p (kk d i j rr) -> p kk d i j rr",
                                        kk=4, d=8, i=4, j=8, rr=4)
                    vsrc = Vv[:, :, :, :, src_t]
                    ldst = Lv[:, :, :, :, dst_t]
                lprev = S.emit("vector", lambda e, vsrc=vsrc, ldst=ldst,
                               tap=tap, br=br: e.scalar_tensor_tensor(
                    out=ldst, in0=vsrc, scalar=pc[:, br * 3 + tap:br * 3 + tap + 1],
                    in1=ldst, op0=ALU.mult, op1=ALU.add), waits=[lprev],
                    inc="tv")

            prev_on = None
            prev_rc = None
            for i in range(8):
                den_mm = None
                o_mm = None
                prev_gmm = None
                prev_den = {}
                prev_pb = {}
                for j in range(8):
                    slot = Pslot[j % 2]
                    pw = [tok_rp, tok_rp2]
                    if prev_gmm:
                        pw.append(prev_gmm)
                    if prev_on:
                        pw.append(prev_on)
                    pr = S.emit("vector", lambda e, a=sq_(i), b=sk_(j),
                                o=flatlike(prod, br): e.tensor_mul(
                        out=o, in0=a, in1=b), waits=pw, inc="tv")
                    g_mm = S.emit("tensor", lambda e: e.matmul(
                        pb[2][0:16, :], gsel, prod, start=True, stop=True),
                        waits=[pr], inc="tt")
                    prev_gmm = g_mm
                    exw = [g_mm]
                    if j >= 2:
                        exw += [prev_den[j - 2], prev_pb[j - 2]]
                    ex = S.emit("scalar", lambda e, slot=slot: e.activation(
                        out=slot, in_=pb[2][0:16, :], func=AF.Exp,
                        scale=float(8) ** -0.5), waits=exw, inc="tsc")
                    dw = [ex]
                    if j == 0 and prev_rc:
                        dw.append(prev_rc)
                    den_mm = S.emit("tensor", lambda e, j=j, slot=slot:
                                    e.matmul(
                        pb[3][0:16, :], i16b, slot,
                        start=(j == 0), stop=(j == 7)), waits=dw, inc="tt")
                    prev_den[j] = den_mm
                    pb_mm = S.emit("tensor", lambda e, slot=slot: e.matmul(
                        pb[4][:, :], ebc, slot, start=True, stop=True),
                        waits=[ex], inc="tt")
                    prev_pb[j] = pb_mm
                    mu_ = S.emit("vector", lambda e, b=sv_(j),
                                 o=flatlike(pv, br),
                                 a=flatlike(pb[4][:, :], br): e.tensor_mul(
                        out=o, in0=a, in1=b),
                        waits=[pb_mm] + ([o_mm] if o_mm else []), inc="tv")
                    o_mm = S.emit("tensor", lambda e, j=j: e.matmul(
                        pb[5][:, :], identb, pv,
                        start=(j == 0), stop=(j == 7)), waits=[mu_], inc="tt")
                rc = S.emit("vector", lambda e: e.reciprocal(
                    out=rden, in_=pb[3][0:16, :]), waits=[den_mm], inc="tv")
                prev_rc = rc
                rb_mm = S.emit("tensor", lambda e: e.matmul(
                    pb[2][:, :], ebc, rden, start=True, stop=True),
                    waits=[rc], inc="tt")
                rb_ev = S.emit("vector", lambda e: e.tensor_copy(
                    out=pv, in_=pb[2][:, :]), waits=[rb_mm, o_mm], inc="tv")
                prev_on = S.emit("vector", lambda e, oi=so_(i),
                                 a=flatlike(pb[5][:, :], br),
                                 b=flatlike(pv, br): e.tensor_mul(
                    out=oi, in0=a, in1=b), waits=[rb_ev], inc="tv")

        brc = [S.barrier_token("tv"), S.barrier_token("tt"),
               S.barrier_token("tsc")]

        apply_done = []
        for oct_ in range(2):
            for nch in range(8):
                csl = slice(nch * 512, (nch + 1) * 512)
                bank = pb[4 + (nch % 2)]
                mm = None
                first = True
                for br in range(2):
                    lhs = projbd_sb[:, (br * 2 + oct_) * 128:
                                    (br * 2 + oct_ + 1) * 128]
                    for buf in (On[br], lep[br]):
                        w = []
                        if first:
                            w = brc + ([apply_done[-2]] if len(apply_done) >= 2
                                       else [])
                        mm = S.emit("tensor", lambda e, lhs=lhs, buf=buf,
                                    csl=csl, bank=bank, first=first,
                                    last=(br == 1 and buf is lep[1]): e.matmul(
                            bank[:, :], lhs, buf[:, csl], start=first,
                            stop=last), waits=w, inc="tt")
                        first = False
                aev = S.emit("vector", lambda e, bank=bank: e.tensor_scalar(
                    out=attsb, in0=bank[:, :], scalar1=pc[:, 10:11],
                    scalar2=None, op0=ALU.add),
                    waits=[mm] + ([apply_done[-1]] if apply_done else []),
                    inc="tv")
                sq = S.emit("scalar", lambda e: e.activation(
                    out=sqsb, in_=attsb, func=AF.Square), waits=[aev],
                    inc="tsc")
                mu_mm = S.emit("tensor", lambda e: e.matmul(
                    pb[2][0:8, :], glnsel, attsb, start=True, stop=True),
                    waits=[aev], inc="tt")
                sq_mm = S.emit("tensor", lambda e: e.matmul(
                    pb[3][0:8, :], glnsel, sqsb, start=True, stop=True),
                    waits=[sq], inc="tt")
                muev = S.emit("vector", lambda e: e.tensor_copy(
                    out=musb, in_=pb[2][0:8, :]), waits=[mu_mm], inc="tv")
                msq = S.emit("vector", lambda e: e.tensor_mul(
                    out=varsb, in0=musb, in1=musb), waits=[muev], inc="tv")
                vv = S.emit("vector", lambda e: e.tensor_sub(
                    out=varsb, in0=pb[3][0:8, :], in1=varsb),
                    waits=[msq, sq_mm], inc="tv")
                sd = S.emit("scalar", lambda e: e.activation(
                    out=stdsb, in_=varsb, func=AF.Sqrt,
                    bias=pc[0:8, 16:17], scale=1.0), waits=[vv], inc="tsc")
                rs = S.emit("vector", lambda e: e.reciprocal(
                    out=rstdsb, in_=stdsb), waits=[sd], inc="tv")
                mb_mm = S.emit("tensor", lambda e: e.matmul(
                    pb[2][:, :], ebc8, musb, start=True, stop=True),
                    waits=[rs], inc="tt")
                rb_mm2 = S.emit("tensor", lambda e: e.matmul(
                    pb[3][:, :], ebc8, rstdsb, start=True, stop=True),
                    waits=[rs], inc="tt")
                t1 = S.emit("vector", lambda e: e.tensor_sub(
                    out=sqsb, in0=attsb, in1=pb[2][:, :]),
                    waits=[mb_mm], inc="tv")
                t2 = S.emit("vector", lambda e: e.tensor_tensor(
                    out=attsb, in0=sqsb, in1=pb[3][:, :], op=ALU.mult),
                    waits=[t1, rb_mm2], inc="tv")
                t3 = S.emit("vector", lambda e: e.tensor_scalar(
                    out=sqsb, in0=attsb, scalar1=pc[:, 8:9],
                    scalar2=pc[:, 9:10], op0=ALU.mult, op1=ALU.add),
                    waits=[t2], inc="tv")
                apply_done.append(S.emit("vector",
                    lambda e, oct_=oct_, csl=csl: e.tensor_tensor(
                    out=tokb[oct_][:, csl], in0=sqsb,
                    in1=tokb[oct_][:, csl], op=ALU.add), waits=[t3], inc="tv"))
        apb = S.barrier_token("tv")

        for oct_ in range(2):
            ro_src = tokb[oct_].rearrange("p (kk d r) -> p kk d r",
                                          kk=4, d=8, r=128)
            ro_src = ro_src.transpose([0, 3, 1, 2])
            ro_dst = tok2b[oct_].rearrange("p (r kk d) -> p r kk d",
                                           r=128, kk=4, d=8)
            ro = S.emit("vector", lambda e, ro_dst=ro_dst, ro_src=ro_src:
                        e.tensor_copy(out=ro_dst, in_=ro_src),
                        waits=[apb], inc="tv")
            dstv = h_tok2.rearrange(
                "o (rt r kq m) -> o rt kq r m", rt=8, r=128, kq=4, m=32)
            for quad in range(2):
                rtl = oct_ * 2 + quad
                rt = chunk * 4 + rtl
                for o in range(16):
                    src = tok2b[oct_][o * 8 + quad * 4:o * 8 + quad * 4 + 4, :]
                    src2 = src.rearrange("so (r m) -> so r m", r=128, m=32)
                    S.dma("sync", dstv[o, rt], src2, waits=[ro], sem="d_t2")



def emit_wprep_stats(S, P):
    ar2, ar3 = P["ar2"], P["ar3"]
    smalls = P["smalls"]
    pc = P["pconst_sb"]
    pb = P["pb"]
    w0 = [P["tok_consts"], P["tok_constsg"], P["tok_xn"],
          S.barrier_token("tsc")]

    acc = smalls[:, 0:1]
    acc2 = smalls[:, 1:2]
    tmpc = smalls[:, 2:3]

    w2v = P["w2s"].rearrange("(a p) d -> a p d", p=128)
    srcs = [P["w1s"][0:128], P["w1s"][128:256],
            w2v[0:4].transpose([1, 0, 2]), w2v[4:8].transpose([1, 0, 2])]
    P["wp_srcs"] = srcs
    prev_red = None
    reds = []
    for i, src in enumerate(srcs):
        buf = ar2 if i % 2 == 0 else ar3
        bufv = buf if i < 2 else buf.rearrange("p (a d) -> p a d", a=4)
        tw = S.dma("scalar", bufv, src, waits=w0 + ([reds[i - 2]] if i >= 2
                                                    else []), sem="d_wps")
        pr = S.emit("vector", lambda e, buf=buf: e.tensor_reduce(
            out=tmpc, in_=buf, axis=AX.X, op=ALU.add,
            apply_absolute_value=True),
            waits=[tw] + ([prev_red] if prev_red else []), inc="tv")
        dst = acc if i < 2 else acc2
        prev_red = S.emit("vector", lambda e, dst=dst, first=(i % 2 == 0):
                          e.tensor_copy(out=dst, in_=tmpc) if first else
                          e.tensor_add(out=dst, in0=dst, in1=tmpc),
                          waits=[pr], inc="tv")
        reds.append(prev_red)
    pm1 = S.emit("tensor", lambda e: e.matmul(
        pb[0][0:1, 0:2], pc[:, 11:12], smalls[:, 0:2],
        start=True, stop=True), waits=[prev_red], inc="tt")
    ev = S.emit("vector", lambda e: e.tensor_copy(
        out=smalls[0:1, 4:6], in_=pb[0][0:1, 0:2]), waits=[pm1], inc="tv")
    st1 = S.dma("sync", P["h_stat_i"], smalls[0:1, 4:6], waits=[ev],
                sem="d_wp")
    wc = S.dma("gpsimd", P["h_W1c"], P["W1s"], waits=[], sem="d_wc")
    wc_wq = S.dma("gpsimd", P["h_wqcs"], P["wqs"], waits=[], sem="d_wc")
    wc_wkv = S.dma("gpsimd", P["h_wkvcs"], P["wkvs"], waits=[], sem="d_wc")
    wc_wo = S.dma("gpsimd", P["h_wocs"], P["wos"], waits=[], sem="d_wc")
    wc_w0 = S.dma("gpsimd", P["h_W0cs"], P["W0s"], waits=[], sem="d_wc")
    P["tok_wc_W1"] = wc
    P["tok_wc_W0"] = wc_w0
    P["tok_wc_wq"], P["tok_wc_wkv"], P["tok_wc_wo"] = wc_wq, wc_wkv, wc_wo
    ar_tok = S.emit("gpsimd", lambda e: e.collective_compute(
        "AllReduce", ALU.add, replica_groups=G8,
        ins=[P["h_stat_i"]], outs=[P["h_stat_o"]]), waits=[st1], inc="tpo")
    P["tok_stat_ar"] = ar_tok
    P["tok_stat_ld"] = S.dma("gpsimd", smalls[0:1, 5:7], P["h_stat_o"],
                             waits=[ar_tok], sem="d_wc")
    P["tok_wprep_bufs_free"] = S.barrier_token("d_wps")
    P["tok_wprep_v"] = S.barrier_token("tv")


def emit_wprep_quant(S, P):
    ar1 = P["ar1"]
    ar1b = P["xn_sb"]
    ar1i = ar1.bitcast(I32)
    smalls = P["smalls"]
    pb = P["pb"]
    srcs = P["wp_srcs"]
    head = [P["tok_stat_ld"], S.barrier_token("tt"), P["tok_wprep_v"],
            S.barrier_token("d_rps")]

    c1_ = S.emit("vector", lambda e: e.tensor_scalar(
        out=smalls[0:1, 5:7], in0=smalls[0:1, 5:7],
        scalar1=1.0 / (D * FF), scalar2=EPS, op0=ALU.mult, op1=ALU.max),
        waits=head, inc="tv")
    c1b_ = S.emit("vector", lambda e: e.tensor_copy(
        out=smalls[0:1, 3:5], in_=smalls[0:1, 5:7]), waits=[c1_], inc="tv")
    c2_ = S.emit("vector", lambda e: e.reciprocal(
        out=smalls[0:1, 5:7], in_=smalls[0:1, 5:7]), waits=[c1b_], inc="tv")
    bc = S.emit("tensor", lambda e: e.matmul(
        pb[0][:, 0:4], P["onesrow_sb"], smalls[0:1, 3:7],
        start=True, stop=True), waits=[c2_, S.barrier_token("tt")],
        inc="tt")
    sc = S.emit("vector", lambda e: e.tensor_copy(
        out=smalls[:, 4:8], in_=pb[0][:, 0:4]), waits=[bc], inc="tv")
    s1col = smalls[:, 6:7]
    s2col = smalls[:, 7:8]
    P["s1col"], P["s2col"] = s1col, s2col
    P["m1col"] = smalls[:, 4:5]
    P["m2col"] = smalls[:, 5:6]

    wslot = [ar1[:, 0:2048], ar1[:, 2048:4096]]
    yb = ar1[:, 4096:5120]
    tib = ar1i[:, 5120:6144]
    c1b = ar1[:, 6144:7168]
    cmb = ar1[:, 7168:8192]
    w2qv = P["h_w2q"].rearrange("(a p) d -> a p d", p=128)
    dsts = [P["h_w1q"][0:128], P["h_w1q"][128:256],
            w2qv[0:4].transpose([1, 0, 2]), w2qv[4:8].transpose([1, 0, 2])]
    slot_store = [None, None]
    slot_lastread = [None, None]
    qprev = sc
    ag1 = None
    for i in range(4):
        scol = s1col if i < 2 else s2col
        for q in range(4):
            sl = (i * 4 + q) % 2
            lsrc = srcs[i][:, q * 2048:(q + 1) * 2048] if i < 2 else \
                srcs[i][:, q]
            wv = [P["tok_attn_tt_c0"]] if (i, q) == (0, 0) else []
            if slot_store[sl]:
                wv.append(slot_store[sl])
            tw = S.dma("scalar", wslot[sl], lsrc, waits=wv, sem="d_wps")
            for b in range(2):
                bsl = slice(b * 1024, (b + 1) * 1024)
                q1 = S.emit("vector", lambda e, sl=sl, bsl=bsl, scol=scol:
                            e.tensor_scalar(
                    out=yb, in0=wslot[sl][:, bsl], scalar1=scol,
                    scalar2=None, op0=ALU.mult), waits=[tw, sc, qprev],
                    inc="tv")
                q2 = S.emit("scalar", lambda e: e.copy(out=tib, in_=yb),
                            waits=[q1], inc="tsc")
                q3 = S.emit("vector", lambda e: e.scalar_tensor_tensor(
                    out=c1b, in0=tib, scalar=0.5, in1=yb, op0=ALU.add,
                    op1=ALU.is_lt), waits=[q2], inc="tv")
                q4 = S.emit("vector", lambda e: e.scalar_tensor_tensor(
                    out=cmb, in0=tib, scalar=-0.5, in1=yb, op0=ALU.add,
                    op1=ALU.is_gt), waits=[q3], inc="tv")
                q5 = S.emit("vector", lambda e: e.tensor_add(
                    out=yb, in0=tib, in1=c1b), waits=[q4], inc="tv")
                q6 = S.emit("vector", lambda e: e.tensor_sub(
                    out=c1b, in0=yb, in1=cmb), waits=[q5], inc="tv")
                q7 = S.emit("vector", lambda e, sl=sl, b=b: e.tensor_scalar(
                    out=ar1b[:, sl * 4096 + b * 2048:
                             sl * 4096 + b * 2048 + 1024],
                    in0=c1b, scalar1=1.0, scalar2=-1.0,
                    op0=ALU.min, op1=ALU.max), waits=[q6], inc="tv")
                qprev = q7
            dsthbm = dsts[i][:, q * 2048:(q + 1) * 2048] if i < 2 else \
                dsts[i][:, q]
            st = S.dma("scalar", dsthbm.rearrange("p (b r) -> p b r", b=2),
                       ar1b[:, sl * 4096:sl * 4096 + 4096].rearrange(
                           "p (b r) -> p b r", b=2)[:, :, 0:1024],
                       waits=[qprev], sem="d_wps")
            slot_store[sl] = st
            if i == 1 and q == 3:
                ag1 = S.emit("gpsimd", lambda e: e.collective_compute(
                    "AllGather", ALU.bypass, replica_groups=G8,
                    ins=[P["h_w1q"]], outs=[P["h_w1g"]]),
                    waits=[st], inc="tpo")
                P["tok_w1g_ffn"] = ag1
    ag2 = S.emit("gpsimd", lambda e: e.collective_compute(
        "AllGather", ALU.bypass, replica_groups=G8,
        ins=[P["h_w2q"]], outs=[P["h_w2g"]]),
        waits=[ag1, slot_store[0], slot_store[1]], inc="tpo")
    P["tok_wq"] = ag2
    P["tok_quant_done"] = S.barrier_token("d_wps")


def emit_wprep_ags(S, P):
    P["tok_ag"] = S.emit("gpsimd", lambda e: e.collective_compute(
        "AllGather", ALU.bypass, replica_groups=G4,
        ins=[P["h_tok2"]], outs=[P["h_tok2g"]]),
        waits=[S.barrier_token("d_t2"), P["tok_wq"]], inc="tpo")
    ag_wq = S.emit("gpsimd", lambda e: e.collective_compute(
        "AllGather", ALU.bypass, replica_groups=G8,
        ins=[P["h_wqcs"]], outs=[P["h_wqc"]]),
        waits=[P["tok_ag"], P["tok_wc_wq"]], inc="tpo")
    ag_wkv = S.emit("gpsimd", lambda e: e.collective_compute(
        "AllGather", ALU.bypass, replica_groups=G8,
        ins=[P["h_wkvcs"]], outs=[P["h_wkvc"]]),
        waits=[ag_wq, P["tok_wc_wkv"]], inc="tpo")
    ag_wo = S.emit("gpsimd", lambda e: e.collective_compute(
        "AllGather", ALU.bypass, replica_groups=G8,
        ins=[P["h_wocs"]], outs=[P["h_woc"]]), waits=[ag_wkv, P["tok_wc_wo"]],
        inc="tpo")
    P["tok_wqg"] = ag_wq
    P["tok_wkvg"] = ag_wkv
    P["tok_wog"] = ag_wo


def emit_l0ffn(S, P):
    ar1b = P["xn_sb"]
    ar2b, ar3b = P["pk"], P["pk2"]
    ar4, ar56 = P["ar4"], P["ar56"]
    ar4b, ar56b = P["ar4b"], P["ar56b"]
    ar56i, ar4i = P["ar56i"], P["ar4"].bitcast(I32)
    smalls, smb = P["smalls"], P["smb"]
    pc = P["pconst_sb"]
    pb = P["pb"]
    pbt_f = P["pbt"].bitcast(F32)
    pbt2_f = P["pbt2"].bitcast(F32)
    banks = P["pb"] + [pbt_f[:, 0:512], pbt2_f[:, 0:512]]
    xn_sb = P["xn_sb"]
    x_sb = P["x_sb"]
    identb = P["identb"]
    s1col, s2col = P["s1col"], P["s2col"]

    ffw = [S.barrier_token("d_t2"), S.barrier_token("tv"),
           S.barrier_token("tt"), S.barrier_token("tsc"),
           S.barrier_token("d_rp"), S.barrier_token("d_rps"),
           S.barrier_token("d_gb"),
           S.barrier_token("d_wp"), S.barrier_token("d_wps")]

    amcol = smalls[:, 10:11]
    invcols = smalls[:, 12:20]
    sccol = smalls[:, 11:12]
    xfb = ar4[:, 0:2048]
    yb = ar4[:, 2048:4096]
    tib = ar56i[:, 0:2048]
    c1b = ar56[:, 2048:4096]
    cmb = ar4[:, 0:2048]
    xqb = ar56b[:, 0:2048]

    prev = None
    for t in range(NT):
        w = ffw + ([prev] if prev else [])
        a0 = S.emit("vector", lambda e, t=t: e.tensor_scalar(
            out=xfb, in0=x_sb[:, t * D:(t + 1) * D],
            scalar1=P["mucols"][:, t:t + 1], scalar2=P["rstdcols"][:, t:t + 1],
            op0=ALU.subtract, op1=ALU.mult), waits=w, inc="tv")
        xt = xfb
        a1 = S.emit("vector", lambda e, xt=xt: e.tensor_reduce(
            out=amcol, in_=xt, axis=AX.X, op=ALU.max,
            apply_absolute_value=True), waits=[a0], inc="tv")
        a2 = S.emit("vector", lambda e: e.tensor_scalar(
            out=amcol, in0=amcol, scalar1=EPS, scalar2=None, op0=ALU.max),
            waits=[a1], inc="tv")
        a3 = S.emit("vector", lambda e: e.reciprocal(out=sccol, in_=amcol),
                    waits=[a2], inc="tv")
        a4 = S.emit("vector", lambda e: e.tensor_scalar(
            out=sccol, in0=sccol, scalar1=127.0, scalar2=None, op0=ALU.mult),
            waits=[a3], inc="tv")
        a5 = S.emit("vector", lambda e, t=t: e.tensor_scalar(
            out=invcols[:, t:t + 1], in0=amcol, scalar1=1.0 / 127.0,
            scalar2=None, op0=ALU.mult), waits=[a4], inc="tv")
        a5b = S.emit("vector", lambda e, t=t: e.tensor_tensor(
            out=invcols[:, t:t + 1], in0=invcols[:, t:t + 1], in1=P["m1col"],
            op=ALU.mult), waits=[a5], inc="tv")
        q1 = S.emit("vector", lambda e, xt=xt: e.tensor_scalar(
            out=yb, in0=xt, scalar1=sccol, scalar2=None, op0=ALU.mult),
            waits=[a5b], inc="tv")
        q2 = S.emit("scalar", lambda e: e.copy(out=tib, in_=yb),
                    waits=[q1] + ffw, inc="tsc")
        q3 = S.emit("vector", lambda e: e.scalar_tensor_tensor(
            out=c1b, in0=tib, scalar=0.5, in1=yb, op0=ALU.add,
            op1=ALU.is_lt), waits=[q2], inc="tv")
        q4 = S.emit("vector", lambda e: e.scalar_tensor_tensor(
            out=cmb, in0=tib, scalar=-0.5, in1=yb, op0=ALU.add,
            op1=ALU.is_gt), waits=[q3], inc="tv")
        q5 = S.emit("vector", lambda e: e.tensor_add(
            out=yb, in0=tib, in1=c1b), waits=[q4], inc="tv")
        q6 = S.emit("vector", lambda e: e.tensor_sub(
            out=xqb, in0=yb, in1=cmb), waits=[q5], inc="tv")
        tts = []
        for dc in range(16):
            bank = P["pbt"] if dc < 8 else P["pbt2"]
            kk8 = dc % 8
            w2 = [q6]
            if prev and dc == 0:
                w2 = [q6, prev]
            tts.append(S.emit("tensor", lambda e, dc=dc, bank=bank, kk8=kk8:
                              e.transpose(
                bank[:, kk8 * 128:(kk8 + 1) * 128],
                xqb[:, dc * 128:(dc + 1) * 128], identb),
                waits=w2 if dc == 0 else [], inc="tt"))
        e1 = S.emit("vector", lambda e, t=t: e.tensor_copy(
            out=ar3b.rearrange("p (dc t r) -> p dc t r", dc=16, t=8)[
                :, 0:8, t, :].rearrange("p a b -> p (a b)") if False else
            ar3b.rearrange("p (dc tt r) -> p dc tt r", dc=16, tt=8)[
                :, 0:8, t], in_=P["pbt"].rearrange(
                "p (a r) -> p a r", a=8)), waits=[tts[7]], inc="tv")
        e2 = S.emit("vector", lambda e, t=t: e.tensor_copy(
            out=ar3b.rearrange("p (dc tt r) -> p dc tt r", dc=16, tt=8)[
                :, 8:16, t], in_=P["pbt2"].rearrange(
                "p (a r) -> p a r", a=8)), waits=[tts[15]], inc="tv")
        prev = e2

    xqT_done = prev

    gbc = ar1b[:, 0:8192]
    bbc = ar1b[:, 8192:16384]
    tg = S.dma("gpsimd", gbc, P["vrows"][1:2, :].broadcast_to([128, FF]),
               waits=[xqT_done, P["tok_quant_done"]], sem="d_fg")
    tb = S.dma("gpsimd", bbc, P["vrows"][2:3, :].broadcast_to([128, FF]),
               waits=[xqT_done], sem="d_fg")

    b1bc = ar4[:, 0:512]
    hbou = [ar4[:, 512 + i * 512:1024 + i * 512] for i in range(2)]
    hbou_b = [ar4.bitcast(BF16)[:, 1024 + i * 512:1536 + i * 512]
              for i in range(2)]
    slab = [ar2b[:, 0:8192], ar2b[:, 8192:16384]]
    slab_tok = [None, None]
    evs = {}
    mm_last = {}
    ev_last = {}
    prev_b1 = None
    for fc in range(16):
        sl = fc % 2
        fsl = slice(fc * 512, (fc + 1) * 512)
        w = [P["tok_w1g_ffn"], xqT_done]
        if fc >= 2:
            w.append(mm_last[fc - 2])
        w = [x for x in w if x]
        src = P["h_w1g"][:, fsl].rearrange("(dc p) f -> dc p f", p=128)
        sv = slab[sl].rearrange("p (dc f) -> p dc f", dc=16)
        lta = S.dma("sync", sv[:, 0:8], src[0:8].transpose([1, 0, 2]),
                    waits=w, sem="d_ff")
        ltb = S.dma("scalar", sv[:, 8:16], src[8:16].transpose([1, 0, 2]),
                    waits=w, sem="d_ffs")
        slab_tok[sl] = lta
        bt = S.dma("gpsimd", ar4[:, 0:512],
                   P["vrows"][0:1, fsl].broadcast_to([128, 512]),
                   waits=[prev_b1] if prev_b1 else ffw, sem="d_fg")
        mms = []
        mm = None
        for dc in range(16):
            for t in range(NT):
                w2 = [lta, ltb]
                if fc >= 1 and dc == 0 and t == 0:
                    w2.append(ev_last[fc - 1])
                lhs = ar3b.rearrange("p (dc tt r) -> p dc tt r",
                                     dc=16, tt=8)[:, dc, t]
                mm = S.emit("tensor", lambda e, lhs=lhs, sl=sl, dc=dc, t=t:
                            e.matmul(
                    banks[t][:, :], lhs,
                    slab[sl][:, dc * 512:(dc + 1) * 512],
                    start=(dc == 0), stop=(dc == 15)),
                    waits=w2 if (dc == 0 and t == 0) else [], inc="tt")
                mms.append(mm)
        eprev = None
        if fc == 0:
            slot_last = [None, None]
        for t in range(NT):
            wv = [mms[-(NT - t)], bt] + ([eprev] if eprev else [])
            if slot_last[t % 2]:
                wv.append(slot_last[t % 2])
            eprev = S.emit("vector", lambda e, t=t, fc=fc: e.scalar_tensor_tensor(
                out=hbou_b[t % 2], in0=banks[t][:, :],
                scalar=invcols[:, t:t + 1], in1=ar4[:, 0:512],
                op0=ALU.mult, op1=ALU.add), waits=wv, inc="tv")
            slot_last[t % 2] = S.dma("gpsimd",
                                     P["h_h"][t * 128:(t + 1) * 128, fsl],
                                     hbou_b[t % 2], waits=[eprev],
                                     sem="d_fsp")
            evs[(fc, t)] = slot_last[t % 2]
        mm_last[fc] = mms[-1]
        ev_last[fc] = eprev
        prev_b1 = eprev
    ffn1_done = S.barrier_token("d_fsp")
    ffn1s_done = S.barrier_token("d_ffs")

    gtile = P["ar3"]
    htbs = [P["pk"][:, 0:8192], P["pk"][:, 8192:16384]]
    ydb = P["ar4"][:, 0:2048]
    c1db = P["ar4"][:, 2048:4096]
    tidb = P["ar56i"][:, 0:2048]
    cmdb = P["ar56"][:, 2048:4096]
    hqblk = smb[:, 96:2144]
    hbnc = smb[:, 2144:3168]
    mucol = smalls[:, 20:21]
    esqcol = smalls[:, 21:22]
    prev_spill = None
    bounce_free = None
    htb_free = [None, None]
    gtile_free = None
    scratch_free = None
    for t in range(NT):
        sl = t % 2
        htb = htbs[sl]
        w = [ffn1_done, ffn1s_done]
        if htb_free[sl]:
            w.append(htb_free[sl])
        if t < 2:
            w.append(S.barrier_token("tv"))
            w.append(S.barrier_token("tsc"))
        l1a = S.dma("sync", htb[:, 0:4096],
                    P["h_h"][t * 128:(t + 1) * 128, 0:4096],
                    waits=w, sem="d_hp")
        l1b = S.dma("scalar", htb[:, 4096:8192],
                    P["h_h"][t * 128:(t + 1) * 128, 4096:8192],
                    waits=w, sem="d_hps")
        wv = [l1a, l1b]
        if gtile_free:
            wv.append(gtile_free)
        g1 = S.emit("scalar", lambda e, htb=htb: e.activation(
            out=gtile, in_=htb, func=AF.Gelu), waits=wv, inc="tsc")
        htb_free[sl] = g1
        m1 = S.emit("vector", lambda e: e.tensor_reduce(
            out=mucol, in_=gtile, axis=AX.X, op=ALU.add), waits=[g1],
            inc="tv")
        m2 = S.emit("vector", lambda e: e.tensor_scalar(
            out=mucol, in0=mucol, scalar1=1.0 / FF, scalar2=None,
            op0=ALU.mult), waits=[m1], inc="tv")
        sq_ac = []
        for i4 in range(4):
            wv = [g1] + ([sq_ac[-1]] if sq_ac else [])
            if i4 == 0 and scratch_free:
                wv.append(scratch_free)
            sq_ac.append(S.emit("scalar", lambda e, i4=i4: e.activation(
                out=ydb, in_=gtile[:, i4 * 2048:(i4 + 1) * 2048],
                func=AF.Square, scale=float(FF) ** -0.5,
                accum_out=smalls[:, 22 + i4:23 + i4]), waits=wv, inc="tsc"))
        s1_ = S.emit("vector", lambda e: e.tensor_add(
            out=esqcol, in0=smalls[:, 22:23], in1=smalls[:, 23:24]),
            waits=[sq_ac[-1]], inc="tv")
        s2_ = S.emit("vector", lambda e: e.tensor_add(
            out=smalls[:, 22:23], in0=smalls[:, 24:25], in1=smalls[:, 25:26]),
            waits=[s1_], inc="tv")
        s3_ = S.emit("vector", lambda e: e.tensor_add(
            out=esqcol, in0=esqcol, in1=smalls[:, 22:23]), waits=[s2_],
            inc="tv")
        v1 = S.emit("vector", lambda e: e.tensor_mul(
            out=smalls[:, 22:23], in0=mucol, in1=mucol), waits=[s3_, m2],
            inc="tv")
        v2 = S.emit("vector", lambda e: e.tensor_sub(
            out=esqcol, in0=esqcol, in1=smalls[:, 22:23]), waits=[v1],
            inc="tv")
        v3 = S.emit("scalar", lambda e: e.activation(
            out=esqcol, in_=esqcol, func=AF.Sqrt, bias=pc[:, 16:17],
            scale=1.0), waits=[v2], inc="tsc")
        v4 = S.emit("vector", lambda e: e.reciprocal(
            out=esqcol, in_=esqcol), waits=[v3], inc="tv")
        z1 = S.emit("vector", lambda e: e.tensor_scalar(
            out=gtile, in0=gtile, scalar1=mucol, scalar2=esqcol,
            op0=ALU.subtract, op1=ALU.mult),
            waits=[v4, sq_ac[-1], tg, tb], inc="tv")
        z2 = S.emit("vector", lambda e: e.tensor_mul(
            out=gtile, in0=gtile, in1=gbc), waits=[z1], inc="tv")
        z3 = S.emit("vector", lambda e: e.tensor_add(
            out=gtile, in0=gtile, in1=bbc), waits=[z2], inc="tv")
        a1 = S.emit("vector", lambda e: e.tensor_reduce(
            out=amcol, in_=gtile, axis=AX.X, op=ALU.max,
            apply_absolute_value=True), waits=[z3], inc="tv")
        a2 = S.emit("vector", lambda e: e.tensor_scalar(
            out=amcol, in0=amcol, scalar1=EPS, scalar2=None, op0=ALU.max),
            waits=[a1], inc="tv")
        a3 = S.emit("vector", lambda e: e.reciprocal(out=sccol, in_=amcol),
                    waits=[a2], inc="tv")
        a4 = S.emit("vector", lambda e: e.tensor_scalar(
            out=sccol, in0=sccol, scalar1=127.0, scalar2=None, op0=ALU.mult),
            waits=[a3], inc="tv")
        a5 = S.emit("vector", lambda e, t=t: e.tensor_scalar(
            out=invcols[:, t:t + 1], in0=amcol, scalar1=1.0 / 127.0,
            scalar2=None, op0=ALU.mult), waits=[a4], inc="tv")
        a5b = S.emit("vector", lambda e, t=t: e.tensor_tensor(
            out=invcols[:, t:t + 1], in0=invcols[:, t:t + 1], in1=P["m2col"],
            op=ALU.mult), waits=[a5], inc="tv")
        qprev = a5b
        blk_free = None
        ev_prev_grp = None
        for i4 in range(4):
            bsl = slice(i4 * 2048, (i4 + 1) * 2048)
            q1 = S.emit("vector", lambda e, bsl=bsl: e.tensor_scalar(
                out=ydb, in0=gtile[:, bsl], scalar1=sccol, scalar2=None,
                op0=ALU.mult), waits=[qprev], inc="tv")
            if i4 == 3:
                gtile_free = q1
            q2 = S.emit("scalar", lambda e: e.copy(out=tidb, in_=ydb),
                        waits=[q1], inc="tsc")
            q3 = S.emit("vector", lambda e: e.scalar_tensor_tensor(
                out=c1db, in0=tidb, scalar=0.5, in1=ydb, op0=ALU.add,
                op1=ALU.is_lt), waits=[q2], inc="tv")
            q4 = S.emit("vector", lambda e: e.scalar_tensor_tensor(
                out=cmdb, in0=tidb, scalar=-0.5, in1=ydb, op0=ALU.add,
                op1=ALU.is_gt), waits=[q3], inc="tv")
            q5 = S.emit("vector", lambda e: e.tensor_add(
                out=ydb, in0=tidb, in1=c1db), waits=[q4], inc="tv")
            wv = [q5]
            if blk_free:
                wv.append(blk_free)
            q6 = S.emit("vector", lambda e: e.tensor_sub(
                out=hqblk, in0=ydb, in1=cmdb), waits=wv, inc="tv")
            qprev = q6
            if i4 == 3:
                scratch_free = q6
            for gj in range(2):
                g8 = i4 * 2 + gj
                tts = []
                for i8 in range(8):
                    wv = []
                    if i8 == 0:
                        wv = [q6]
                        if ev_prev_grp:
                            wv.append(ev_prev_grp)
                    tts.append(S.emit("tensor", lambda e, gj=gj, i8=i8:
                                      e.transpose(
                        P["pbt"][:, i8 * 128:(i8 + 1) * 128],
                        hqblk[:, (gj * 8 + i8) * 128:
                              (gj * 8 + i8 + 1) * 128],
                        identb), waits=wv, inc="tt"))
                ev = S.emit("vector", lambda e: e.tensor_copy(
                    out=hbnc, in_=P["pbt"][:, :]),
                    waits=[tts[-1]] +
                    ([bounce_free] if bounce_free else []), inc="tv")
                ev_prev_grp = ev
                blk_free = ev
                dst = P["h_hqT"].rearrange("(fc p) tok -> fc p tok", p=128)[
                    g8 * 8:(g8 + 1) * 8, :,
                    t * 128:(t + 1) * 128].transpose([1, 0, 2])
                prev_spill = S.dma("gpsimd", dst,
                                   hbnc.rearrange("p (a r) -> p a r", a=8),
                                   waits=[ev], sem="d_hsp")
                bounce_free = prev_spill
    hq_done = S.barrier_token("d_hsp")

    w2b = [ar2b[:, 0:512], ar2b[:, 512:1024]]
    hqb = [ar2b[:, 1024 + i * 1024: 2048 + i * 1024] for i in range(2)]
    b2bc = ar4[:, 0:2048]
    tb2 = S.dma("gpsimd", b2bc,
                P["vrows"][3:4, 0:2048].broadcast_to([128, 2048]),
                waits=[hq_done, S.barrier_token("tv")], sem="d_fg")
    x1parts = []
    slot_mm = [None, None]
    prev_dq_ev = None
    for dq in range(4):
        dsl = slice(dq * 512, (dq + 1) * 512)
        mm = None
        for fc in range(64):
            sl = fc % 2
            w = [hq_done, P["tok_wq"]]
            if slot_mm[sl] is not None:
                w.append(slot_mm[sl])
            qb, sb2 = ("scalar", "d_f2s") if fc % 2 == 0 else \
                ("sync", "d_f2")
            wt = S.dma("gpsimd", w2b[sl],
                       P["h_w2g"][fc * 128:(fc + 1) * 128, dsl],
                       waits=w, sem="d_f2g")
            ht = S.dma(qb, hqb[sl],
                       P["h_hqT"][fc * 128:(fc + 1) * 128, :],
                       waits=w, sem=sb2)
            for t in range(NT):
                w2_ = []
                if t == 0:
                    w2_ = [wt, ht]
                    if fc == 0 and prev_dq_ev is not None:
                        w2_.append(prev_dq_ev)
                mm = S.emit("tensor", lambda e, sl=sl, t=t, fc=fc: e.matmul(
                    banks[t][:, :], hqb[sl][:, t * 128:(t + 1) * 128],
                    w2b[sl], start=(fc == 0), stop=(fc == 63)),
                    waits=w2_, inc="tt")
            slot_mm[sl] = mm
        eprev = None
        for t in range(NT):
            xsl = x_sb[:, t * D + dq * 512: t * D + (dq + 1) * 512]
            e1 = S.emit("vector", lambda e, t=t, dq=dq: e.scalar_tensor_tensor(
                out=ar4[:, 2048:2560], in0=banks[t][:, :],
                scalar=invcols[:, t:t + 1],
                in1=ar4[:, dq * 512:(dq + 1) * 512],
                op0=ALU.mult, op1=ALU.add),
                waits=[mm, tb2] + ([eprev] if eprev else []), inc="tv")
            eprev = S.emit("vector", lambda e, xsl=xsl: e.tensor_add(
                out=xsl, in0=ar4[:, 2048:2560], in1=xsl), waits=[e1],
                inc="tv")
            x1parts.append(eprev)
        prev_dq_ev = eprev
    P["tok_x1p"] = S.barrier_token("tv")


def emit_x1(S, P):
    x_sb = P["x_sb"]
    prevs = []

    def consume(t, recv, tok):
        w = [tok, P["tok_x1p"]] + prevs[-1:]
        r = S.emit("vector", lambda e, t=t, recv=recv: e.tensor_add(
            out=x_sb[:, t * D:(t + 1) * D], in0=recv,
            in1=x_sb[:, t * D:(t + 1) * D]), waits=w, inc="tv")
        prevs.append(r)
        return r

    emit_attn0_gather(S, P, to_out=False, extra_waits=[P["tok_x1p"]],
                      consume_fn=consume)
    P["tok_x1"] = S.barrier_token("tv")


def emit_l1(S, P):
    x_sb = P["x_sb"]
    ar1, ar4 = P["ar1"], P["ar4"]
    ar1b = P["xn_sb"]
    ar2b, ar3b = P["pk"], P["pk2"]
    ar4b, ar56b = P["ar4b"], P["ar56b"]
    smalls, smb = P["smalls"], P["smb"]
    pc = P["pconst_sb"]
    pb = P["pb"]
    banks = pb + [P["pbt"].bitcast(F32)[:, 0:512],
                  P["pbt2"].bitcast(F32)[:, 0:512]]
    identb = P["identb"]
    ar3v = ar3b.rearrange("p (dc tt r) -> p dc tt r", dc=16, tt=8)

    l1w = [P["tok_x1"], S.barrier_token("tt"), S.barrier_token("tsc"),
           S.barrier_token("d_f2"), S.barrier_token("d_gb"),
           S.barrier_token("d_fg")]
    cut = os.environ.get("KBD_L1CUT", "")

    def cut_here(name):
        if cut == name:
            P["tok_a1_added"] = S.barrier_token("tv")
            return True
        return False

    mu = smalls[:, 0:1]; esq = smalls[:, 1:2]; var = smalls[:, 2:3]
    std = smalls[:, 3:4]; rstd = smalls[:, 4:5]; sc5 = smalls[:, 5:6]
    epsc = pc[:, 16:17]

    prev = None
    for t in range(NT):
        xt = x_sb[:, t * D:(t + 1) * D]
        w = l1w + ([prev] if prev else [])
        t1 = S.emit("vector", lambda e, xt=xt: e.tensor_reduce(
            out=mu, in_=xt, axis=AX.X, op=ALU.add), waits=w, inc="tv")
        t2 = S.emit("scalar", lambda e, xt=xt: e.activation(
            out=P["ar2"][:, 0:D], in_=xt, func=AF.Square,
            scale=float(D) ** -0.5, accum_out=esq), waits=w, inc="tsc")
        t3 = S.emit("vector", lambda e: e.tensor_scalar(
            out=mu, in0=mu, scalar1=1.0 / D, scalar2=None, op0=ALU.mult),
            waits=[t1, t2], inc="tv")
        t5 = S.emit("vector", lambda e: e.tensor_mul(
            out=sc5, in0=mu, in1=mu), waits=[t3], inc="tv")
        t5b = S.emit("vector", lambda e: e.tensor_sub(
            out=var, in0=esq, in1=sc5), waits=[t5], inc="tv")
        t6 = S.emit("scalar", lambda e: e.activation(
            out=std, in_=var, func=AF.Sqrt, bias=epsc, scale=1.0),
            waits=[t5b], inc="tsc")
        t7 = S.emit("vector", lambda e: e.reciprocal(out=rstd, in_=std),
                    waits=[t6], inc="tv")
        prev = S.emit("vector", lambda e, t=t, xt=xt: e.tensor_scalar(
            out=ar1b[:, t * D:(t + 1) * D], in0=xt, scalar1=mu, scalar2=rstd,
            op0=ALU.subtract, op1=ALU.mult), waits=[t7], inc="tv")
    xn1_done = prev

    prevt = None
    for t in range(NT):
        tts = []
        for dc in range(16):
            bank = P["pbt"] if dc < 8 else P["pbt2"]
            kk8 = dc % 8
            w = []
            if dc == 0:
                w = [xn1_done] + ([prevt] if prevt else [])
            tts.append(S.emit("tensor", lambda e, t=t, dc=dc, bank=bank,
                              kk8=kk8: e.transpose(
                bank[:, kk8 * 128:(kk8 + 1) * 128],
                ar1b[:, t * D + dc * 128:t * D + (dc + 1) * 128], identb),
                waits=w, inc="tt"))
        e1 = S.emit("vector", lambda e, t=t: e.tensor_copy(
            out=ar3v[:, 0:8, t], in_=P["pbt"].rearrange(
                "p (a r) -> p a r", a=8)), waits=[tts[7]], inc="tv")
        e2 = S.emit("vector", lambda e, t=t: e.tensor_copy(
            out=ar3v[:, 8:16, t], in_=P["pbt2"].rearrange(
                "p (a r) -> p a r", a=8)), waits=[tts[15], e1], inc="tv")
        prevt = e2
    xn1T_done = prevt
    if cut_here("B"):
        return

    cst0 = P["prod_sb"]
    cst1 = P["pv_sb"]
    dmask_sb = smb[:, 3168:3680]
    tk_c0 = S.dma("sync", cst0, P["cstab_in"][:, 0:512], waits=l1w,
                  sem="d_l1")
    tk_c1 = S.dma("sync", cst1, P["cstab_in"][:, 512:1024], waits=l1w,
                  sem="d_l1")
    tk_dm = S.dma("sync", dmask_sb, P["dmask_in"], waits=l1w, sem="d_l1")

    def costl(t):
        base = (cst0 if t < 4 else cst1)
        return base[:, (t % 4) * 128:(t % 4) * 128 + 64]

    def sintl(t):
        base = (cst0 if t < 4 else cst1)
        return base[:, (t % 4) * 128 + 64:(t % 4) * 128 + 128]

    slabs = [ar1b[:, 0:8192], ar1b[:, 8192:16384]]
    bias_bufs = [ar4[:, 0:512], ar4[:, 512:1024]]
    q_tm = ar2b
    kv_tm = ar56b[:, 0:4096]
    slab_free = [None, None]
    bias_free = [None, None]
    evac_last = None
    mm_last = None
    for ch in range(5):
        sl = ch % 2
        if ch < 4:
            wsrc = P["h_wqc"][:, ch * 512:(ch + 1) * 512]
            agw = P["tok_wqg"]
            brow, bsl = 6, slice(ch * 512, (ch + 1) * 512)
        else:
            wsrc = P["h_wkvc"]
            agw = P["tok_wkvg"]
            brow, bsl = 7, slice(0, 512)
        w = [xn1T_done, agw]
        if slab_free[sl]:
            w.append(slab_free[sl])
        psrc = wsrc.rearrange("(dc p) f -> dc p f", p=128)
        pdst = slabs[sl].rearrange("p (dc f) -> p dc f", dc=16)
        lt_a = S.dma("sync", pdst[:, 0:8], psrc[0:8].transpose([1, 0, 2]),
                     waits=w, sem="d_l1")
        lt = S.dma("scalar", pdst[:, 8:16], psrc[8:16].transpose([1, 0, 2]),
                   waits=w, sem="d_l1s")
        bw = [bias_free[sl]] if bias_free[sl] else l1w
        bt = S.dma("gpsimd", bias_bufs[sl],
                   P["vrows"][brow:brow + 1, bsl].broadcast_to([128, 512]),
                   waits=bw, sem="d_fg")
        mm = None
        for dc in range(16):
            for t in range(NT):
                w2 = []
                if dc == 0 and t == 0:
                    w2 = [lt_a, lt] + ([evac_last] if evac_last else [])
                mm = S.emit("tensor", lambda e, dc=dc, t=t, sl=sl: e.matmul(
                    banks[t][:, :], ar3v[:, dc, t],
                    slabs[sl][:, dc * 512:(dc + 1) * 512],
                    start=(dc == 0), stop=(dc == 15)), waits=w2, inc="tt")
        slab_free[sl] = mm
        mm_last = mm
        eprev = None
        for t in range(NT):
            if ch < 4:
                dst = q_tm[:, t * 2048 + ch * 512:t * 2048 + (ch + 1) * 512]
            else:
                dst = kv_tm[:, t * 512:(t + 1) * 512]
            eprev = S.emit("vector", lambda e, t=t, dst=dst, sl=sl:
                           e.tensor_tensor(
                out=dst, in0=banks[t][:, :], in1=bias_bufs[sl],
                op=ALU.add),
                waits=[mm, bt] + ([eprev] if eprev else []), inc="tv")
        bias_free[sl] = eprev
        evac_last = eprev
    proj_done = evac_last
    if cut_here("D"):
        return

    r1 = ar4b[:, 0:1024]; r2 = ar4b[:, 1024:2048]
    r3 = ar4b[:, 2048:3072]; r4 = ar4b[:, 3072:4096]
    rprev = proj_done
    for t in range(NT):
        for which in (0, 1):
            if which == 0:
                nh = 16
                base = q_tm[:, t * 2048:(t + 1) * 2048]
            else:
                nh = 2
                base = kv_tm[:, t * 512:t * 512 + 256]
            v3 = base.rearrange("p (h d) -> p h d", h=nh)
            d1 = v3[:, :, 0:64]
            d2 = v3[:, :, 64:128]
            cosb = costl(t).rearrange("p (o f) -> p o f", o=1).broadcast_to(
                [128, nh, 64])
            sinb = sintl(t).rearrange("p (o f) -> p o f", o=1).broadcast_to(
                [128, nh, 64])
            n64 = nh * 64
            r1f = r1[:, 0:n64].rearrange("p (h d) -> p h d", h=nh)
            r2f = r2[:, 0:n64].rearrange("p (h d) -> p h d", h=nh)
            r3f = r3[:, 0:n64].rearrange("p (h d) -> p h d", h=nh)
            r4f = r4[:, 0:n64].rearrange("p (h d) -> p h d", h=nh)
            o1 = S.emit("vector", lambda e, d1=d1, cosb=cosb, r1f=r1f:
                        e.tensor_mul(out=r1f, in0=d1, in1=cosb),
                        waits=[rprev, tk_c0, tk_c1], inc="tv")
            o2 = S.emit("vector", lambda e, d2=d2, sinb=sinb, r2f=r2f:
                        e.tensor_mul(out=r2f, in0=d2, in1=sinb),
                        waits=[o1], inc="tv")
            o3 = S.emit("vector", lambda e, d1=d1, sinb=sinb, r3f=r3f:
                        e.tensor_mul(out=r3f, in0=d1, in1=sinb),
                        waits=[o2], inc="tv")
            o4 = S.emit("vector", lambda e, d2=d2, cosb=cosb, r4f=r4f:
                        e.tensor_mul(out=r4f, in0=d2, in1=cosb),
                        waits=[o3], inc="tv")
            o5 = S.emit("vector", lambda e, d1=d1, r1f=r1f, r2f=r2f:
                        e.tensor_sub(out=d1, in0=r1f, in1=r2f),
                        waits=[o4], inc="tv")
            rprev = S.emit("vector", lambda e, d2=d2, r3f=r3f, r4f=r4f:
                           e.tensor_add(out=d2, in0=r4f, in1=r3f),
                           waits=[o5], inc="tv")
    rope_done = rprev
    if cut_here("E"):
        return

    for t in range(NT):
        S.dma("sync", P["h_kv"][t * 128:(t + 1) * 128, :],
              kv_tm[:, t * 512:(t + 1) * 512], waits=[rope_done],
              sem="d_kv")
    ag_kv = S.emit("gpsimd", lambda e: e.collective_compute(
        "AllGather", ALU.bypass, replica_groups=G4,
        ins=[P["h_kv"]], outs=[P["h_kvg"]]),
        waits=[S.barrier_token("d_kv")], inc="tpo")
    if DBG_STAGE >= 5:
        ag_W0 = S.emit("gpsimd", lambda e: e.collective_compute(
            "AllGather", ALU.bypass, replica_groups=G8,
            ins=[P["h_W0cs"]], outs=[P["h_W0g"]]),
            waits=[ag_kv, P["tok_wc_W0"]], inc="tpo")
        ag_W1 = S.emit("gpsimd", lambda e: e.collective_compute(
            "AllGather", ALU.bypass, replica_groups=G8,
            ins=[P["h_W1c"]], outs=[P["h_W1g"]]),
            waits=[ag_W0, P["tok_wc_W1"]], inc="tpo")
        P["tok_W0g"], P["tok_W1g"] = ag_W0, ag_W1

    qf_prev = None
    for t in range(NT):
        tts = []
        for dc in range(16):
            bank = P["pbt"] if dc < 8 else P["pbt2"]
            kk8 = dc % 8
            w = []
            if dc == 0:
                w = [rope_done] + ([qf_prev] if qf_prev else [])
            tts.append(S.emit("tensor", lambda e, t=t, dc=dc, bank=bank,
                              kk8=kk8: e.transpose(
                bank[:, kk8 * 128:(kk8 + 1) * 128],
                q_tm[:, t * 2048 + dc * 128:t * 2048 + (dc + 1) * 128],
                identb), waits=w, inc="tt"))
        ev1 = S.emit("vector", lambda e: e.tensor_copy(
            out=ar4b[:, 0:1024], in_=P["pbt"][:, :]), waits=[tts[7]],
            inc="tv")
        ev2 = S.emit("vector", lambda e: e.tensor_copy(
            out=ar4b[:, 1024:2048], in_=P["pbt2"][:, :]),
            waits=[tts[15], ev1], inc="tv")
        qf_prev = S.dma("sync", P["h_qf"].rearrange(
            "(dc p) tok -> dc p tok", p=128)[
            :, :, t * 128:(t + 1) * 128].transpose([1, 0, 2]),
            ar4b[:, 0:2048].rearrange("p (dc r) -> p dc r", dc=16),
            waits=[ev2], sem="d_qf")
    qf_done = S.barrier_token("d_qf")
    if cut_here("G"):
        return

    kfm = ar56b
    kvsrc = P["h_kvg"].rearrange("(cc tt p) c -> cc tt p c", cc=4, tt=8)
    kb_prev = None
    kfm_prev = None
    tp_prev = [None, None]
    for kvh in range(2):
        w = [ag_kv, qf_done] if kvh == 0 else [kb_prev]
        if kvh == 0:
            w.append(S.barrier_token("tt"))
        for cc in range(4):
            kld = S.dma("sync",
                        ar1b[:, kvh * 4096:(kvh + 1) * 4096].rearrange(
                            "p (tt cc r) -> p tt cc r", tt=8, cc=4)[
                            :, :, cc, :],
                        kvsrc[cc, :, :, kvh * 128:(kvh + 1) * 128].transpose(
                            [1, 0, 2]), waits=(w if cc == 0 else []),
                        sem="d_kf")
        kb_prev = kld
        if cut == "H1" and kvh == 0:
            P["tok_a1_added"] = S.barrier_token("d_kf")
            return
        for kt in range(32):
            sl = kt % 2
            tt_, cc_ = kt // 4, kt % 4
            src = ar1b[:, kvh * 4096 + (tt_ * 4 + cc_) * 128:
                       kvh * 4096 + (tt_ * 4 + cc_ + 1) * 128]
            w2 = [kld]
            if tp_prev[sl]:
                w2.append(tp_prev[sl])
            tbank = P["pbt"] if sl == 0 else P["pbt2"]
            tp = S.emit("tensor", lambda e, tbank=tbank, src=src: e.transpose(
                tbank[:, 0:128], src, identb), waits=w2, inc="tt")
            kfm_prev = S.emit("vector", lambda e, kvh=kvh, kt=kt,
                              tbank=tbank: e.tensor_copy(
                out=kfm[:, kvh * 4096 + kt * 128:kvh * 4096 + (kt + 1) * 128],
                in_=tbank[:, 0:128]),
                waits=[tp] + ([kfm_prev] if kfm_prev else []), inc="tv")
            tp_prev[sl] = kfm_prev
    kfm_done = kfm_prev
    if cut_here("H2"):
        return
    for cc in range(4):
        vload = S.dma("sync",
                      ar4b.rearrange("p (tt cc c) -> p tt cc c",
                                     tt=8, cc=4)[:, :, cc, :],
                      kvsrc[cc, :, :, 256:512].transpose([1, 0, 2]),
                      waits=([ag_kv, qf_done] if cc == 0 else []),
                      sem="d_kf")
    if cut_here("H"):
        return

    qbig = [ar1b[:, 0:8192], ar1b[:, 8192:16384]]
    pts = [smb[:, 96 + i * 512:608 + i * 512] for i in range(4)]
    rdenb = smalls[:, 1072:1584]
    rden = P["prod_sb"][0:1, 0:512]
    m_my = None
    scale_s = float(128) ** -0.5
    s_free = [None, None, None]
    pt_free = [None, None, None, None]
    o_free = [None, None]
    d_free = [None, None]
    rden_free = None
    rdenb_free = None
    pb4_free = [None, None]
    exp_prev = None
    te_last = None
    qld = [None, None]
    for jg in range(2):
        w = [qf_done, kfm_done]
        qld[jg] = S.dma("sync", qbig[jg].rearrange("p (h c) -> p h c", h=16),
                        P["h_qf"][:, jg * 512:(jg + 1) * 512].rearrange(
                            "(h p) c -> h p c", p=128).transpose([1, 0, 2]),
                        waits=w, sem="d_qr")
        tiles = [(kb, kt) for kb in range(jg * 4 + 4) for kt in range(4)]
        for h in range(16):
            kvh = h // 8
            hp = h % 2
            obank = pb[2 + 3 * hp]
            dband = (pb[3] if hp == 0 else
                     P["pbt"].bitcast(F32))[0:1, 0:512]
            bcbank = P["pbt2"].bitcast(F32)[:, 0:512]
            qrhs = qbig[jg][:, h * 512:(h + 1) * 512]
            omm = None
            dmm = None
            pt_ready = {}

            def emit_s(idx):
                nonlocal te_last
                kb, kt = tiles[idx]
                par = idx % 3
                sbank = pb[par if par < 2 else 4]
                gk = kb * 4 + kt
                w2 = [qld[jg]]
                if s_free[par]:
                    w2.append(s_free[par])
                if jg == 0 and h == 0 and idx == 0:
                    w2.append(vload)
                tok = S.emit("tensor", lambda e, kvh=kvh, gk=gk,
                             sbank=sbank, qrhs=qrhs: e.matmul(
                    sbank[:, 0:512],
                    kfm[:, kvh * 4096 + gk * 128:kvh * 4096 + (gk + 1) * 128],
                    qrhs, start=True, stop=True), waits=w2, inc="tt")
                te_last = tok
                return tok

            def emit_exp(idx, sm):
                nonlocal exp_prev
                kb, kt = tiles[idx]
                par = idx % 3
                sbank = pb[par if par < 2 else 4]
                pp = idx % 4
                wv = [sm]
                if pt_free[pp]:
                    wv.append(pt_free[pp])
                if exp_prev:
                    wv.append(exp_prev)
                ex = S.emit("scalar", lambda e, sbank=sbank, pp=pp:
                            e.activation(
                    out=pts[pp], in_=sbank[:, 0:512], func=AF.Exp,
                    scale=scale_s), waits=wv, inc="tsc")
                exp_prev = ex
                s_free[par] = ex
                kbb = kb - jg * 4
                if kbb >= 0:
                    if kbb > 0:
                        ex = S.emit("vector", lambda e, pp=pp, kbb=kbb:
                                    e.memset(pts[pp][:, 0:kbb * 128], 0.0),
                                    waits=[ex], inc="tv")
                    ex = S.emit("vector", lambda e, pp=pp, kbb=kbb, kt=kt:
                                e.tensor_tensor(
                        out=pts[pp][:, kbb * 128:(kbb + 1) * 128],
                        in0=pts[pp][:, kbb * 128:(kbb + 1) * 128],
                        in1=dmask_sb[:, kt * 128:(kt + 1) * 128],
                        op=ALU.mult), waits=[ex, tk_dm], inc="tv")
                return ex

            def emit_ov(idx):
                nonlocal omm, dmm, te_last
                kb, kt = tiles[idx]
                pp = idx % 4
                gk = kb * 4 + kt
                first = idx == 0
                last = idx == len(tiles) - 1
                w2 = [pt_ready[idx]]
                if first and o_free[hp]:
                    w2.append(o_free[hp])
                omm = S.emit("tensor", lambda e, gk=gk, pp=pp, kvh=kvh,
                             obank=obank, first=first, last=last: e.matmul(
                    obank[:, 0:512],
                    ar4b[:, gk * 256 + kvh * 128:gk * 256 + (kvh + 1) * 128],
                    pts[pp][:, 0:512], start=first, stop=last),
                    waits=w2, inc="tt")
                w2 = [pt_ready[idx]]
                if first and d_free[hp]:
                    w2.append(d_free[hp])
                dmm = S.emit("tensor", lambda e, pp=pp, dband=dband,
                             first=first, last=last: e.matmul(
                    dband, P["onescol_b"], pts[pp][:, 0:512],
                    start=first, stop=last), waits=w2, inc="tt")
                te_last = dmm
                pt_free[pp] = dmm

            n_t = len(tiles)
            stoks = {0: emit_s(0)}
            if n_t > 1:
                stoks[1] = emit_s(1)
            for idx in range(2, n_t):
                pt_ready[idx - 2] = emit_exp(idx - 2, stoks[idx - 2])
                stoks[idx] = emit_s(idx)
                emit_ov(idx - 2)
            for idx in range(max(n_t - 2, 0), n_t):
                pt_ready[idx] = emit_exp(idx, stoks[idx])
                emit_ov(idx)
            wv = [dmm]
            if rden_free:
                wv.append(rden_free)
            rc = S.emit("vector", lambda e, dband=dband: e.reciprocal(
                out=rden, in_=dband), waits=wv, inc="tv")
            d_free[hp] = rc
            wv = [rc, P["tok_constsg"]]
            if pb4_free[0]:
                wv.append(pb4_free[0])
            bcmm = S.emit("tensor", lambda e, bcbank=bcbank: e.matmul(
                bcbank, P["onesrow_b"], rden, start=True, stop=True),
                waits=wv, inc="tt")
            te_last = bcmm
            rden_free = bcmm
            wv = [bcmm]
            if rdenb_free:
                wv.append(rdenb_free)
            rbe = S.emit("vector", lambda e, bcbank=bcbank: e.tensor_copy(
                out=rdenb, in_=bcbank), waits=wv, inc="tv")
            pb4_free[0] = rbe
            on = S.emit("vector", lambda e, h=h, jg=jg, obank=obank:
                        e.tensor_mul(
                out=ar2b[:, (h * 8 + jg * 4) * 128:
                         (h * 8 + jg * 4 + 4) * 128],
                in0=obank[:, 0:512], in1=rdenb), waits=[rbe, omm],
                inc="tv")
            o_free[hp] = on
            rdenb_free = on
    if cut_here("I"):
        return
    flash_dve_done = S.barrier_token("tv")
    flash_te_done = S.barrier_token("tt")

    woslab = [ar1b[:, 0:512], ar1b[:, 512:1024]]
    wo_slab_free = [None, None]
    evac_last = None
    first = True
    for dq in range(4):
        mm = None
        for dc in range(16):
            sl = dc % 2
            w = [P["tok_wog"]]
            if first:
                w += [flash_dve_done, flash_te_done]
                first = False
            if wo_slab_free[sl]:
                w.append(wo_slab_free[sl])
            lt = S.dma("sync", woslab[sl],
                       P["h_woc"][dc * 128:(dc + 1) * 128,
                                  dq * 512:(dq + 1) * 512],
                       waits=w, sem="d_wo")
            for t in range(NT):
                w2 = []
                if t == 0:
                    w2 = [lt]
                    if dc == 0 and evac_last:
                        w2.append(evac_last)
                mm = S.emit("tensor", lambda e, dc=dc, t=t, sl=sl: e.matmul(
                    banks[t][:, :],
                    ar2b[:, (dc * 8 + t) * 128:(dc * 8 + t + 1) * 128],
                    woslab[sl], start=(dc == 0), stop=(dc == 15)),
                    waits=w2, inc="tt")
            wo_slab_free[sl] = mm
        eprev = None
        for t in range(NT):
            xsl = x_sb[:, t * D + dq * 512:t * D + (dq + 1) * 512]
            eprev = S.emit("vector", lambda e, t=t, xsl=xsl: e.tensor_tensor(
                out=xsl, in0=banks[t][:, :], in1=xsl, op=ALU.add),
                waits=[mm] + ([eprev] if eprev else []), inc="tv")
        evac_last = eprev
    tok_wo_mm = S.barrier_token("tt")
    P["tok_a1_added"] = S.barrier_token("tv")
    if DBG_STAGE == 4:
        return

    wsl_bufs = [ar56b[:, 0:2048], ar56b[:, 2048:4096]]
    dbou = [ar4b[:, 0:1024], ar4b[:, 1024:2048]]
    k_slab_free = [None, None]
    k_dbou_free = [None, None]
    gelu_prev = None
    kmm_last = {}
    for ft in range(64):
        sl = ft % 2
        w = [P["tok_W0g"], flash_te_done, flash_dve_done]
        if k_slab_free[sl]:
            w.append(k_slab_free[sl])
        ksrc = P["h_W0g"][:, ft * 128:(ft + 1) * 128].rearrange(
            "(dc p) f -> dc p f", p=128)
        kdst = wsl_bufs[sl].rearrange("p (dc f) -> p dc f", dc=16)
        lt_a = S.dma("sync", kdst[:, 0:8], ksrc[0:8].transpose([1, 0, 2]),
                     waits=w, sem="d_w0")
        lt = S.dma("scalar", kdst[:, 8:16], ksrc[8:16].transpose([1, 0, 2]),
                   waits=w, sem="d_w0s")
        mm = None
        for half in range(2):
            bank = pb[sl * 2 + half]
            for dc in range(16):
                w2 = []
                if dc == 0:
                    w2 = [lt_a, lt]
                    if kmm_last.get((sl, half)):
                        w2.append(kmm_last[(sl, half)])
                    else:
                        w2.append(P["tok_a1_added"])
                mm = S.emit("tensor", lambda e, sl=sl, half=half, dc=dc,
                            bank=bank: e.matmul(
                    bank[:, :], wsl_bufs[sl][:, dc * 128:(dc + 1) * 128],
                    ar3b[:, dc * 1024 + half * 512:
                         dc * 1024 + (half + 1) * 512],
                    start=(dc == 0), stop=(dc == 15)), waits=w2, inc="tt")
            wg = [mm, P["tok_consts"]]
            if gelu_prev:
                wg.append(gelu_prev)
            if k_dbou_free[sl] and half == 0:
                wg.append(k_dbou_free[sl])
            gelu_prev = S.emit("scalar", lambda e, sl=sl, half=half, ft=ft,
                               bank=bank: e.activation(
                out=dbou[sl][:, half * 512:(half + 1) * 512], in_=bank[:, :],
                func=AF.Gelu, bias=P["b0c_sb"][:, ft:ft + 1], scale=1.0),
                waits=wg, inc="tsc")
            kmm_last[(sl, half)] = gelu_prev
        k_slab_free[sl] = mm
        k_dbou_free[sl] = S.dma("gpsimd",
                                P["h_DqT"][ft * 128:(ft + 1) * 128, :],
                                dbou[sl], waits=[gelu_prev], sem="d_d1")
    dqt_done = S.barrier_token("d_d1")
    gelu_done = S.barrier_token("tsc")

    w1b = [ar2b[:, 0:512], ar2b[:, 512:1024]]
    dqb = [ar2b[:, 1024 + i * 1024:2048 + i * 1024] for i in range(2)]
    b1bc = ar4[:, 0:2048]
    tb1 = S.dma("gpsimd", b1bc,
                P["vrows"][5:6, 0:2048].broadcast_to([128, 2048]),
                waits=[dqt_done, gelu_done], sem="d_fg")
    slot_mm = [None, None]
    prev_dq_ev = P["tok_a1_added"]
    for dq in range(4):
        mm = None
        for fc in range(64):
            sl = fc % 2
            w = [dqt_done, P["tok_W1g"], tok_wo_mm]
            if slot_mm[sl] is not None:
                w.append(slot_mm[sl])
            qb, sb2 = ("scalar", "d_f2bs") if fc % 2 == 0 else \
                ("sync", "d_f2b")
            wt = S.dma("gpsimd", w1b[sl],
                       P["h_W1g"][fc * 128:(fc + 1) * 128,
                                  dq * 512:(dq + 1) * 512],
                       waits=w, sem="d_f2bg")
            ht = S.dma(qb, dqb[sl],
                       P["h_DqT"][fc * 128:(fc + 1) * 128, :],
                       waits=w, sem=sb2)
            for t in range(NT):
                w2 = []
                if t == 0:
                    w2 = [wt, ht]
                    if fc == 0:
                        w2 += [prev_dq_ev, gelu_done]
                mm = S.emit("tensor", lambda e, sl=sl, t=t, fc=fc: e.matmul(
                    banks[t][:, :], dqb[sl][:, t * 128:(t + 1) * 128],
                    w1b[sl], start=(fc == 0), stop=(fc == 63)),
                    waits=w2, inc="tt")
            slot_mm[sl] = mm
        eprev = None
        for t in range(NT):
            xsl = x_sb[:, t * D + dq * 512:t * D + (dq + 1) * 512]
            e1 = S.emit("vector", lambda e, t=t, dq=dq: e.tensor_tensor(
                out=ar4[:, 2048:2560], in0=banks[t][:, :],
                in1=b1bc[:, dq * 512:(dq + 1) * 512], op=ALU.add),
                waits=[mm, tb1, P["tok_a1_added"]] +
                ([eprev] if eprev else []), inc="tv")
            eprev = S.emit("vector", lambda e, xsl=xsl: e.tensor_add(
                out=xsl, in0=ar4[:, 2048:2560], in1=xsl), waits=[e1],
                inc="tv")
        prev_dq_ev = eprev
    P["tok_final"] = S.barrier_token("tv")


def emit_debug_out(S, P):
    out_t, x_sb = P["out_t"], P["x_sb"]
    if DBG_STAGE == 0:
        xn_sb = P["xn_sb"]
        for t in range(NT):
            S.dma("gpsimd", out_t[t * 128:(t + 1) * 128, :],
                  xn_sb[:, t * D:(t + 1) * D], waits=[P["tok_xn"]], sem="d_out")
        S.emit("gpsimd", lambda e: None, waits=[S.barrier_token("d_out")],
               inc="done")
        return
    if DBG_STAGE == 1:
        emit_attn0_gather(S, P, to_out=True)
        S.emit("gpsimd", lambda e: None, waits=[S.barrier_token("d_out")],
               inc="done")
        return
    if DBG_STAGE in (2, 3, 4, 99):
        tok = {2: P.get("tok_x1p"), 3: P.get("tok_x1"),
               4: P.get("tok_a1_added"), 99: P.get("tok_final")}[DBG_STAGE]
        for t in range(NT):
            S.dma("sync", P["out_t"][t * 128:(t + 1) * 128, :],
                  P["x_sb"][:, t * D:(t + 1) * D], waits=[tok], sem="d_out")
        S.emit("sync", lambda e: None, waits=[S.barrier_token("d_out")],
               inc="done")
        return
    raise NotImplementedError(f"stage {DBG_STAGE}")


def emit_attn0_gather(S, P, to_out=False, extra_waits=(), consume_fn=None):
    h_tok2g = P["h_tok2g"]
    out_t = P["out_t"]
    recv = P["ar4b"]
    rows_view = h_tok2g.rearrange("a (b c) -> (a b) c", b=64, c=2048)
    gi = P["gidx_sb"]
    res = []
    slot_done = [None] * 4
    for t in range(NT):
        dst = recv[:, (t % 4) * 2048:(t % 4) * 2048 + 2048]
        w = [P["tok_ag"], P["tok_gidx"]] + list(extra_waits)
        if slot_done[t % 4]:
            w.append(slot_done[t % 4])
        w.append(S.barrier_token("d_gb"))
        tk = S.emit("gpsimd", lambda e, t=t, dst=dst: e.indirect_dma_start(
            out=dst, out_offset=None, in_=rows_view,
            in_offset=IndirectOffsetOnAxis(gi[:, t:t + 1], 0)),
            waits=w, inc="d_gb", inc_amt=16)
        if to_out:
            slot_done[t % 4] = S.dma("gpsimd",
                                     out_t[t * 128:(t + 1) * 128, :], dst,
                                     waits=[tk], sem="d_out")
        elif consume_fn is not None:
            slot_done[t % 4] = consume_fn(t, dst, tk)
        res.append((dst, tk))
    return res



def host_prep(inputs):
    x = np.ascontiguousarray(inputs["L0_attn_x"].reshape(B * S, D))
    w1T = np.ascontiguousarray(inputs["L0_ffn_w1"].T)
    w2T = np.ascontiguousarray(inputs["L0_ffn_w2"].T)
    W0T = np.ascontiguousarray(inputs["L1_ffn_W0"].T)
    W1T = np.ascontiguousarray(inputs["L1_ffn_W1"].T)
    onesrow = np.ones((1, 128), np.float32)
    wqT = np.ascontiguousarray(inputs["L1_attn_wq"].T)
    wkvT = np.ascontiguousarray(
        np.concatenate([inputs["L1_attn_wk"].T, inputs["L1_attn_wv"].T], axis=1))
    woT = np.ascontiguousarray(inputs["L1_attn_wo"].T)

    qkv_w = inputs["L0_attn_qkv_w"]
    qkvbd3 = np.zeros((3, 128, 128), np.float32)
    for g in range(3):
        for d8 in range(8):
            qkvbd3[g, d8 * 16:(d8 + 1) * 16, d8 * 16:(d8 + 1) * 16] = \
                qkv_w[:, g * 16:(g + 1) * 16]
    qkvbd = np.ascontiguousarray(qkvbd3.transpose(1, 0, 2).reshape(128, 3 * 128))

    proj_w = inputs["L0_attn_proj_w"]
    projbd4 = np.zeros((4, 128, 128), np.float32)
    for br in range(2):
        for oct_ in range(2):
            m = projbd4[br * 2 + oct_]
            for c in range(8):
                for s8 in range(8):
                    s = 8 * oct_ + s8
                    for o in range(16):
                        m[c * 16 + s, o * 8 + s8] = proj_w[c + 8 * br, o]

    projbd = np.ascontiguousarray(projbd4.transpose(1, 0, 2).reshape(128, 4 * 128))
    ident = np.eye(128, dtype=np.float32)

    pconst = np.zeros((128, 32), np.float32)
    cw0 = inputs["L0_attn_conv_w0"]
    cw1 = inputs["L0_attn_conv_w1"]
    cb0 = inputs["L0_attn_conv_b0"]
    cb1 = inputs["L0_attn_conv_b1"]
    for c in range(8):
        for s in range(16):
            p = c * 16 + s
            pconst[p, 0:3] = cw0[c, 0, 1, 0:3]
            pconst[p, 3:6] = cw1[c, 0, 0:3, 1]
            pconst[p, 6] = cb0[c]
            pconst[p, 7] = cb1[c]
    ln_w = inputs["L0_attn_ln_w"]
    ln_b = inputs["L0_attn_ln_b"]
    proj_b = inputs["L0_attn_proj_b"]
    for o in range(16):
        for so in range(8):
            p = o * 8 + so
            pconst[p, 8] = ln_w[o]
            pconst[p, 9] = ln_b[o]
            pconst[p, 10] = proj_b[o]
    pconst[:, 11] = 1.0
    pconst[:, 16] = EPS

    vrows = np.zeros((10, FF), np.float32)
    vrows[0] = inputs["L0_ffn_b1"]
    vrows[1] = inputs["L0_ffn_gamma"]
    vrows[2] = inputs["L0_ffn_beta"]
    vrows[3, :D] = inputs["L0_ffn_b2"]
    vrows[4] = inputs["L1_ffn_bias0"]
    vrows[5, :D] = inputs["L1_ffn_bias1"]
    vrows[6, :D] = inputs["L1_attn_bq"]
    vrows[7, :256] = inputs["L1_attn_bk"]
    vrows[7, 256:512] = inputs["L1_attn_bv"]

    cmaskv = np.where(np.tril(np.ones((128, 128), bool)), 0.0, -30000.0
                      ).astype(np.float32)

    gselv = np.zeros((128, 16), np.float32)
    for c in range(8):
        for s_ in range(16):
            gselv[c * 16 + s_, s_] = 1.0
    ebcv = np.zeros((16, 128), np.float32)
    for c in range(8):
        for s_ in range(16):
            ebcv[s_, c * 16 + s_] = 1.0
    i16v = np.eye(16, dtype=np.float32)
    glnv = np.zeros((128, 8), np.float32)
    for o in range(16):
        for so in range(8):
            glnv[o * 8 + so, so] = 1.0 / 16.0
    ebc8v = np.zeros((8, 128), np.float32)
    for o in range(16):
        for so in range(8):
            ebc8v[so, o * 8 + so] = 1.0

    in_maps = []
    for p in range(8):
        rows = core_rows(p)
        postab = np.zeros((128, NT), np.float32)
        gidx = np.zeros((128, NT), np.int32)
        for t in range(NT):
            postab[:, t] = (rows[t * 128:(t + 1) * 128] % S).astype(np.float32)
            g = core_tiles(p)[t]
            for pp in range(128):
                h_b = g * 128 + pp
                c = h_b // 256
                tval = h_b % 256
                g_s = tval // 8
                snd = g_s % 4
                j_s = g_s // 4
                gidx[pp, t] = (snd * 16 + c) * 64 + j_s * 8 + (tval % 8)
        inv = (10000.0 ** (-np.arange(64, dtype=np.float64) / 64.0))
        cstab = np.zeros((128, NT * 128), np.float32)
        for t in range(NT):
            pos = (rows[t * 128:(t + 1) * 128] % S).astype(np.float64)
            ang = pos[:, None] * inv[None, :]
            cstab[:, t * 128:t * 128 + 64] = np.cos(ang)
            cstab[:, t * 128 + 64:t * 128 + 128] = np.sin(ang)
        dmaskv = np.zeros((128, 512), np.float32)
        m = p % 4
        rr, cc2 = np.arange(128)[:, None], np.arange(128)[None, :]
        for kt in range(4):
            if kt < m:
                dmaskv[:, kt * 128:(kt + 1) * 128] = 1.0
            elif kt == m:
                dmaskv[:, kt * 128:(kt + 1) * 128] = (rr <= cc2)
        b0colsv = np.ascontiguousarray(
            inputs["L1_ffn_bias0"].reshape(64, 128).T.astype(np.float32))
        im = {
            "x_in": np.ascontiguousarray(x[rows]),
            "w1s": np.ascontiguousarray(w1T[256 * p:256 * (p + 1)]),
            "w2s": np.ascontiguousarray(w2T[1024 * p:1024 * (p + 1)]),
            "W0s": np.ascontiguousarray(W0T[256 * p:256 * (p + 1)]),
            "W1s": np.ascontiguousarray(W1T[1024 * p:1024 * (p + 1)]),
            "onesrow": onesrow,
            "wqs": np.ascontiguousarray(wqT[256 * p:256 * (p + 1)]),
            "wkvs": np.ascontiguousarray(wkvT[256 * p:256 * (p + 1)]),
            "wos": np.ascontiguousarray(woT[256 * p:256 * (p + 1)]),
            "cstab_in": cstab.astype(BF),
            "dmask_in": dmaskv.astype(BF),
            "b0cols": b0colsv,
            "qkvbd": qkvbd, "projbd": projbd, "ident_in": ident,
            "pconst": pconst, "vrows": vrows, "cmask": cmaskv,
            "postab": postab, "gidx": gidx,
            "gsel_in": gselv, "ebc_in": ebcv, "i16_in": i16v,
            "glnsel_in": glnv, "ebc8_in": ebc8v,
        }
        in_maps.append(im)
    return in_maps


def kernel(**inputs):
    in_maps = host_prep(inputs)
    nc = build_program()
    res = run_bass_kernel_spmd(nc, in_maps, core_ids=list(range(8)))
    out = np.zeros((B * S, D), np.float32)
    for p in range(8):
        out[core_rows(p)] = np.asarray(res.results[p]["out"])
    return out.reshape(B, S, D)
